# version 01
# baseline (speedup 1.0000x reference)
import numpy as np
import ml_dtypes
import concourse.bass as bass
import concourse.mybir as mybir
from concourse.bass_utils import run_bass_kernel_spmd

F32 = mybir.dt.float32
BF16 = mybir.dt.bfloat16
AF = mybir.ActivationFunctionType
OP = mybir.AluOpType
AX = mybir.AxisListType

D = 2048
T = 1024
KT = 16
NE = 32
TAU = 0.05
EPS_LN = 1e-5
RSQ_D = float(np.sqrt(2048.0))
INV_SQ_D = float(1.0 / np.sqrt(2048.0))

CONSTS = [EPS := 1e-5, 1.0 / 2048.0, -2048.0, 1e-12, float(np.sqrt(2048.0)), -1.0,
          1e-30, 0.5, 0.05, float(1.0 / np.sqrt(2048.0))]

ARENA_N = 57344
RING_O = 40960
SLOT_N = 8192
FSCR_N = 6144


class Sched:
    ENGINES = ("tensor", "vector", "scalar", "gpsimd", "sync")

    def __init__(self):
        self.ops = {e: [] for e in self.ENGINES}
        self.cnt = {}

    def emit(self, eng, fn):
        self.ops[eng].append(fn)

    def inc(self, sem, n=1):
        self.cnt[sem] = self.cnt.get(sem, 0) + n
        return self.cnt[sem]

    def get(self, sem):
        return self.cnt.get(sem, 0)


class Ctx:
    pass


def arena_v(c, off, n):
    return c.arena[:, off:off + n]


def slot_ap(c, slot, n_in, width):
    return arena_v(c, RING_O + slot * SLOT_N, n_in * width).rearrange(
        "p (i w) -> p i w", w=width)


def ring_load(c, slot, dmas, queue="gpsimd"):
    S = c.S
    if c.slot_free[slot]:
        S.emit(queue, lambda g, v=c.slot_free[slot]: g.wait_ge(c.s_ev, v))
    for dst, src in dmas:
        S.emit(queue, lambda g, d=dst, s=src, sl=slot: g.dma_start(out=d, in_=s).then_inc(c.s_ld[sl], 16))
    c.slot_val[slot] += 16 * len(dmas)
    return (slot, c.slot_val[slot])


def end_ring(c, chunk_free, n):
    for ci in (n - 1, n - 2):
        if ci >= 0 and ci in chunk_free:
            c.slot_free[ci % 2] = max(c.slot_free[ci % 2], chunk_free[ci])


def wait_chunk(c, eng, rdy):
    c.S.emit(eng, lambda t, sl=rdy[0], v=rdy[1]: t.wait_ge(c.s_ld[sl], v))


def pe_bank(c, b):
    if c.bank_free[b]:
        c.S.emit("tensor", lambda t, v=c.bank_free[b]: t.wait_ge(c.s_ev, v))


def pe_bank2(c, b0, b1):
    w = max(c.bank_free[b0], c.bank_free[b1])
    if w:
        c.S.emit("tensor", lambda t, v=w: t.wait_ge(c.s_ev, v))



def proj_groups(c, w3, n_kt, mt_list, rhs_src, rhs_base, pre_wait=None):
    S = c.S
    for i, mt in enumerate(mt_list):
        b0, b1 = (0, 1) if i % 2 == 0 else (2, 3)
        pe_bank2(c, b0, b1)
        def group(t, mt=mt, b0=b0, b1=b1, w3=w3, n_kt=n_kt, rhs_src=rhs_src,
                  rhs_base=rhs_base, pw=(pre_wait if i == 0 else None)):
            if pw:
                t.wait_ge(c.s_ev, pw)
            for kt in range(n_kt):
                lhsT = w3[:, kt, mt * 128:(mt + 1) * 128]
                t.matmul(c.banks[b0][:, :], lhsT,
                         rhs_src[:, kt * T + rhs_base: kt * T + rhs_base + 512],
                         start=(kt == 0), stop=(kt == n_kt - 1), skip_group_check=True)
                ins = t.matmul(c.banks[b1][:, :], lhsT,
                         rhs_src[:, kt * T + rhs_base + 512: kt * T + rhs_base + 1024],
                         start=(kt == 0), stop=(kt == n_kt - 1), skip_group_check=True)
            ins.then_inc(c.s_pe, 1)
        S.emit("tensor", group)
        tag = S.inc("pe")
        yield mt, tag, b0, b1


def evac_add_xacc(c, mt, tag, b0, b1, bias_col=None):
    S = c.S
    for tb, b in ((0, b0), (1, b1)):
        if bias_col is None:
            S.emit("vector", lambda v, tag=tag, mt=mt, tb=tb, b=b: (
                v.wait_ge(c.s_pe, tag),
                v.tensor_add(c.xacc[:, mt * T + tb * 512: mt * T + tb * 512 + 512],
                             c.xacc[:, mt * T + tb * 512: mt * T + tb * 512 + 512],
                             c.banks[b][:, :]).then_inc(c.s_ev, 1)))
        else:
            S.emit("vector", lambda v, tag=tag, mt=mt, tb=tb, b=b, bc=bias_col: (
                v.wait_ge(c.s_pe, tag),
                v.scalar_tensor_tensor(c.xacc[:, mt * T + tb * 512: mt * T + tb * 512 + 512],
                                       c.banks[b][:, :], bc,
                                       c.xacc[:, mt * T + tb * 512: mt * T + tb * 512 + 512],
                                       OP.add, OP.add).then_inc(c.s_ev, 1)))
        c.bank_free[b] = S.inc("ev")


def emit_stats(c, want_mean, mrow, s2row):
    S = c.S
    castA = arena_v(c, 16384, 1024)
    castB = arena_v(c, 17408, 1024)
    sqA = arena_v(c, 18432, 1024)
    sqB = arena_v(c, 19456, 1024)
    cast_free = [0, 0]
    pre = S.get("ev")
    for b in range(4):
        pe_bank(c, b)
    for kt in range(KT):
        sl = kt % 2
        cbuf = castA if sl == 0 else castB
        qbuf = sqA if sl == 0 else sqB
        aw = cast_free[sl]
        S.emit("scalar", lambda a, kt=kt, cb=cbuf, aw=aw, pre=pre: (
            (a.wait_ge(c.s_ev, pre) if kt == 0 and pre else None),
            (a.wait_ge(c.s_pe, aw) if aw else None),
            a.copy(cb, c.xacc[:, kt * T:(kt + 1) * T]).then_inc(c.s_act, 1)))
        t_cast = S.inc("act")
        S.emit("vector", lambda v, cb=cbuf, sq=qbuf, tc=t_cast, aw=aw: (
            v.wait_ge(c.s_act, tc),
            (v.wait_ge(c.s_pe, aw) if aw else None),
            v.tensor_mul(sq, cb, cb).then_inc(c.s_ev, 1)))
        t_sq = S.inc("ev")
        def mmst(t, kt=kt, cb=cbuf, sq=qbuf, tc=t_cast, ts=t_sq, wm=want_mean):
            t.wait_ge(c.s_act, tc)
            t.wait_ge(c.s_ev, ts)
            first, last = kt == 0, kt == KT - 1
            if wm:
                t.matmul(c.psA[0:1, :], c.ones_bf[:, :], cb[:, 0:512], start=first, stop=last, skip_group_check=True)
                t.matmul(c.psB[0:1, :], c.ones_bf[:, :], cb[:, 512:1024], start=first, stop=last, skip_group_check=True)
            t.matmul(c.psC[0:1, :], c.ones_bf[:, :], sq[:, 0:512], start=first, stop=last, skip_group_check=True)
            ins = t.matmul(c.psD[0:1, :], c.ones_bf[:, :], sq[:, 512:1024], start=first, stop=last, skip_group_check=True)
            ins.then_inc(c.s_pe, 1)
        S.emit("tensor", mmst)
        cast_free[sl] = S.inc("pe")
    tag = S.get("pe")
    def evst(v, tag=tag, mrow=mrow, s2row=s2row, wm=want_mean):
        v.wait_ge(c.s_pe, tag)
        if wm:
            v.tensor_copy(mrow[:, 0:512], c.psA[0:1, :])
            v.tensor_copy(mrow[:, 512:1024], c.psB[0:1, :])
        v.tensor_copy(s2row[:, 0:512], c.psC[0:1, :])
        v.tensor_copy(s2row[:, 512:1024], c.psD[0:1, :]).then_inc(c.s_ev, 1)
    S.emit("vector", evst)
    done = S.inc("ev")
    for b in range(4):
        c.bank_free[b] = done
    return done


def bcast_row(c, row_ap, dst_f32_ap, aux, width=1024, pre_ev=None):
    S = c.S
    psx = c.psE if aux == 0 else c.psF
    nb = (width + 511) // 512
    for tb in range(nb):
        w = min(512, width - tb * 512)
        if c.aux_free[aux]:
            S.emit("tensor", lambda t, v=c.aux_free[aux]: t.wait_ge(c.s_ev, v))
        pv = pre_ev if tb == 0 else None
        S.emit("tensor", lambda t, tb=tb, r=row_ap, p=psx, w=w, pv=pv: (
            (t.wait_ge(c.s_ev, pv) if pv else None),
            t.matmul(p[:, 0:w], c.ones_f1[:, :], r[:, tb * 512:tb * 512 + w],
                     start=True, stop=True).then_inc(c.s_pe, 1)))
        tag = S.inc("pe")
        S.emit("vector", lambda v, tag=tag, tb=tb, p=psx, d=dst_f32_ap, w=w: (
            v.wait_ge(c.s_pe, tag),
            v.tensor_copy(d[:, tb * 512:tb * 512 + w], p[:, 0:w]).then_inc(c.s_ev, 1)))
        c.aux_free[aux] = S.inc("ev")
    return c.aux_free[aux]


def emit_ln(c, mode, dst_hb=None):
    S = c.S
    rows = c.fscr[0:1, :]
    mrow = rows[:, 2048:3072]
    s2row = rows[:, 3072:4096]
    trow = rows[:, 4096:5120]
    want_mean = mode.startswith("ln")
    st = emit_stats(c, want_mean, mrow, s2row)
    if want_mean:
        def rowm(v, st=st):
            v.wait_ge(c.s_ev, st)
            v.tensor_scalar_mul(mrow, mrow, 1.0 / 2048.0)
            v.drain()
            v.tensor_mul(trow, mrow, mrow)
            v.tensor_scalar(s2row, s2row, 1.0 / 2048.0, None, OP.mult)
            v.drain()
            v.tensor_sub(s2row, s2row, trow).then_inc(c.s_ev, 1)
        S.emit("vector", rowm)
        vtag = S.inc("ev")
        S.emit("scalar", lambda a, vt=vtag: (
            a.wait_ge(c.s_ev, vt),
            a.activation(trow, s2row, AF.Sqrt, bias=EPS_LN).then_inc(c.s_act, 1)))
        atag = S.inc("act")
        S.emit("vector", lambda v, at=atag: (
            v.wait_ge(c.s_act, at),
            v.reciprocal(s2row, trow).then_inc(c.s_ev, 1)))
        rtag = S.inc("ev")
        mb = c.fscr[:, 0:1024]
        rb = c.fscr[:, 1024:2048]
        bcast_row(c, mrow, mb, 0, pre_ev=rtag)
        done = bcast_row(c, s2row, rb, 1, pre_ev=rtag)
        for kt in range(KT):
            if mode == "ln_inplace":
                S.emit("vector", lambda v, kt=kt, d=done: (
                    (v.wait_ge(c.s_ev, d) if kt == 0 else None),
                    v.tensor_sub(c.xacc[:, kt * T:(kt + 1) * T], c.xacc[:, kt * T:(kt + 1) * T], mb).then_inc(c.s_ev, 1)))
                stag = S.inc("ev")
                S.emit("gpsimd", lambda g, kt=kt, stg=stag: (
                    g.wait_ge(c.s_ev, stg),
                    g.tensor_mul(c.xacc[:, kt * T:(kt + 1) * T], c.xacc[:, kt * T:(kt + 1) * T], rb).then_inc(c.s_pool, 1)))
                S.inc("pool")
            else:
                tmp = c.fscr[:, 4096 + (kt % 2) * 512:4096 + (kt % 2) * 512 + 512]
                tmp2 = c.fscr[:, 5120 + (kt % 2) * 512:5120 + (kt % 2) * 512 + 512]
                pw = c.ln_tmp_free[kt % 2]
                S.emit("vector", lambda v, kt=kt, d=done, tmp=tmp, tmp2=tmp2, pw=pw: (
                    (v.wait_ge(c.s_ev, d) if kt == 0 else None),
                    (v.wait_ge(c.s_pool, pw) if pw else None),
                    v.tensor_sub(tmp, c.xacc[:, kt * T:kt * T + 512], mb[:, 0:512]),
                    v.tensor_sub(tmp2, c.xacc[:, kt * T + 512:kt * T + 1024], mb[:, 512:1024]).then_inc(c.s_ev, 1)))
                stag = S.inc("ev")
                S.emit("gpsimd", lambda g, kt=kt, stg=stag, hb=dst_hb, tmp=tmp, tmp2=tmp2: (
                    g.wait_ge(c.s_ev, stg),
                    g.tensor_mul(hb[:, kt * T:kt * T + 512], tmp, rb[:, 0:512]),
                    g.tensor_mul(hb[:, kt * T + 512:kt * T + 1024], tmp2, rb[:, 512:1024]).then_inc(c.s_pool, 1)))
                c.ln_tmp_free[kt % 2] = S.inc("pool")
        pend = S.get("pool")
        S.emit("vector", lambda v, pe=pend: (v.wait_ge(c.s_pool, pe), v.engine_nop().then_inc(c.s_ev, 1)))
        S.inc("ev")
    else:
        S.emit("scalar", lambda a, st=st: (
            a.wait_ge(c.s_ev, st),
            a.activation(trow, s2row, AF.Sqrt).then_inc(c.s_act, 1)))
        atag = S.inc("act")
        def rowr(v, at=atag):
            v.wait_ge(c.s_act, at)
            v.tensor_scalar_max(trow, trow, 1e-12)
            v.drain()
            v.reciprocal(s2row, trow)
            v.drain()
            v.tensor_scalar_mul(s2row, s2row, RSQ_D).then_inc(c.s_ev, 1)
        S.emit("vector", rowr)
        rtag = S.inc("ev")
        sbb = c.fscr[:, 0:1024]
        done = bcast_row(c, s2row, sbb, 0, pre_ev=rtag)
        for kt in range(KT):
            S.emit("vector", lambda v, kt=kt, d=done, hb=dst_hb: (
                (v.wait_ge(c.s_ev, d) if kt == 0 else None),
                v.tensor_mul(hb[:, kt * T:(kt + 1) * T], c.xacc[:, kt * T:(kt + 1) * T], sbb).then_inc(c.s_ev, 1)))
            S.inc("ev")
    return S.get("ev")



def emit_s1(c, dram):
    S = c.S
    hb = arena_v(c, 0, KT * T)
    emit_ln(c, "ln_inplace")
    emit_ln(c, "rms_hb", dst_hb=hb)
    u_ready = S.get("ev")

    rows = c.fscr[0:1, :]
    hm = rows[:, 5120:5123]
    hs = rows[:, 5123:5126]
    ht_ = rows[:, 5126:5129]
    hsq = c.fscr[:, 5632:5680]
    S.emit("vector", lambda v: v.tensor_mul(hsq, c.xh[:, :], c.xh[:, :]).then_inc(c.s_ev, 1))
    hq = S.inc("ev")
    if c.aux_free[0]:
        S.emit("tensor", lambda t, v=c.aux_free[0]: t.wait_ge(c.s_ev, v))
    if c.aux_free[1]:
        S.emit("tensor", lambda t, v=c.aux_free[1]: t.wait_ge(c.s_ev, v))
    def halo_stats(t, hq=hq):
        t.wait_ge(c.s_ev, hq)
        for kt in range(KT):
            t.matmul(c.psE[0:1, 0:3], c.ones_f128[:, :], c.xh[:, kt * 3:(kt + 1) * 3],
                     start=(kt == 0), stop=(kt == KT - 1), skip_group_check=True)
            ins = t.matmul(c.psF[0:1, 0:3], c.ones_f128[:, :], hsq[:, kt * 3:(kt + 1) * 3],
                     start=(kt == 0), stop=(kt == KT - 1), skip_group_check=True)
        ins.then_inc(c.s_pe, 1)
    S.emit("tensor", halo_stats)
    hstag = S.inc("pe")
    def halo_rows(v, hstag=hstag):
        v.wait_ge(c.s_pe, hstag)
        v.tensor_copy(hm, c.psE[0:1, 0:3])
        v.tensor_copy(hs, c.psF[0:1, 0:3])
        v.drain()
        v.tensor_scalar_mul(hm, hm, 1.0 / 2048.0)
        v.drain()
        v.tensor_mul(ht_, hm, hm)
        v.drain()
        v.tensor_scalar(ht_, ht_, -2048.0, None, OP.mult)
        v.drain()
        v.tensor_add(ht_, ht_, hs)
        v.drain()
        v.tensor_scalar_mul(hs, ht_, 1.0 / 2048.0).then_inc(c.s_ev, 1)
    S.emit("vector", halo_rows)
    c.aux_free[0] = c.aux_free[1] = S.inc("ev")
    v1 = c.aux_free[0]
    S.emit("scalar", lambda a, v1=v1: (
        a.wait_ge(c.s_ev, v1),
        a.activation(hs, hs, AF.Sqrt, bias=EPS_LN).then_inc(c.s_act, 1)))
    a1 = S.inc("act")
    def halo_rows2(v, a1=a1):
        v.wait_ge(c.s_act, a1)
        v.reciprocal(hs, hs)
        v.drain()
        v.tensor_mul(ht_, ht_, hs)
        v.drain()
        v.tensor_mul(ht_, ht_, hs).then_inc(c.s_ev, 1)
    S.emit("vector", halo_rows2)
    v2 = S.inc("ev")
    S.emit("scalar", lambda a, v2=v2: (
        a.wait_ge(c.s_ev, v2),
        a.activation(ht_, ht_, AF.Sqrt).then_inc(c.s_act, 1)))
    a2 = S.inc("act")
    def halo_rows3(v, a2=a2):
        v.wait_ge(c.s_act, a2)
        v.tensor_scalar_max(ht_, ht_, 1e-12)
        v.drain()
        v.reciprocal(ht_, ht_)
        v.drain()
        v.tensor_scalar_mul(ht_, ht_, RSQ_D).then_inc(c.s_ev, 1)
    S.emit("vector", halo_rows3)
    v3 = S.inc("ev")
    if c.aux_free[1]:
        S.emit("tensor", lambda t, v=c.aux_free[1]: t.wait_ge(c.s_ev, v))
    S.emit("tensor", lambda t, v3=v3: (
        t.wait_ge(c.s_ev, v3),
        t.matmul(c.psF[:, 0:9], c.ones_f1[:, :], rows[:, 5120:5129],
                 start=True, stop=True).then_inc(c.s_pe, 1)))
    hbtag = S.inc("pe")
    def halo_u(v, hbtag=hbtag):
        v.wait_ge(c.s_pe, hbtag)
        for kt in range(KT):
            v.tensor_sub(hsq[:, kt * 3:(kt + 1) * 3], c.xh[:, kt * 3:(kt + 1) * 3], c.psF[:, 0:3])
        v.drain()
        for kt in range(KT):
            v.tensor_mul(hsq[:, kt * 3:(kt + 1) * 3], hsq[:, kt * 3:(kt + 1) * 3], c.psF[:, 3:6])
        v.drain()
        for kt in range(KT):
            ins = v.tensor_mul(c.uh[:, kt * 3:(kt + 1) * 3], hsq[:, kt * 3:(kt + 1) * 3], c.psF[:, 6:9])
        ins.then_inc(c.s_ev, 1)
    S.emit("vector", halo_u)
    c.aux_free[1] = S.inc("ev")
    uh_tag = c.aux_free[1]

    winT, woT = dram["winT"], dram["woT"]
    xpc = arena_v(c, 16384, 4 * 1027)
    ych = arena_v(c, 20492, 4 * 1024)
    xc = c.fscr[:, 1024:2048]
    xc2 = c.fscr[:, 4096:5120]
    xcs = c.fscr[:, 2048:3072]
    xcs2 = c.fscr[:, 0:1024]
    zs0 = c.fscr[:, 3072:3584]
    zs1 = c.fscr[:, 3584:4096]
    zs0b = c.fscr[:, 5120:5632]
    zs1b = c.fscr[:, 5632:6144]

    def dma_chunk(kind_ch, slot):
        kind, ch = kind_ch
        if kind == "xp":
            src = winT.rearrange("(kt p) n -> p kt n", p=128)[:, :, ch * 512:(ch + 1) * 512]
            dst = slot_ap(c, slot, 16, 512)
            dmas = [(dst[:, 0:8, :], src[:, 0:8, :]), (dst[:, 8:16, :], src[:, 8:16, :])]
        elif kind == "z":
            src = winT.rearrange("(kt p) n -> p kt n", p=128)[:, :, 4096 + ch * 512:4096 + (ch + 1) * 512]
            dst = slot_ap(c, slot, 16, 512)
            dmas = [(dst[:, 0:8, :], src[:, 0:8, :]), (dst[:, 8:16, :], src[:, 8:16, :])]
        else:
            src = woT[ch * 512:(ch + 1) * 512, :].rearrange("(kt p) n -> p kt n", p=128)
            dst = slot_ap(c, slot, 4, 2048)
            dmas = [(dst[:, 0:4, :], src[:, 0:4, :])]
        return ring_load(c, slot, dmas)

    chunks = []
    for ch in range(8):
        chunks += [("xp", ch), ("z", ch), ("wo", ch)]
    rdy, chunk_free = {}, {}
    ci_next = 0

    def advance():
        nonlocal ci_next
        if ci_next < len(chunks):
            ci = ci_next
            if ci >= 2:
                c.slot_free[ci % 2] = chunk_free[ci - 2]
            rdy[ci] = dma_chunk(chunks[ci], ci % 2)
            ci_next += 1

    advance(); advance()
    for ch in range(8):
        ci_xp, ci_z, ci_wo = 3 * ch, 3 * ch + 1, 3 * ch + 2
        wait_chunk(c, "tensor", rdy[ci_xp])
        w = slot_ap(c, ci_xp % 2, 16, 512)
        for mt, tag, b0, b1 in proj_groups(c, w, KT, range(4), hb, 0,
                                           pre_wait=(u_ready if ch == 0 else None)):
            for tb, b in ((0, b0), (1, b1)):
                S.emit("vector", lambda v, tag=tag, mt=mt, tb=tb, b=b: (
                    v.wait_ge(c.s_pe, tag),
                    v.tensor_copy(xpc[:, mt * 1027 + 3 + tb * 512: mt * 1027 + 3 + tb * 512 + 512],
                                  c.banks[b][:, :]).then_inc(c.s_ev, 1)))
                c.bank_free[b] = S.inc("ev")
        if c.aux_free[1]:
            S.emit("tensor", lambda t, v=c.aux_free[1]: t.wait_ge(c.s_ev, v))
        def haloxp_mm(t, w=w, uh_tag=uh_tag):
            t.wait_ge(c.s_ev, uh_tag)
            for mt in range(4):
                for kt in range(KT):
                    ins = t.matmul(c.psF[:, 4 * mt:4 * mt + 3], w[:, kt, mt * 128:(mt + 1) * 128],
                                   c.uh[:, kt * 3:(kt + 1) * 3],
                                   start=(kt == 0), stop=(kt == KT - 1), skip_group_check=True)
            ins.then_inc(c.s_pe, 1)
        S.emit("tensor", haloxp_mm)
        htag = S.inc("pe")
        def haloxp_ev(v, htag=htag):
            v.wait_ge(c.s_pe, htag)
            for mt in range(4):
                ins = v.tensor_copy(xpc[:, mt * 1027: mt * 1027 + 3], c.psF[:, 4 * mt:4 * mt + 3])
            ins.then_inc(c.s_ev, 1)
        S.emit("vector", haloxp_ev)
        c.aux_free[1] = S.inc("ev")
        xp_done = c.aux_free[1]
        chunk_free[ci_xp] = xp_done
        advance()
        wait_chunk(c, "tensor", rdy[ci_z])
        wz = slot_ap(c, ci_z % 2, 16, 512)
        for mt, ztag, b0, b1 in proj_groups(c, wz, KT, range(4), hb, 0):
            mtg = ch * 4 + mt
            par = mt % 2
            xcp = xc if par == 0 else xc2
            S.emit("scalar", lambda a, mt=mt, mtg=mtg, xd=xp_done, xcp=xcp,
                   tg=c.conv_tap_done[par], yg=c.conv_ymul_done[par]: (
                a.wait_ge(c.s_ev, xd),
                (a.wait_ge(c.s_ev, max(tg, yg)) if max(tg, yg) else None),
                a.activation(xcp, xpc[:, mt * 1027 + 0: mt * 1027 + 1024], AF.Identity,
                             bias=c.convb[:, mtg:mtg + 1],
                             scale=c.convw[:, 4 * mtg:4 * mtg + 1]).then_inc(c.s_act, 1)))
            c0 = S.inc("act")
            def conv_taps(v, mt=mt, mtg=mtg, c0=c0, xcp=xcp):
                v.wait_ge(c.s_act, c0)
                for k in (1, 2, 3):
                    ins = v.scalar_tensor_tensor(xcp, xpc[:, mt * 1027 + k: mt * 1027 + k + 1024],
                                           c.convw[:, 4 * mtg + k:4 * mtg + k + 1], xcp,
                                           OP.mult, OP.add)
                    if k < 3:
                        v.drain()
                ins.then_inc(c.s_ev, 1)
            S.emit("vector", conv_taps)
            ctap = S.inc("ev")
            c.conv_tap_done[par] = ctap
            xcsp = xcs if par == 0 else xcs2
            zs0p = zs0 if par == 0 else zs0b
            zs1p = zs1 if par == 0 else zs1b
            def silus(a, ctap=ctap, ztag=ztag, b0=b0, b1=b1, xcp=xcp, xcsp=xcsp,
                      zs0p=zs0p, zs1p=zs1p, yg=c.conv_ymul_done[par]):
                a.wait_ge(c.s_ev, ctap)
                (a.wait_ge(c.s_ev, yg) if yg else None)
                a.activation(xcsp, xcp, AF.Silu)
                a.wait_ge(c.s_pe, ztag)
                a.activation(zs0p, c.banks[b0][:, :], AF.Silu)
                a.activation(zs1p, c.banks[b1][:, :], AF.Silu).then_inc(c.s_act, 1)
            S.emit("scalar", silus)
            stag = S.inc("act")
            def ymul(v, mt=mt, stag=stag, xcsp=xcsp, zs0p=zs0p, zs1p=zs1p):
                v.wait_ge(c.s_act, stag)
                v.tensor_mul(ych[:, mt * 1024: mt * 1024 + 512], xcsp[:, 0:512], zs0p)
                v.tensor_mul(ych[:, mt * 1024 + 512: mt * 1024 + 1024], xcsp[:, 512:1024], zs1p).then_inc(c.s_ev, 1)
            S.emit("vector", ymul)
            yt = S.inc("ev")
            c.conv_ymul_done[par] = yt
            c.bank_free[b0] = yt
            c.bank_free[b1] = yt
        y_done = S.get("ev")
        chunk_free[ci_z] = y_done
        advance()
        wait_chunk(c, "tensor", rdy[ci_wo])
        wo = slot_ap(c, ci_wo % 2, 4, 2048)
        for mo, tag, b0, b1 in proj_groups(c, wo, 4, range(KT), ych, 0, pre_wait=y_done):
            evac_add_xacc(c, mo, tag, b0, b1)
        chunk_free[ci_wo] = S.get("ev")
        advance()
    end_ring(c, chunk_free, len(chunks))

    emit_ln(c, "rms_hb", dst_hb=hb)
    h2_ready = S.get("ev")
    w1T, w2T = dram["w1T"], dram["w2T"]
    achunk = arena_v(c, 16384, 16384)
    def dma_mlp(kind_ch, slot):
        kind, ch = kind_ch
        if kind == "a":
            src = w1T.rearrange("(kt p) n -> p kt n", p=128)[:, :, ch * 512:(ch + 1) * 512]
            dst = slot_ap(c, slot, 16, 512)
            dmas = [(dst[:, 0:8, :], src[:, 0:8, :]), (dst[:, 8:16, :], src[:, 8:16, :])]
        else:
            src = w2T[ch * 512:(ch + 1) * 512, :].rearrange("(kt p) n -> p kt n", p=128)
            dst = slot_ap(c, slot, 4, 2048)
            dmas = [(dst[:, 0:4, :], src[:, 0:4, :])]
        return ring_load(c, slot, dmas)

    mchunks = []
    for big in range(4):
        mchunks += [("a", big * 4 + i) for i in range(4)] + [("w2", big * 4 + i) for i in range(4)]
    mrdy, mfree = {}, {}
    mi = 0
    def madv():
        nonlocal mi
        if mi < len(mchunks):
            if mi >= 2:
                c.slot_free[mi % 2] = mfree[mi - 2]
            mrdy[mi] = dma_mlp(mchunks[mi], mi % 2)
            mi += 1
    madv(); madv()
    for big in range(4):
        for sub in range(4):
            ci = big * 8 + sub
            wait_chunk(c, "tensor", mrdy[ci])
            w = slot_ap(c, ci % 2, 16, 512)
            for mt, tag, b0, b1 in proj_groups(c, w, KT, range(4), hb, 0,
                                               pre_wait=(h2_ready if ci == 0 else None)):
                at = sub * 4 + mt
                for tb, b in ((0, b0), (1, b1)):
                    S.emit("scalar", lambda a, tag=tag, at=at, tb=tb, b=b: (
                        a.wait_ge(c.s_pe, tag),
                        a.activation(achunk[:, at * 1024 + tb * 512: at * 1024 + tb * 512 + 512],
                                     c.banks[b][:, :], AF.Relu).then_inc(c.s_act, 1)))
                    S.inc("act")
                atv = S.get("act")
                S.emit("vector", lambda v, at=atv: (v.wait_ge(c.s_act, at),
                                                    v.engine_nop().then_inc(c.s_ev, 1)))
                fence = S.inc("ev")
                c.bank_free[b0] = fence
                c.bank_free[b1] = fence
            mfree[ci] = S.get("ev")
            madv()
        a_ready = S.get("ev")
        for sub in range(4):
            ci = big * 8 + 4 + sub
            wait_chunk(c, "tensor", mrdy[ci])
            w = slot_ap(c, ci % 2, 4, 2048)
            sub_rhs = arena_v(c, 16384 + sub * 4 * T, 4 * T)
            for mo, tag, b0, b1 in proj_groups(c, w, 4, range(KT), sub_rhs,
                                               0, pre_wait=(a_ready if sub == 0 else None)):
                evac_add_xacc(c, mo, tag, b0, b1)
            mfree[ci] = S.get("ev")
            madv()
    end_ring(c, mfree, len(mchunks))

    land = [c.fscr[:, 0:1024], c.fscr[:, 1024:2048]]
    xr = dram["xT"].rearrange("(kt p) t -> p kt t", p=128)
    land_free = [0, 0]
    S.emit("vector", lambda v: v.drain())
    ev_now, act_now = S.get("ev"), S.get("act")
    S.emit("sync", lambda s, e=ev_now: s.wait_ge(c.s_ev, e))
    S.emit("sync", lambda s, a=act_now: s.wait_ge(c.s_act, a))
    for kt in range(KT):
        sl = kt % 2
        if land_free[sl]:
            S.emit("sync", lambda s, v=land_free[sl]: s.wait_ge(c.s_ev, v))
        S.emit("sync", lambda s, kt=kt, sl=sl: s.dma_start(out=land[sl], in_=xr[:, kt, :]).then_inc(c.s_xl[sl], 16))
        lv = S.inc("xl%d" % sl, 16)
        S.emit("vector", lambda v, kt=kt, sl=sl, lv=lv: (
            v.wait_ge(c.s_xl[sl], lv),
            v.tensor_add(c.xacc[:, kt * T:(kt + 1) * T], c.xacc[:, kt * T:(kt + 1) * T],
                         land[sl]).then_inc(c.s_ev, 1)))
        land_free[sl] = S.inc("ev")



def emit_gating(c, dram):
    S = c.S
    hb = arena_v(c, 0, KT * T)
    logT = c.fscr[0:32, 2048:3072]
    ltok = c.fscr[:, 3072:3328]
    crT = c.fscr[0:32, 5120:6144]
    ptok = c.fscr[:, 3328:3360]
    msk = c.fscr[:, 3400:3432]
    msk2 = c.fscr[:, 3432:3464]
    cw = c.fscr[:, 3360:3368]
    ctile = c.fscr[:, 3464:3496]
    hb_ready = S.get("ev")
    if c.aux_free[0]:
        S.emit("tensor", lambda t, v=c.aux_free[0]: t.wait_ge(c.s_ev, v))
    if c.aux_free[1]:
        S.emit("tensor", lambda t, v=c.aux_free[1]: t.wait_ge(c.s_ev, v))
    def log_mm(t, hr=hb_ready):
        t.wait_ge(c.s_ev, hr)
        for kt in range(KT):
            t.matmul(c.psE[0:32, :], c.gwT[:, kt * 32:(kt + 1) * 32], hb[:, kt * T: kt * T + 512],
                     start=(kt == 0), stop=(kt == KT - 1), skip_group_check=True)
            ins = t.matmul(c.psF[0:32, :], c.gwT[:, kt * 32:(kt + 1) * 32], hb[:, kt * T + 512: kt * T + 1024],
                     start=(kt == 0), stop=(kt == KT - 1), skip_group_check=True)
        ins.then_inc(c.s_pe, 1)
    S.emit("tensor", log_mm)
    lt = S.inc("pe")
    S.emit("vector", lambda v, lt=lt: (
        v.wait_ge(c.s_pe, lt),
        v.tensor_copy(logT[:, 0:512], c.psE[0:32, :]),
        v.tensor_copy(logT[:, 512:1024], c.psF[0:32, :]).then_inc(c.s_ev, 1)))
    c.aux_free[0] = c.aux_free[1] = S.inc("ev")
    lev = c.aux_free[0]
    for i in range(8):
        if c.aux_free[0]:
            S.emit("tensor", lambda t, v=c.aux_free[0]: t.wait_ge(c.s_ev, v))
        S.emit("tensor", lambda t, i=i, lev=lev: (
            (t.wait_ge(c.s_ev, lev) if i == 0 else None),
            t.transpose(c.psE[:, 0:32], logT[:, i * 128:(i + 1) * 128], c.ident_f[0:32, 0:32]).then_inc(c.s_pe, 1)))
        tg = S.inc("pe")
        S.emit("vector", lambda v, i=i, tg=tg: (
            v.wait_ge(c.s_pe, tg),
            v.tensor_copy(ltok[:, i * 32:(i + 1) * 32], c.psE[:, 0:32]).then_inc(c.s_ev, 1)))
        c.aux_free[0] = S.inc("ev")
        c.ltok_tags[i] = c.aux_free[0]
    ltok_tags = dict(c.ltok_tags)
    for i in range(8):
        def gtile1(v, i=i, lt=ltok_tags[i]):
            v.wait_ge(c.s_ev, lt)
            l = ltok[:, i * 32:(i + 1) * 32]
            v.max(out=c.m8[:, :], in_=l)
            v.drain()
            v.tensor_scalar_mul(cw[:, 0:1], c.m8[:, 0:1], -1.0).then_inc(c.s_ev, 1)
        S.emit("vector", gtile1)
        t1 = S.inc("ev")
        S.emit("scalar", lambda a, i=i, t1=t1: (
            a.wait_ge(c.s_ev, t1),
            a.activation(ptok, ltok[:, i * 32:(i + 1) * 32], AF.Exp, bias=cw[:, 0:1]).then_inc(c.s_act, 1)))
        t2 = S.inc("act")
        def gtile2(v, i=i, t2=t2):
            l = ltok[:, i * 32:(i + 1) * 32]
            v.wait_ge(c.s_act, t2)
            v.tensor_scalar(msk, l, c.m8[:, 7:8], None, OP.is_ge)
            v.drain()
            v.tensor_mul(ptok, ptok, msk)
            v.drain()
            v.reduce_sum(cw[:, 1:2], ptok, AX.X)
            v.drain()
            v.tensor_scalar_max(cw[:, 2:3], cw[:, 1:2], 1e-12)
            v.drain()
            v.reciprocal(cw[:, 2:3], cw[:, 2:3])
            v.drain()
            v.tensor_mul(cw[:, 3:4], cw[:, 2:3], c.al128[:, 0:1])
            v.drain()
            v.tensor_scalar(msk, ptok, cw[:, 3:4], TAU, OP.mult, OP.is_ge)
            v.drain()
            v.reduce_sum(cw[:, 4:5], msk, AX.X)
            v.drain()
            v.tensor_scalar(cw[:, 5:6], cw[:, 4:5], 0.5, None, OP.is_lt)
            v.tensor_scalar(msk2, l, c.m8[:, 0:1], None, OP.is_ge)
            v.drain()
            v.tensor_scalar_mul(msk2, msk2, cw[:, 5:6])
            v.drain()
            v.tensor_max(msk, msk, msk2)
            v.drain()
            v.tensor_mul(ptok, ptok, msk)
            v.drain()
            v.reduce_sum(cw[:, 6:7], ptok, AX.X)
            v.drain()
            v.tensor_scalar_max(cw[:, 6:7], cw[:, 6:7], 1e-12)
            v.drain()
            v.reciprocal(cw[:, 6:7], cw[:, 6:7])
            v.drain()
            v.tensor_scalar_mul(ctile, ptok, cw[:, 6:7]).then_inc(c.s_ev, 1)
        S.emit("vector", gtile2)
        t3 = S.inc("ev")
        if c.aux_free[1]:
            S.emit("tensor", lambda t, v=c.aux_free[1]: t.wait_ge(c.s_ev, v))
        S.emit("tensor", lambda t, i=i, t3=t3: (
            t.wait_ge(c.s_ev, t3),
            t.transpose(c.psF[0:32, 0:128], ctile, c.ident_f[:, :]).then_inc(c.s_pe, 1)))
        tg = S.inc("pe")
        S.emit("vector", lambda v, i=i, tg=tg: (
            v.wait_ge(c.s_pe, tg),
            v.tensor_copy(crT[:, i * 128:(i + 1) * 128], c.psF[0:32, 0:128]).then_inc(c.s_ev, 1)))
        c.aux_free[1] = S.inc("ev")
    done = S.get("ev")
    S.emit("sync", lambda s, d=done: s.wait_ge(c.s_ev, d))
    S.emit("sync", lambda s: s.dma_start(out=dram["crows"][:, :], in_=crT).then_inc(c.s_crs, 16))
    S.inc("crs", 16)


def emit_moe(c, dram):
    S = c.S
    hb = arena_v(c, 0, KT * T)
    gsc = arena_v(c, 16384, 6144)
    hbuf = arena_v(c, 22528, 6144)
    cb = c.fscr[:, 0:2048]
    silu_t = c.fscr[:, 3072:4096]
    q_t = c.fscr[:, 4096:5120]
    guT, dnT, crows = dram["guT"], dram["dnT"], dram["crows"]
    crs_need = S.get("crs")

    def dma_moe(ek, slot):
        e, kind = ek
        if kind[0] == "t":
            j = int(kind[1])
            src = guT[e].rearrange("(kt p) n -> p kt n", p=128)[:, :, j * 512:(j + 1) * 512]
            dst = slot_ap(c, slot, 16, 512)
            dmas = [(dst[:, 0:8, :], src[:, 0:8, :]), (dst[:, 8:16, :], src[:, 8:16, :])]
        else:
            j = int(kind[1])
            src = dnT[e].rearrange("(kt p) n -> p kt n", p=128)[:, :, j * 1024:(j + 1) * 1024]
            dst = slot_ap(c, slot, 6, 1024)
            dmas = [(dst[:, 0:6, :], src[:, 0:6, :])]
        return ring_load(c, slot, dmas)

    kinds = ["t0", "t1", "t2", "d0", "d1"]
    chunks = [(e, k) for e in range(NE) for k in kinds]
    rdy, chunk_free = {}, {}
    ci_next = 0

    def advance():
        nonlocal ci_next
        if ci_next < len(chunks):
            ci = ci_next
            if ci >= 2:
                c.slot_free[ci % 2] = chunk_free[ci - 2]
            rdy[ci] = dma_moe(chunks[ci], ci % 2)
            ci_next += 1

    advance(); advance()
    crow_ready = {}
    crow_free = c.crow_free
    pool_all = {}
    silu_prev = {}
    bc_free = [c.aux_free[0], c.aux_free[1]]

    def load_crow(e):
        ns = (c.crow_phase + e) % 2
        if e <= 1:
            S.emit("gpsimd", lambda g, cn=crs_need: g.wait_ge(c.s_crs, cn))
        if crow_free[ns]:
            S.emit("gpsimd", lambda g, v=crow_free[ns]: g.wait_ge(c.s_pe, v))
        S.emit("gpsimd", lambda g, e=e, ns=ns: g.dma_start(out=c.crow_ap[ns], in_=crows[e:e + 1, :]).then_inc(c.s_cr[ns], 16))
        crow_ready[e] = (ns, S.inc("cr%d" % ns, 16))

    def emit_bcast(e):
        cslot = (c.crow_phase + e) % 2
        S.emit("tensor", lambda t, sv=crow_ready[e]: t.wait_ge(c.s_cr[sv[0]], sv[1]))
        psx = [c.psE, c.psF]
        for tb in range(2):
            if bc_free[tb]:
                S.emit("tensor", lambda t, v=bc_free[tb]: t.wait_ge(c.s_ev, v))
            S.emit("tensor", lambda t, cs=cslot, tb=tb: t.matmul(
                psx[tb][:, :], c.ones_f1[:, :], c.crow_ap[cs][:, tb * 512:(tb + 1) * 512],
                start=True, stop=True).then_inc(c.s_pe, 1))
            tag = S.inc("pe")
            pg = pool_all.get(e - 2, 0)
            S.emit("vector", lambda v, tag=tag, tb=tb, pg=pg, cs=cslot: (
                v.wait_ge(c.s_pe, tag),
                (v.wait_ge(c.s_pool, pg) if pg else None),
                v.tensor_copy(cb[:, cs * T + tb * 512: cs * T + tb * 512 + 512],
                              psx[tb][:, :]).then_inc(c.s_ev, 1)))
            bc_free[tb] = S.inc("ev")
            c.aux_free[tb] = bc_free[tb]
        crow_free[cslot] = S.get("pe")
        if e + 1 < NE:
            load_crow(e + 1)

    load_crow(0)
    for e in range(NE):
        if e == 0:
            emit_bcast(0)
        g_evac = {}
        pool_done = [0, 0]
        for j in range(12):
            tchunk = 5 * e + j // 4
            if j % 4 == 0:
                wait_chunk(c, "tensor", rdy[tchunk])
            w = slot_ap(c, tchunk % 2, 16, 512)
            mtj = j % 4
            b0, b1 = (0, 1) if j % 2 == 0 else (2, 3)
            pe_bank2(c, b0, b1)
            def gu_group(t, mtj=mtj, b0=b0, b1=b1, w=w):
                for kt in range(KT):
                    lhsT = w[:, kt, mtj * 128:(mtj + 1) * 128]
                    t.matmul(c.banks[b0][:, :], lhsT, hb[:, kt * T: kt * T + 512],
                             start=(kt == 0), stop=(kt == KT - 1), skip_group_check=True)
                    ins = t.matmul(c.banks[b1][:, :], lhsT, hb[:, kt * T + 512: kt * T + 1024],
                             start=(kt == 0), stop=(kt == KT - 1), skip_group_check=True)
                ins.then_inc(c.s_pe, 1)
            S.emit("tensor", gu_group)
            tag = S.inc("pe")
            if j < 6:
                for tb, b in ((0, b0), (1, b1)):
                    sg = silu_prev.get((j, tb), 0)
                    S.emit("vector", lambda v, tag=tag, j=j, tb=tb, b=b, sg=sg: (
                        v.wait_ge(c.s_pe, tag),
                        (v.wait_ge(c.s_act, sg) if sg else None),
                        v.tensor_copy(gsc[:, j * T + tb * 512: j * T + tb * 512 + 512],
                                      c.banks[b][:, :]).then_inc(c.s_ev, 1)))
                    c.bank_free[b] = S.inc("ev")
                    g_evac[(j, tb)] = c.bank_free[b]
            else:
                ju = j - 6
                for tb, b in ((0, b0), (1, b1)):
                    gt = max(g_evac[(ju, tb)], c.moe_silu_free[tb])
                    S.emit("scalar", lambda a, gt=gt, ju=ju, tb=tb: (
                        a.wait_ge(c.s_ev, gt),
                        a.activation(silu_t[:, tb * 512:(tb + 1) * 512],
                                     gsc[:, ju * T + tb * 512: ju * T + tb * 512 + 512],
                                     AF.Silu).then_inc(c.s_act, 1)))
                    tag_s = S.inc("act")
                    silu_prev[(ju, tb)] = tag_s
                    qf = c.moe_q_free[tb]
                    S.emit("vector", lambda v, tag=tag, ts=tag_s, tb=tb, b=b, qf=qf: (
                        v.wait_ge(c.s_pe, tag), v.wait_ge(c.s_act, ts),
                        (v.wait_ge(c.s_pool, qf) if qf else None),
                        v.tensor_mul(q_t[:, tb * 512:(tb + 1) * 512],
                                     silu_t[:, tb * 512:(tb + 1) * 512],
                                     c.banks[b][:, :]).then_inc(c.s_ev, 1)))
                    c.bank_free[b] = S.inc("ev")
                    tag_q = c.bank_free[b]
                    c.moe_silu_free[tb] = tag_q
                    cs = (c.crow_phase + e) % 2
                    S.emit("gpsimd", lambda g, tq=tag_q, ju=ju, tb=tb, cs=cs: (
                        g.wait_ge(c.s_ev, tq),
                        g.tensor_mul(hbuf[:, ju * T + tb * 512: ju * T + tb * 512 + 512],
                                     q_t[:, tb * 512:(tb + 1) * 512],
                                     cb[:, cs * T + tb * 512: cs * T + tb * 512 + 512]).then_inc(c.s_pool, 1)))
                    pool_done[tb] = S.inc("pool")
                    c.moe_q_free[tb] = pool_done[tb]
            if j in (3, 7, 11):
                chunk_free[5 * e + j // 4] = S.get("ev")
                advance()
        pool_all[e] = S.get("pool")
        if e + 1 < NE:
            emit_bcast(e + 1)
        for dh in range(2):
            dchunk = 5 * e + 3 + dh
            wait_chunk(c, "tensor", rdy[dchunk])
            S.emit("tensor", lambda t, v=pool_done[0]: t.wait_ge(c.s_pool, v))
            S.emit("tensor", lambda t, v=pool_done[1]: t.wait_ge(c.s_pool, v))
            w = slot_ap(c, dchunk % 2, 6, 1024)
            for mo, tag, b0, b1 in proj_groups(c, w, 6, range(8), hbuf, 0):
                evac_add_xacc(c, dh * 8 + mo, tag, b0, b1)
            chunk_free[dchunk] = S.get("ev")
            advance()
    end_ring(c, chunk_free, len(chunks))
    c.crow_phase += NE


def evac_add_xacc_mo(c, mo, tag, b0, b1):
    evac_add_xacc(c, mo, tag, b0, b1)



def emit_s3(c, dram):
    S = c.S
    h3 = arena_v(c, 0, KT * T)
    qbuf = arena_v(c, 16384, 16384)
    kstage = [arena_v(c, 32768, 1024), arena_v(c, 33792, 1024)]
    vtst = arena_v(c, 34816, 1024)
    wqT = dram["wqT"]

    def dma_qkv(j, slot):
        src = wqT.rearrange("(kt p) n -> p kt n", p=128)[:, :, j * 512:(j + 1) * 512]
        dst = slot_ap(c, slot, 16, 512)
        return ring_load(c, slot, [(dst[:, 0:8, :], src[:, 0:8, :]), (dst[:, 8:16, :], src[:, 8:16, :])])

    order = [4, 5, 6, 7, 8, 9, 10, 11, 0, 1, 2, 3]
    rdy, chunk_free = {}, {}
    ci_next = 0
    def advance():
        nonlocal ci_next
        if ci_next < len(order):
            ci = ci_next
            if ci >= 2:
                c.slot_free[ci % 2] = chunk_free[ci - 2]
            rdy[ci] = dma_qkv(order[ci], ci % 2)
            ci_next += 1
    advance(); advance()
    kst_free = [0, 0]
    h3_ready = S.get("ev")

    for ci, j in enumerate(order):
        wait_chunk(c, "tensor", rdy[ci])
        w = slot_ap(c, ci % 2, 16, 512)
        kind = "q" if j < 4 else ("k" if j < 8 else "v")
        for mt, tag, b0, b1 in proj_groups(c, w, KT, range(4), h3, 0,
                                           pre_wait=(h3_ready if ci == 0 else None)):
            fo = (j % 4) * 4 + mt
            bcol = {"q": fo, "k": 16 + fo, "v": 32 + fo}[kind]
            if kind == "q":
                for tb, b in ((0, b0), (1, b1)):
                    S.emit("scalar", lambda a, tag=tag, fo=fo, tb=tb, b=b, bc=bcol: (
                        a.wait_ge(c.s_pe, tag),
                        a.activation(qbuf[:, fo * T + tb * 512: fo * T + tb * 512 + 512],
                                     c.banks[b][:, :], AF.Identity,
                                     bias=c.bqkvT[:, bc:bc + 1], scale=INV_SQ_D).then_inc(c.s_act, 1)))
                    S.inc("act")
                atv = S.get("act")
                S.emit("vector", lambda v, at=atv: (v.wait_ge(c.s_act, at),
                                                    v.engine_nop().then_inc(c.s_ev, 1)))
                fence = S.inc("ev")
                c.bank_free[b0] = c.bank_free[b1] = fence
            else:
                ksl = fo % 2
                if kst_free[ksl]:
                    S.emit("scalar", lambda a, v=kst_free[ksl], ksl=ksl: a.wait_ge(c.s_ks[ksl], v))
                for tb, b in ((0, b0), (1, b1)):
                    S.emit("scalar", lambda a, tag=tag, tb=tb, b=b, bc=bcol, ksl=ksl: (
                        a.wait_ge(c.s_pe, tag),
                        a.activation(kstage[ksl][:, tb * 512:(tb + 1) * 512], c.banks[b][:, :],
                                     AF.Identity, bias=c.bqkvT[:, bc:bc + 1]).then_inc(c.s_act, 1)))
                    S.inc("act")
                atv = S.get("act")
                S.emit("vector", lambda v, at=atv: (v.wait_ge(c.s_act, at),
                                                    v.engine_nop().then_inc(c.s_ev, 1)))
                fence = S.inc("ev")
                c.bank_free[b0] = c.bank_free[b1] = fence
                if kind == "k":
                    S.emit("sync", lambda s, fence=fence, fo=fo, ksl=ksl: (
                        s.wait_ge(c.s_ev, fence),
                        s.dma_start(out=dram["kT"][fo * 128:(fo + 1) * 128, :],
                                    in_=kstage[ksl]).then_inc(c.s_ks[ksl], 16)))
                    kst_free[ksl] = S.inc("ks%d" % ksl, 16)
                else:
                    vg = c.vdma_prev
                    if vg is not None:
                        S.emit("tensor", lambda t, vg=vg: t.wait_ge(c.s_ks[vg[0]], vg[1]))
                    for jb in range(8):
                        pi = jb % 2
                        pst = c.psT if pi == 0 else c.psU
                        if c.pst_free[pi]:
                            S.emit("tensor", lambda t, v=c.pst_free[pi]: t.wait_ge(c.s_ev, v))
                        S.emit("tensor", lambda t, jb=jb, fence=fence, pst=pst, ksl=ksl: (
                            (t.wait_ge(c.s_ev, fence) if jb == 0 else None),
                            t.transpose(pst[:, 0:128], kstage[ksl][:, jb * 128:(jb + 1) * 128],
                                        c.ident_bf[:, :]).then_inc(c.s_pe, 1)))
                        ttag = S.inc("pe")
                        S.emit("vector", lambda v, jb=jb, ttag=ttag, pst=pst: (
                            v.wait_ge(c.s_pe, ttag),
                            v.tensor_copy(vtst.rearrange("p (jb f) -> p jb f", f=128)[:, jb, :],
                                          pst[:, 0:128]).then_inc(c.s_ev, 1)))
                        c.pst_free[pi] = S.inc("ev")
                    vt_done = S.get("ev")
                    S.emit("sync", lambda s, vt=vt_done, fo=fo, ksl=ksl: (
                        s.wait_ge(c.s_ev, vt),
                        s.dma_start(out=dram["vtok"][:, fo * 128:(fo + 1) * 128].rearrange("(jb p) f -> p jb f", p=128),
                                    in_=vtst.rearrange("p (jb f) -> p jb f", f=128)).then_inc(c.s_ks[ksl], 16)))
                    kst_free[ksl] = S.inc("ks%d" % ksl, 16)
                    c.vdma_prev = (ksl, kst_free[ksl])
        chunk_free[ci] = S.get("ev")
        advance()
        if ci == 7:
            ks0, ks1 = S.get("ks0"), S.get("ks1")
            S.emit("gpsimd", lambda g, k0=ks0, k1=ks1: (
                g.wait_ge(c.s_ks[0], k0),
                g.wait_ge(c.s_ks[1], k1),
                g.collective_compute(
                    "AllGather", OP.bypass,
                    replica_groups=[[0, 1, 2, 3], [4, 5, 6, 7]],
                    ins=[dram["kvl"][:, :]],
                    outs=[dram["kvag"][:, :]],
                ).then_inc(c.s_coll, 1)))
            S.inc("coll", 1)

    end_ring(c, chunk_free, len(order))
    qkv_done = S.get("ev")
    S.emit("tensor", lambda t, qd=qkv_done: t.wait_ge(c.s_ev, qd))

    ST = arena_v(c, 0, 16384)
    STv = ST.rearrange("p (jt q) -> p jt q", q=512)
    stmps = [arena_v(c, 32768, 512), arena_v(c, 33280, 512)]
    stmp_free = [0, 0]
    ctx = arena_v(c, 32768, 8192)
    ctxv = ctx.rearrange("p (ht q) -> p ht q", q=512)
    nmb = c.fscr[:, 0:512]
    rsb = c.fscr[:, 512:1024]
    rows = c.fscr[0:1, :]
    rmrow = rows[:, 2048:2560]
    asm4 = c.fscr[0:4, 2560:2688]
    et = c.fscr[:, 1024:1536]
    coll_done = S.get("coll")
    woatT = dram["woatT"]
    allb = [c.psA, c.psB, c.psC, c.psD, c.psE, c.psF]

    for half in range(2):
        q0 = half * 512
        S.emit("vector", lambda v: v.memset(c.rmax[:, 0:4], -1e30).then_inc(c.s_ev, 1))
        rm_init = S.inc("ev")
        for chn in range(8):
            r, tb2 = chn // 2, chn % 2
            slot = chn % 2
            src = dram["kvag"][2 * r].rearrange("(kt p t) -> p kt t", p=128, t=T)[:, :, tb2 * 512:(tb2 + 1) * 512]
            if c.slot_free[slot]:
                S.emit("gpsimd", lambda s, v=c.slot_free[slot]: s.wait_ge(c.s_ev, v))
            if half == 0 and chn < 2:
                S.emit("gpsimd", lambda s, cd=coll_done: s.wait_ge(c.s_coll, cd))
            dst = slot_ap(c, slot, 16, 512)
            S.emit("gpsimd", lambda s, d=dst, sr=src, sl=slot: s.dma_start(out=d[:, :, :], in_=sr).then_inc(c.s_ld[sl], 16))
            c.slot_val[slot] += 16
            krdy = (slot, c.slot_val[slot])
            wait_chunk(c, "tensor", krdy)
            w = slot_ap(c, slot, 16, 512)
            for qt in range(4):
                bq = qt % 2
                pe_bank(c, bq)
                def sc_group(t, qt=qt, bq=bq, w=w, q0=q0):
                    for kt in range(KT):
                        ins = t.matmul(c.banks[bq][:, :],
                                       qbuf[:, kt * T + q0 + qt * 128: kt * T + q0 + (qt + 1) * 128],
                                       w[:, kt, :], start=(kt == 0), stop=(kt == KT - 1),
                                       skip_group_check=True)
                    ins.then_inc(c.s_pe, 1)
                S.emit("tensor", sc_group)
                tag = S.inc("pe")
                sp = (chn * 4 + qt) % 2
                stw = stmp_free[sp]
                S.emit("vector", lambda v, tag=tag, qt=qt, bq=bq, sp=sp, stw=stw: (
                    v.wait_ge(c.s_pe, tag),
                    (v.wait_ge(c.s_pe, stw) if stw else None),
                    v.tensor_copy(stmps[sp], c.banks[bq][:, :]),
                    v.reduce_max(c.m8[:, qt:qt + 1], c.banks[bq][:, :], AX.X),
                    v.drain(),
                    v.tensor_max(c.rmax[:, qt:qt + 1], c.rmax[:, qt:qt + 1],
                                 c.m8[:, qt:qt + 1]).then_inc(c.s_ev, 1)))
                c.bank_free[bq] = S.inc("ev")
                ev_st = c.bank_free[bq]
                for i2 in range(4):
                    pi = i2 % 2
                    pst = c.psT if pi == 0 else c.psU
                    if c.pst_free[pi]:
                        S.emit("tensor", lambda t, v=c.pst_free[pi]: t.wait_ge(c.s_ev, v))
                    S.emit("tensor", lambda t, i2=i2, pst=pst, es=ev_st, sp=sp: (
                        (t.wait_ge(c.s_ev, es) if i2 == 0 else None),
                        t.transpose(pst[:, 0:128],
                                    stmps[sp].rearrange("p (a b) -> p a b", a=4)[:, i2, :],
                                    c.ident_bf[:, :]).then_inc(c.s_pe, 1)))
                    tt = S.inc("pe")
                    jt = chn * 4 + i2
                    S.emit("vector", lambda v, tt=tt, jt=jt, qt=qt, pst=pst: (
                        v.wait_ge(c.s_pe, tt),
                        v.tensor_copy(STv[:, jt, qt * 128:(qt + 1) * 128],
                                      pst[:, 0:128]).then_inc(c.s_ev, 1)))
                    c.pst_free[pi] = S.inc("ev")
                stmp_free[sp] = S.get("pe")
            c.slot_free[slot] = S.get("ev")
        sc_done = S.get("ev")
        if c.aux_free[0]:
            S.emit("tensor", lambda t, v=c.aux_free[0]: t.wait_ge(c.s_ev, v))
        S.emit("tensor", lambda t, sd=sc_done: (
            t.wait_ge(c.s_ev, sd),
            t.transpose(c.psE[0:4, 0:128], c.rmax[:, 0:4], c.ident_f[:, :]).then_inc(c.s_pe, 1)))
        t1 = S.inc("pe")
        S.emit("vector", lambda v, t1=t1: (
            v.wait_ge(c.s_pe, t1),
            v.tensor_copy(asm4, c.psE[0:4, 0:128]).then_inc(c.s_ev, 1)))
        c.aux_free[0] = S.inc("ev")
        t2 = c.aux_free[0]
        S.emit("sync", lambda s, t2=t2: (
            s.wait_ge(c.s_ev, t2),
            s.dma_start(out=dram["rscr"][:, :], in_=asm4).then_inc(c.s_xl[0], 16)))
        b1 = S.inc("xl0", 16)
        S.emit("sync", lambda s, b1=b1: (
            s.wait_ge(c.s_xl[0], b1),
            s.dma_start(out=rmrow, in_=dram["rscr"].rearrange("a b -> (a b)").rearrange("(o n) -> o n", o=1)).then_inc(c.s_xl[1], 16)))
        b2 = S.inc("xl1", 16)
        S.emit("vector", lambda v, b2=b2: (
            v.wait_ge(c.s_xl[1], b2),
            v.tensor_scalar_mul(rmrow, rmrow, -1.0).then_inc(c.s_ev, 1)))
        nm_pre = S.inc("ev")
        nm_done = bcast_row(c, rmrow, nmb, 0, width=512, pre_ev=nm_pre)
        ets = [c.fscr[:, 1024:1536], c.fscr[:, 1536:2048]]
        et_free = [0, 0]
        for jt in range(32):
            sl2 = jt % 2
            S.emit("vector", lambda v, jt=jt, nd=nm_done, sl2=sl2, ef=et_free[sl2]: (
                (v.wait_ge(c.s_ev, nd) if jt == 0 else None),
                (v.wait_ge(c.s_act, ef) if ef else None),
                v.tensor_tensor(ets[sl2], STv[:, jt, :], nmb, OP.add).then_inc(c.s_ev, 1)))
            vt = S.inc("ev")
            S.emit("scalar", lambda a, jt=jt, vt=vt, sl2=sl2: (
                a.wait_ge(c.s_ev, vt),
                a.activation(STv[:, jt, :], ets[sl2], AF.Exp).then_inc(c.s_act, 1)))
            et_free[sl2] = S.inc("act")
        exp_done = S.get("act")
        if c.aux_free[0]:
            S.emit("tensor", lambda t, v=c.aux_free[0]: t.wait_ge(c.s_ev, v))
        def sum_mm(t, ed=exp_done):
            t.wait_ge(c.s_act, ed)
            for jt in range(32):
                ins = t.matmul(c.psE[0:1, :], c.ones_bf[:, :], STv[:, jt, :],
                               start=(jt == 0), stop=(jt == 31), skip_group_check=True)
            ins.then_inc(c.s_pe, 1)
        S.emit("tensor", sum_mm)
        smt = S.inc("pe")
        S.emit("vector", lambda v, smt=smt: (
            v.wait_ge(c.s_pe, smt),
            v.tensor_copy(rmrow, c.psE[0:1, :]),
            v.drain(),
            v.tensor_scalar_max(rmrow, rmrow, 1e-30),
            v.drain(),
            v.reciprocal(rmrow, rmrow).then_inc(c.s_ev, 1)))
        c.aux_free[0] = S.inc("ev")
        rs_done = bcast_row(c, rmrow, rsb, 1, width=512, pre_ev=c.aux_free[0])
        for b in range(4):
            pe_bank(c, b)
        vfree = [0, 0]
        for hg, hts in enumerate((range(0, 6), range(6, 12), range(12, 16))):
            hts = list(hts)
            for b in range(4):
                pe_bank(c, b)
            for i in range(2):
                if c.aux_free[i]:
                    S.emit("tensor", lambda t, v=c.aux_free[i]: t.wait_ge(c.s_ev, v))
            vrdy = {}
            def v_load(jb, hg=hg):
                slot = jb % 2
                if vfree[slot]:
                    S.emit("gpsimd", lambda s, v=vfree[slot]: s.wait_ge(c.s_pe, v))
                elif c.slot_free[slot]:
                    S.emit("gpsimd", lambda s, v=c.slot_free[slot]: s.wait_ge(c.s_ev, v))
                r2, jl = jb // 8, jb % 8
                src = dram["kvag"][2 * r2 + 1].rearrange("(t d) -> t d", d=D)[jl * 128:(jl + 1) * 128, :]
                dst = arena_v(c, RING_O + slot * SLOT_N, 2048)
                S.emit("gpsimd", lambda s, d=dst, sr=src, sl=slot: s.dma_start(out=d, in_=sr).then_inc(c.s_ld[sl], 16))
                c.slot_val[slot] += 16
                vrdy[jb] = (slot, c.slot_val[slot])
            v_load(0); v_load(1)
            for jb in range(32):
                wait_chunk(c, "tensor", vrdy[jb])
                slot = jb % 2
                vsl = arena_v(c, RING_O + slot * SLOT_N, 2048)
                def ctx_mms(t, jb=jb, hts=hts, vsl=vsl, rd=rs_done):
                    if jb == 0:
                        t.wait_ge(c.s_ev, rd)
                    for hi, ht in enumerate(hts):
                        ins = t.matmul(allb[hi][:, :], vsl[:, ht * 128:(ht + 1) * 128],
                                       STv[:, jb, :], start=(jb == 0), stop=(jb == 31),
                                       skip_group_check=True)
                    ins.then_inc(c.s_pe, 1)
                S.emit("tensor", ctx_mms)
                tag = S.inc("pe")
                vfree[slot] = tag
                if jb + 2 < 32:
                    v_load(jb + 2)
            ctx_tag = S.get("pe")
            for hi, ht in enumerate(hts):
                S.emit("vector", lambda v, hi=hi, ht=ht, ct=ctx_tag: (
                    v.wait_ge(c.s_pe, ct),
                    v.tensor_mul(ctxv[:, ht, :], allb[hi][:, :], rsb).then_inc(c.s_ev, 1)))
                tagv = S.inc("ev")
                if hi < 4:
                    c.bank_free[hi] = tagv
                else:
                    c.aux_free[hi - 4] = tagv
        ctx_done = S.get("ev")
        c.slot_free = [max(c.slot_free[0], vfree[0] and 0), 0]
        c.slot_free = [ctx_done, ctx_done]
        for ch4 in range(4):
            src = woatT.rearrange("(kt p) n -> p kt n", p=128)[:, :, ch4 * 512:(ch4 + 1) * 512]
            slot = ch4 % 2
            dst = slot_ap(c, slot, 16, 512)
            ordy = ring_load(c, slot, [(dst[:, 0:8, :], src[:, 0:8, :]), (dst[:, 8:16, :], src[:, 8:16, :])])
            wait_chunk(c, "tensor", ordy)
            w = slot_ap(c, slot, 16, 512)
            for mt in range(4):
                mo = ch4 * 4 + mt
                b0 = mt % 2
                b1 = 2 + mt % 2
                pe_bank2(c, b0, b1)
                def op_group(t, mo=mo, b0=b0, w=w, cd=ctx_done):
                    if mo == 0:
                        t.wait_ge(c.s_ev, cd)
                    for kt in range(KT):
                        ins = t.matmul(c.banks[b0][:, :], w[:, kt, (mo % 4) * 128:((mo % 4) + 1) * 128],
                                       ctxv[:, kt, :], start=(kt == 0), stop=(kt == KT - 1),
                                       skip_group_check=True)
                    ins.then_inc(c.s_pe, 1)
                S.emit("tensor", op_group)
                tag = S.inc("pe")
                S.emit("vector", lambda v, tag=tag, mo=mo, b0=b0, q0=q0: (
                    v.wait_ge(c.s_pe, tag),
                    v.scalar_tensor_tensor(c.xacc[:, mo * T + q0: mo * T + q0 + 512],
                                           c.banks[b0][:, :], c.boT[:, mo:mo + 1],
                                           c.xacc[:, mo * T + q0: mo * T + q0 + 512],
                                           OP.add, OP.add).then_inc(c.s_ev, 1)))
                c.bank_free[b0] = S.inc("ev")
            c.slot_free[slot] = S.get("ev")



def build(upto=4):
    from contextlib import ExitStack
    nc = bass.Bass("TRN2", target_bir_lowering=False, debug=False)
    dram = {}
    def din(name, shape, dt=F32):
        dram[name] = nc.dram_tensor(name, shape, dt, kind="ExternalInput").ap()
    din("xT", [D, T]); din("xhT", [128, 48])
    din("winT", [D, 8192]); din("convw_d", [128, 128]); din("convb_d", [128, 32])
    din("woT", [4096, D]); din("w1T", [D, DHID := 8192]); din("w2T", [8192, D])
    din("gw128", [128, KT * 32]); din("alpha", [1, 1])
    din("guT", [NE, D, 1536]); din("dnT", [NE, 768, D])
    din("wqT", [D, 6144]); din("bq128", [128, 48]); din("woatT", [D, D]); din("bo128", [128, 16])
    din("ones1", [1, 128]); din("ones128", [128, 1]); din("ident", [128, 128])
    din("cvals_d", [128, len(CONSTS)])
    dram["out"] = nc.dram_tensor("out", [D, T], F32, kind="ExternalOutput").ap()
    dram["crows"] = nc.dram_tensor("crows", [NE, T], F32, kind="Internal").ap()
    kvl = nc.dram_tensor("kvl", [2, D * T], BF16, kind="Internal").ap()
    kvag = nc.dram_tensor("kvag", [8, D * T], BF16, kind="Internal").ap()
    dram["kvl"] = kvl
    dram["kvag"] = kvag
    dram["kT"] = kvl[0].rearrange("(f t) -> f t", t=T)
    dram["vtok"] = kvl[1].rearrange("(t d) -> t d", d=D)
    dram["Kfull"] = kvag
    dram["Vfull"] = kvag
    dram["rscr"] = nc.dram_tensor("rscr", [4, 128], F32, kind="Internal").ap()

    es = ExitStack()
    with es:
        c = _mk_ctx(nc, es)
        for i, v in enumerate(CONSTS):
            nc.const_aps.register(F32, v, c.cvals[:, i:i + 1])
        S = c.S
        def aux_dma(q, dst, src, cast=False):
            sm = c.s_aux if q == "sync" else c.s_auxg
            nm = "aux" if q == "sync" else "auxg"
            S.emit(q, lambda g, d=dst, s=src, sm=sm: g.dma_start(out=d, in_=s).then_inc(sm, 16))
            S.inc(nm, 16)
        xr = dram["xT"].rearrange("(kt p) t -> p kt t", p=128)
        xa = c.xacc.rearrange("p (kt t) -> p kt t", t=T)
        for i in range(4):
            aux_dma("sync", xa[:, i * 4:(i + 1) * 4, :], xr[:, i * 4:(i + 1) * 4, :])
        aux_dma("sync", c.xh[:, :], dram["xhT"])
        aux_dma("sync", c.convw[:, :], dram["convw_d"])
        aux_dma("sync", c.convb[:, :], dram["convb_d"])
        aux_dma("sync", c.bqkvT[:, :], dram["bq128"])
        aux_dma("sync", c.boT[:, :], dram["bo128"])
        aux_dma("sync", c.ones_f1[:, :], dram["ones1"])
        aux_dma("sync", c.ones_f128[:, :], dram["ones128"])
        aux_dma("sync", c.ident_f[:, :], dram["ident"])
        aux_dma("sync", c.alpha_sb[:, :], dram["alpha"])
        aux_dma("gpsimd", c.ones_bf[:, :], dram["ones128"])
        aux_dma("gpsimd", c.ident_bf[:, :], dram["ident"])
        aux_dma("gpsimd", c.gwT[:, :], dram["gw128"])
        aux_dma("sync", c.cvals[:, :], dram["cvals_d"])
        naux, nauxg = S.get("aux"), S.get("auxg")
        for eng in ("tensor", "vector", "scalar"):
            S.emit(eng, lambda t, na=naux: t.wait_ge(c.s_aux, na))
            S.emit(eng, lambda t, na=nauxg: t.wait_ge(c.s_auxg, na))
        S.emit("vector", lambda v: v.tensor_scalar_mul(c.bqkvT[:, 0:16], c.bqkvT[:, 0:16], INV_SQ_D).then_inc(c.s_ev, 1))
        S.inc("ev")
        S.emit("tensor", lambda t: t.matmul(c.psE[:, 0:1], c.ones_f1[:, :], c.alpha_sb[:, :],
                                            start=True, stop=True).then_inc(c.s_pe, 1))
        aptag = S.inc("pe")
        S.emit("vector", lambda v, ap=aptag: (
            v.wait_ge(c.s_pe, ap),
            v.tensor_copy(c.al128[:, 0:1], c.psE[:, 0:1]).then_inc(c.s_ev, 1)))
        c.aux_free[0] = S.inc("ev")

        emit_s1(c, dram)
        if upto >= 2:
            emit_ln(c, "ln_hb", dst_hb=arena_v(c, 0, KT * T))
            emit_gating(c, dram)
            emit_moe(c, dram)
        if upto >= 3:
            emit_ln(c, "ln_hb", dst_hb=arena_v(c, 0, KT * T))
            emit_s3(c, dram)
        if upto >= 4:
            emit_ln(c, "ln_hb", dst_hb=arena_v(c, 0, KT * T))
            emit_gating(c, dram)
            emit_moe(c, dram)
        done = S.get("ev")
        outr = dram["out"].rearrange("(kt p) t -> p kt t", p=128)
        S.emit("sync", lambda s, d=done: s.wait_ge(c.s_ev, d))
        for i in range(4):
            S.emit("sync", lambda s, i=i: s.dma_start(out=outr[:, i * 4:(i + 1) * 4, :],
                                                      in_=xa[:, i * 4:(i + 1) * 4, :]).then_inc(c.s_st, 16))
        S.emit("sync", lambda s: s.wait_ge(c.s_st, 64))

        with nc.Block() as block:
            @block.tensor
            def _(t):
                for f in S.ops["tensor"]: f(t)
            @block.vector
            def _(v):
                for f in S.ops["vector"]: f(v)
            @block.scalar
            def _(a):
                for f in S.ops["scalar"]: f(a)
            @block.gpsimd
            def _(g):
                for f in S.ops["gpsimd"]: f(g)
            @block.sync
            def _(s):
                for f in S.ops["sync"]: f(s)
    return nc


def _mk_ctx(nc, es):
    c = Ctx()
    c.nc = nc
    c.S = Sched()
    sb = lambda name, shape, dt: es.enter_context(nc.sbuf_tensor(name, shape, dt))
    ps = lambda name, shape, dt: es.enter_context(nc.psum_tensor(name, shape, dt))
    sem = lambda name: es.enter_context(nc.semaphore(name))
    c.xacc = sb("xacc", [128, KT * T], F32)
    c.arena = sb("arena", [128, ARENA_N], BF16)
    c.fscr = sb("fscr", [128, FSCR_N], F32)
    c.ones_bf = sb("ones_bf", [128, 1], BF16)
    c.ones_f1 = sb("ones_f1", [1, 128], F32)
    c.ones_f128 = sb("ones_f128", [128, 1], F32)
    c.ident_bf = sb("ident_bf", [128, 128], BF16)
    c.ident_f = sb("ident_f", [128, 128], F32)
    c.convw = sb("convw", [128, 128], F32)
    c.convb = sb("convb", [128, 32], F32)
    c.bqkvT = sb("bqkvT", [128, 48], F32)
    c.boT = sb("boT", [128, 16], F32)
    c.gwT = sb("gwT", [128, KT * 32], BF16)
    c.alpha_sb = sb("alpha_sb", [1, 1], F32)
    c.al128 = sb("al128", [128, 1], F32)
    c.cvals = sb("cvals", [128, 10], F32)
    c.xh = sb("xh", [128, KT * 3], F32)
    c.uh = sb("uh", [128, KT * 3], BF16)
    c.rmax = sb("rmax", [128, 8], F32)
    c.m8 = sb("m8", [128, 8], F32)
    c.psA = ps("psA", [128, 512], F32)
    c.psB = ps("psB", [128, 512], F32)
    c.psC = ps("psC", [128, 512], F32)
    c.psD = ps("psD", [128, 512], F32)
    c.psE = ps("psE", [128, 512], F32)
    c.psF = ps("psF", [128, 512], F32)
    c.psT = ps("psT", [128, 512], BF16)
    c.psU = ps("psU", [128, 512], BF16)
    c.banks = [c.psA, c.psB, c.psC, c.psD]
    c.bank_free = [0, 0, 0, 0]
    c.aux_free = [0, 0]
    c.pst_free = [0, 0]
    c.s_pe = sem("s_pe"); c.s_ev = sem("s_ev"); c.s_act = sem("s_act")
    c.s_pool = sem("s_pool")
    c.s_ld = [sem("s_ld0"), sem("s_ld1")]
    c.s_aux = sem("s_aux"); c.s_auxg = sem("s_auxg")
    c.s_cr = [sem("s_cr0"), sem("s_cr1")]
    c.s_crs = sem("s_crs")
    c.s_ks = [sem("s_ks0"), sem("s_ks1")]; c.s_coll = sem("s_coll"); c.s_st = sem("s_st")
    c.s_xl = [sem("s_xl0"), sem("s_xl1")]
    c.slot_free = [0, 0]
    c.slot_val = [0, 0]
    c.crow_ap = [c.fscr[0:1, 2048:3072], c.fscr[0:1, 5120:6144]]
    c.crow_free = [0, 0]
    c.crow_phase = 0
    c.vdma_prev = None
    c.ln_tmp_free = [0, 0]
    c.conv_tap_done = [0, 0]
    c.conv_ymul_done = [0, 0]
    c.moe_silu_free = [0, 0]
    c.moe_q_free = [0, 0]
    c.ltok_tags = {}
    return c



def host_inputs(inputs, upto=4):
    x = np.asarray(inputs["L0_attn_a_x"], dtype=np.float32)
    f32 = lambda a: np.ascontiguousarray(np.asarray(a, dtype=np.float32))
    W_in = f32(inputs["L0_attn_a_W_in"])
    conv_w = f32(inputs["L0_attn_a_conv_w"])
    conv_b = f32(inputs["L0_attn_a_conv_b"])
    W_out = f32(inputs["L0_attn_a_W_out"])
    W1 = f32(inputs["L0_attn_a_W_mlp1"]); W2 = f32(inputs["L0_attn_a_W_mlp2"])
    wqkv = f32(inputs["L0_attn_b_w_qkv"]); bqkv = f32(inputs["L0_attn_b_b_qkv"])
    wo = f32(inputs["L0_attn_b_w_out"]); bo = f32(inputs["L0_attn_b_b_out"])
    gw = f32(inputs["L0_ffn_gate_weight"])
    gup = f32(inputs["L0_ffn_gate_up_proj"]); dnp = f32(inputs["L0_ffn_down_proj"])
    alpha = f32(inputs["L0_ffn_layer_alpha"]).reshape(1, 1)

    shared = {
        "winT": np.ascontiguousarray(W_in.T),
        "convw_d": np.ascontiguousarray(conv_w[:, 0, :].reshape(32, 128, 4).transpose(1, 0, 2).reshape(128, 128)),
        "convb_d": np.ascontiguousarray(conv_b.reshape(32, 128).T),
        "woT": np.ascontiguousarray(W_out.T),
        "w1T": np.ascontiguousarray(W1.T),
        "w2T": np.ascontiguousarray(W2.T),
        "gw128": np.ascontiguousarray(gw.T.reshape(16, 128, 32).transpose(1, 0, 2).reshape(128, 512)),
        "alpha": alpha,
        "guT": np.ascontiguousarray(gup.transpose(0, 2, 1)),
        "dnT": np.ascontiguousarray(dnp.transpose(0, 2, 1)),
        "wqT": np.ascontiguousarray(wqkv.T),
        "bq128": np.ascontiguousarray(bqkv.reshape(48, 128).T),
        "woatT": np.ascontiguousarray(wo.T),
        "bo128": np.ascontiguousarray(bo.reshape(16, 128).T),
        "ones1": np.ones((1, 128), dtype=np.float32),
        "ones128": np.ones((128, 1), dtype=np.float32),
        "ident": np.eye(128, dtype=np.float32),
        "cvals_d": np.tile(np.asarray(CONSTS, dtype=np.float32)[None, :], (128, 1)),
    }
    in_maps = []
    for core in range(8):
        b, cc = core // 4, core % 4
        xs = x[b, cc * T:(cc + 1) * T, :]
        m = dict(shared)
        m["xT"] = np.ascontiguousarray(xs.T)
        if cc == 0:
            xh = np.zeros((3, 2048), dtype=np.float32)
        else:
            xh = x[b, cc * T - 3:cc * T, :]
        m["xhT"] = np.ascontiguousarray(xh.T.reshape(16, 128, 3).transpose(1, 0, 2).reshape(128, 48))
        in_maps.append(m)
    return in_maps


def kernel(**inputs):
    in_maps = host_inputs(inputs)
    nc = build(upto=4)
    res = run_bass_kernel_spmd(nc, in_maps, core_ids=list(range(8)))
    out = np.empty((2, 4096, 2048), dtype=np.float32)
    for core in range(8):
        b, cc = core // 4, core % 4
        out[b, cc * T:(cc + 1) * T, :] = res.results[core]["out"].T
    return out


# version 05 (final)
# speedup vs baseline: 1.0520x; 1.0520x over previous
import numpy as np
import ml_dtypes
import concourse.bass as bass
import concourse.mybir as mybir
from concourse.bass_utils import run_bass_kernel_spmd

F32 = mybir.dt.float32
BF16 = mybir.dt.bfloat16
AF = mybir.ActivationFunctionType
OP = mybir.AluOpType
AX = mybir.AxisListType

D = 2048
T = 1024
KT = 16
NE = 32
TAU = 0.05
EPS_LN = 1e-5
RSQ_D = float(np.sqrt(2048.0))
INV_SQ_D = float(1.0 / np.sqrt(2048.0))

CONSTS = [EPS := 1e-5, 1.0 / 2048.0, -2048.0, 1e-12, float(np.sqrt(2048.0)), -1.0,
          1e-30, 0.5, 0.05, float(1.0 / np.sqrt(2048.0))]

ARENA_N = 57344
RING_O = 40960
SLOT_N = 8192
FSCR_N = 6144


class Sched:
    ENGINES = ("tensor", "vector", "scalar", "gpsimd", "sync")

    def __init__(self):
        self.ops = {e: [] for e in self.ENGINES}
        self.cnt = {}

    def emit(self, eng, fn):
        self.ops[eng].append(fn)

    def inc(self, sem, n=1):
        self.cnt[sem] = self.cnt.get(sem, 0) + n
        return self.cnt[sem]

    def get(self, sem):
        return self.cnt.get(sem, 0)


class Ctx:
    pass


def arena_v(c, off, n):
    return c.arena[:, off:off + n]


def slot_ap(c, slot, n_in, width):
    return arena_v(c, RING_O + slot * SLOT_N, n_in * width).rearrange(
        "p (i w) -> p i w", w=width)


def ring_load(c, slot, dmas, queue="gpsimd"):
    S = c.S
    if c.slot_free[slot]:
        S.emit(queue, lambda g, v=c.slot_free[slot]: g.wait_ge(c.s_ev, v))
    for dst, src in dmas:
        S.emit(queue, lambda g, d=dst, s=src, sl=slot: g.dma_start(out=d, in_=s).then_inc(c.s_ld[sl], 16))
    c.slot_val[slot] += 16 * len(dmas)
    return (slot, c.slot_val[slot])


def end_ring(c, chunk_free, n):
    for ci in (n - 1, n - 2):
        if ci >= 0 and ci in chunk_free:
            c.slot_free[ci % 2] = max(c.slot_free[ci % 2], chunk_free[ci])


def wait_chunk(c, eng, rdy):
    c.S.emit(eng, lambda t, sl=rdy[0], v=rdy[1]: t.wait_ge(c.s_ld[sl], v))


def pe_bank(c, b):
    if c.bank_free[b]:
        c.S.emit("tensor", lambda t, v=c.bank_free[b]: t.wait_ge(c.s_ev, v))


def pe_bank2(c, b0, b1):
    w = max(c.bank_free[b0], c.bank_free[b1])
    if w:
        c.S.emit("tensor", lambda t, v=w: t.wait_ge(c.s_ev, v))



def proj_groups(c, w3, n_kt, mt_list, rhs_src, rhs_base, pre_wait=None):
    S = c.S
    for i, mt in enumerate(mt_list):
        b0, b1 = (0, 1) if i % 2 == 0 else (2, 3)
        pe_bank2(c, b0, b1)
        def group(t, mt=mt, b0=b0, b1=b1, w3=w3, n_kt=n_kt, rhs_src=rhs_src,
                  rhs_base=rhs_base, pw=(pre_wait if i == 0 else None)):
            if pw:
                t.wait_ge(c.s_ev, pw)
            for kt in range(n_kt):
                lhsT = w3[:, kt, mt * 128:(mt + 1) * 128]
                t.matmul(c.banks[b0][:, :], lhsT,
                         rhs_src[:, kt * T + rhs_base: kt * T + rhs_base + 512],
                         start=(kt == 0), stop=(kt == n_kt - 1), skip_group_check=True)
                ins = t.matmul(c.banks[b1][:, :], lhsT,
                         rhs_src[:, kt * T + rhs_base + 512: kt * T + rhs_base + 1024],
                         start=(kt == 0), stop=(kt == n_kt - 1), skip_group_check=True)
            ins.then_inc(c.s_pe, 1)
        S.emit("tensor", group)
        tag = S.inc("pe")
        yield mt, tag, b0, b1


def evac_add_xacc(c, mt, tag, b0, b1, bias_col=None):
    S = c.S
    for tb, b in ((0, b0), (1, b1)):
        if bias_col is None:
            S.emit("vector", lambda v, tag=tag, mt=mt, tb=tb, b=b: (
                v.wait_ge(c.s_pe, tag),
                v.tensor_add(c.xacc[:, mt * T + tb * 512: mt * T + tb * 512 + 512],
                             c.xacc[:, mt * T + tb * 512: mt * T + tb * 512 + 512],
                             c.banks[b][:, :]).then_inc(c.s_ev, 1)))
        else:
            S.emit("vector", lambda v, tag=tag, mt=mt, tb=tb, b=b, bc=bias_col: (
                v.wait_ge(c.s_pe, tag),
                v.scalar_tensor_tensor(c.xacc[:, mt * T + tb * 512: mt * T + tb * 512 + 512],
                                       c.banks[b][:, :], bc,
                                       c.xacc[:, mt * T + tb * 512: mt * T + tb * 512 + 512],
                                       OP.add, OP.add).then_inc(c.s_ev, 1)))
        c.bank_free[b] = S.inc("ev")


def emit_stats(c, want_mean, mrow, s2row):
    S = c.S
    castA = arena_v(c, 16384, 1024)
    castB = arena_v(c, 17408, 1024)
    sqA = arena_v(c, 18432, 1024)
    sqB = arena_v(c, 19456, 1024)
    cast_free = [0, 0]
    pre = S.get("ev")
    for b in range(4):
        pe_bank(c, b)
    for kt in range(KT):
        sl = kt % 2
        cbuf = castA if sl == 0 else castB
        qbuf = sqA if sl == 0 else sqB
        aw = cast_free[sl]
        S.emit("scalar", lambda a, kt=kt, cb=cbuf, aw=aw, pre=pre: (
            (a.wait_ge(c.s_ev, pre) if kt == 0 and pre else None),
            (a.wait_ge(c.s_pe, aw) if aw else None),
            a.copy(cb, c.xacc[:, kt * T:(kt + 1) * T]).then_inc(c.s_act, 1)))
        t_cast = S.inc("act")
        S.emit("vector", lambda v, cb=cbuf, sq=qbuf, tc=t_cast, aw=aw: (
            v.wait_ge(c.s_act, tc),
            (v.wait_ge(c.s_pe, aw) if aw else None),
            v.tensor_mul(sq, cb, cb).then_inc(c.s_ev, 1)))
        t_sq = S.inc("ev")
        def mmst(t, kt=kt, cb=cbuf, sq=qbuf, tc=t_cast, ts=t_sq, wm=want_mean):
            t.wait_ge(c.s_act, tc)
            t.wait_ge(c.s_ev, ts)
            first, last = kt == 0, kt == KT - 1
            if wm:
                t.matmul(c.psA[0:1, :], c.ones_bf[:, :], cb[:, 0:512], start=first, stop=last, skip_group_check=True)
                t.matmul(c.psB[0:1, :], c.ones_bf[:, :], cb[:, 512:1024], start=first, stop=last, skip_group_check=True)
            t.matmul(c.psC[0:1, :], c.ones_bf[:, :], sq[:, 0:512], start=first, stop=last, skip_group_check=True)
            ins = t.matmul(c.psD[0:1, :], c.ones_bf[:, :], sq[:, 512:1024], start=first, stop=last, skip_group_check=True)
            ins.then_inc(c.s_pe, 1)
        S.emit("tensor", mmst)
        cast_free[sl] = S.inc("pe")
    tag = S.get("pe")
    def evst(v, tag=tag, mrow=mrow, s2row=s2row, wm=want_mean):
        v.wait_ge(c.s_pe, tag)
        if wm:
            v.tensor_copy(mrow[:, 0:512], c.psA[0:1, :])
            v.tensor_copy(mrow[:, 512:1024], c.psB[0:1, :])
        v.tensor_copy(s2row[:, 0:512], c.psC[0:1, :])
        v.tensor_copy(s2row[:, 512:1024], c.psD[0:1, :]).then_inc(c.s_ev, 1)
    S.emit("vector", evst)
    done = S.inc("ev")
    for b in range(4):
        c.bank_free[b] = done
    return done


def bcast_row(c, row_ap, dst_f32_ap, aux, width=1024, pre_ev=None):
    S = c.S
    psx = c.psE if aux == 0 else c.psF
    nb = (width + 511) // 512
    for tb in range(nb):
        w = min(512, width - tb * 512)
        if c.aux_free[aux]:
            S.emit("tensor", lambda t, v=c.aux_free[aux]: t.wait_ge(c.s_ev, v))
        pv = pre_ev if tb == 0 else None
        S.emit("tensor", lambda t, tb=tb, r=row_ap, p=psx, w=w, pv=pv: (
            (t.wait_ge(c.s_ev, pv) if pv else None),
            t.matmul(p[:, 0:w], c.ones_f1[:, :], r[:, tb * 512:tb * 512 + w],
                     start=True, stop=True).then_inc(c.s_pe, 1)))
        tag = S.inc("pe")
        S.emit("vector", lambda v, tag=tag, tb=tb, p=psx, d=dst_f32_ap, w=w: (
            v.wait_ge(c.s_pe, tag),
            v.tensor_copy(d[:, tb * 512:tb * 512 + w], p[:, 0:w]).then_inc(c.s_ev, 1)))
        c.aux_free[aux] = S.inc("ev")
    return c.aux_free[aux]


def emit_ln(c, mode, dst_hb=None):
    S = c.S
    rows = c.fscr[0:1, :]
    mrow = rows[:, 2048:3072]
    s2row = rows[:, 3072:4096]
    trow = rows[:, 4096:5120]
    want_mean = mode.startswith("ln")
    st = emit_stats(c, want_mean, mrow, s2row)
    if want_mean:
        def rowm(v, st=st):
            v.wait_ge(c.s_ev, st)
            v.tensor_scalar_mul(mrow, mrow, 1.0 / 2048.0)
            v.drain()
            v.tensor_mul(trow, mrow, mrow)
            v.tensor_scalar(s2row, s2row, 1.0 / 2048.0, None, OP.mult)
            v.drain()
            v.tensor_sub(s2row, s2row, trow).then_inc(c.s_ev, 1)
        S.emit("vector", rowm)
        vtag = S.inc("ev")
        S.emit("scalar", lambda a, vt=vtag: (
            a.wait_ge(c.s_ev, vt),
            a.activation(trow, s2row, AF.Sqrt, bias=EPS_LN).then_inc(c.s_act, 1)))
        atag = S.inc("act")
        S.emit("vector", lambda v, at=atag: (
            v.wait_ge(c.s_act, at),
            v.reciprocal(s2row, trow).then_inc(c.s_ev, 1)))
        rtag = S.inc("ev")
        mb = c.fscr[:, 0:1024]
        rb = c.fscr[:, 1024:2048]
        bcast_row(c, mrow, mb, 0, pre_ev=rtag)
        done = bcast_row(c, s2row, rb, 1, pre_ev=rtag)
        for kt in range(KT):
            if mode == "ln_inplace":
                S.emit("vector", lambda v, kt=kt, d=done: (
                    (v.wait_ge(c.s_ev, d) if kt == 0 else None),
                    v.tensor_sub(c.xacc[:, kt * T:(kt + 1) * T], c.xacc[:, kt * T:(kt + 1) * T], mb).then_inc(c.s_ev, 1)))
                stag = S.inc("ev")
                S.emit("gpsimd", lambda g, kt=kt, stg=stag: (
                    g.wait_ge(c.s_ev, stg),
                    g.tensor_mul(c.xacc[:, kt * T:(kt + 1) * T], c.xacc[:, kt * T:(kt + 1) * T], rb).then_inc(c.s_pool, 1)))
                S.inc("pool")
            else:
                tmp = c.fscr[:, 4096 + (kt % 2) * 512:4096 + (kt % 2) * 512 + 512]
                tmp2 = c.fscr[:, 5120 + (kt % 2) * 512:5120 + (kt % 2) * 512 + 512]
                pw = c.ln_tmp_free[kt % 2]
                S.emit("vector", lambda v, kt=kt, d=done, tmp=tmp, tmp2=tmp2, pw=pw: (
                    (v.wait_ge(c.s_ev, d) if kt == 0 else None),
                    (v.wait_ge(c.s_pool, pw) if pw else None),
                    v.tensor_sub(tmp, c.xacc[:, kt * T:kt * T + 512], mb[:, 0:512]),
                    v.tensor_sub(tmp2, c.xacc[:, kt * T + 512:kt * T + 1024], mb[:, 512:1024]).then_inc(c.s_ev, 1)))
                stag = S.inc("ev")
                S.emit("gpsimd", lambda g, kt=kt, stg=stag, hb=dst_hb, tmp=tmp, tmp2=tmp2: (
                    g.wait_ge(c.s_ev, stg),
                    g.tensor_mul(hb[:, kt * T:kt * T + 512], tmp, rb[:, 0:512]),
                    g.tensor_mul(hb[:, kt * T + 512:kt * T + 1024], tmp2, rb[:, 512:1024]).then_inc(c.s_pool, 1)))
                c.ln_tmp_free[kt % 2] = S.inc("pool")
        pend = S.get("pool")
        S.emit("vector", lambda v, pe=pend: (v.wait_ge(c.s_pool, pe), v.engine_nop().then_inc(c.s_ev, 1)))
        S.inc("ev")
    else:
        S.emit("scalar", lambda a, st=st: (
            a.wait_ge(c.s_ev, st),
            a.activation(trow, s2row, AF.Sqrt).then_inc(c.s_act, 1)))
        atag = S.inc("act")
        def rowr(v, at=atag):
            v.wait_ge(c.s_act, at)
            v.tensor_scalar_max(trow, trow, 1e-12)
            v.drain()
            v.reciprocal(s2row, trow)
            v.drain()
            v.tensor_scalar_mul(s2row, s2row, RSQ_D).then_inc(c.s_ev, 1)
        S.emit("vector", rowr)
        rtag = S.inc("ev")
        sbb = c.fscr[:, 0:1024]
        done = bcast_row(c, s2row, sbb, 0, pre_ev=rtag)
        for kt in range(KT):
            S.emit("vector", lambda v, kt=kt, d=done, hb=dst_hb: (
                (v.wait_ge(c.s_ev, d) if kt == 0 else None),
                v.tensor_mul(hb[:, kt * T:(kt + 1) * T], c.xacc[:, kt * T:(kt + 1) * T], sbb).then_inc(c.s_ev, 1)))
            S.inc("ev")
    return S.get("ev")



def emit_s1(c, dram):
    S = c.S
    hb = arena_v(c, 0, KT * T)
    emit_ln(c, "ln_inplace")
    emit_ln(c, "rms_hb", dst_hb=hb)
    u_ready = S.get("ev")

    rows = c.fscr[0:1, :]
    hm = rows[:, 5120:5123]
    hs = rows[:, 5123:5126]
    ht_ = rows[:, 5126:5129]
    hsq = c.fscr[:, 5632:5680]
    S.emit("vector", lambda v: v.tensor_mul(hsq, c.xh[:, :], c.xh[:, :]).then_inc(c.s_ev, 1))
    hq = S.inc("ev")
    if c.aux_free[0]:
        S.emit("tensor", lambda t, v=c.aux_free[0]: t.wait_ge(c.s_ev, v))
    if c.aux_free[1]:
        S.emit("tensor", lambda t, v=c.aux_free[1]: t.wait_ge(c.s_ev, v))
    def halo_stats(t, hq=hq):
        t.wait_ge(c.s_ev, hq)
        for kt in range(KT):
            t.matmul(c.psE[0:1, 0:3], c.ones_f128[:, :], c.xh[:, kt * 3:(kt + 1) * 3],
                     start=(kt == 0), stop=(kt == KT - 1), skip_group_check=True)
            ins = t.matmul(c.psF[0:1, 0:3], c.ones_f128[:, :], hsq[:, kt * 3:(kt + 1) * 3],
                     start=(kt == 0), stop=(kt == KT - 1), skip_group_check=True)
        ins.then_inc(c.s_pe, 1)
    S.emit("tensor", halo_stats)
    hstag = S.inc("pe")
    def halo_rows(v, hstag=hstag):
        v.wait_ge(c.s_pe, hstag)
        v.tensor_copy(hm, c.psE[0:1, 0:3])
        v.tensor_copy(hs, c.psF[0:1, 0:3])
        v.drain()
        v.tensor_scalar_mul(hm, hm, 1.0 / 2048.0)
        v.drain()
        v.tensor_mul(ht_, hm, hm)
        v.drain()
        v.tensor_scalar(ht_, ht_, -2048.0, None, OP.mult)
        v.drain()
        v.tensor_add(ht_, ht_, hs)
        v.drain()
        v.tensor_scalar_mul(hs, ht_, 1.0 / 2048.0).then_inc(c.s_ev, 1)
    S.emit("vector", halo_rows)
    c.aux_free[0] = c.aux_free[1] = S.inc("ev")
    v1 = c.aux_free[0]
    S.emit("scalar", lambda a, v1=v1: (
        a.wait_ge(c.s_ev, v1),
        a.activation(hs, hs, AF.Sqrt, bias=EPS_LN).then_inc(c.s_act, 1)))
    a1 = S.inc("act")
    def halo_rows2(v, a1=a1):
        v.wait_ge(c.s_act, a1)
        v.reciprocal(hs, hs)
        v.drain()
        v.tensor_mul(ht_, ht_, hs)
        v.drain()
        v.tensor_mul(ht_, ht_, hs).then_inc(c.s_ev, 1)
    S.emit("vector", halo_rows2)
    v2 = S.inc("ev")
    S.emit("scalar", lambda a, v2=v2: (
        a.wait_ge(c.s_ev, v2),
        a.activation(ht_, ht_, AF.Sqrt).then_inc(c.s_act, 1)))
    a2 = S.inc("act")
    def halo_rows3(v, a2=a2):
        v.wait_ge(c.s_act, a2)
        v.tensor_scalar_max(ht_, ht_, 1e-12)
        v.drain()
        v.reciprocal(ht_, ht_)
        v.drain()
        v.tensor_scalar_mul(ht_, ht_, RSQ_D).then_inc(c.s_ev, 1)
    S.emit("vector", halo_rows3)
    v3 = S.inc("ev")
    if c.aux_free[1]:
        S.emit("tensor", lambda t, v=c.aux_free[1]: t.wait_ge(c.s_ev, v))
    S.emit("tensor", lambda t, v3=v3: (
        t.wait_ge(c.s_ev, v3),
        t.matmul(c.psF[:, 0:9], c.ones_f1[:, :], rows[:, 5120:5129],
                 start=True, stop=True).then_inc(c.s_pe, 1)))
    hbtag = S.inc("pe")
    def halo_u(v, hbtag=hbtag):
        v.wait_ge(c.s_pe, hbtag)
        for kt in range(KT):
            v.tensor_sub(hsq[:, kt * 3:(kt + 1) * 3], c.xh[:, kt * 3:(kt + 1) * 3], c.psF[:, 0:3])
        v.drain()
        for kt in range(KT):
            v.tensor_mul(hsq[:, kt * 3:(kt + 1) * 3], hsq[:, kt * 3:(kt + 1) * 3], c.psF[:, 3:6])
        v.drain()
        for kt in range(KT):
            ins = v.tensor_mul(c.uh[:, kt * 3:(kt + 1) * 3], hsq[:, kt * 3:(kt + 1) * 3], c.psF[:, 6:9])
        ins.then_inc(c.s_ev, 1)
    S.emit("vector", halo_u)
    c.aux_free[1] = S.inc("ev")
    uh_tag = c.aux_free[1]

    winT, woT = dram["winT"], dram["woT"]
    xpc = arena_v(c, 16384, 4 * 1027)
    ych = arena_v(c, 20492, 4 * 1024)
    xc = c.fscr[:, 1024:2048]
    xc2 = c.fscr[:, 4096:5120]
    xcs = c.fscr[:, 2048:3072]
    xcs2 = c.fscr[:, 0:1024]
    zs0 = c.fscr[:, 3072:3584]
    zs1 = c.fscr[:, 3584:4096]
    zs0b = c.fscr[:, 5120:5632]
    zs1b = c.fscr[:, 5632:6144]

    def dma_chunk(kind_ch, slot):
        kind, ch = kind_ch
        if kind == "xp":
            src = winT.rearrange("(kt p) n -> p kt n", p=128)[:, :, ch * 512:(ch + 1) * 512]
            dst = slot_ap(c, slot, 16, 512)
            dmas = [(dst[:, 0:8, :], src[:, 0:8, :]), (dst[:, 8:16, :], src[:, 8:16, :])]
        elif kind == "z":
            src = winT.rearrange("(kt p) n -> p kt n", p=128)[:, :, 4096 + ch * 512:4096 + (ch + 1) * 512]
            dst = slot_ap(c, slot, 16, 512)
            dmas = [(dst[:, 0:8, :], src[:, 0:8, :]), (dst[:, 8:16, :], src[:, 8:16, :])]
        else:
            src = woT[ch * 512:(ch + 1) * 512, :].rearrange("(kt p) n -> p kt n", p=128)
            dst = slot_ap(c, slot, 4, 2048)
            dmas = [(dst[:, 0:4, :], src[:, 0:4, :])]
        return ring_load(c, slot, dmas)

    chunks = []
    for ch in range(8):
        chunks += [("xp", ch), ("z", ch), ("wo", ch)]
    rdy, chunk_free = {}, {}
    ci_next = 0

    def advance():
        nonlocal ci_next
        if ci_next < len(chunks):
            ci = ci_next
            if ci >= 2:
                c.slot_free[ci % 2] = chunk_free[ci - 2]
            rdy[ci] = dma_chunk(chunks[ci], ci % 2)
            ci_next += 1

    advance(); advance()
    for ch in range(8):
        ci_xp, ci_z, ci_wo = 3 * ch, 3 * ch + 1, 3 * ch + 2
        wait_chunk(c, "tensor", rdy[ci_xp])
        w = slot_ap(c, ci_xp % 2, 16, 512)
        for mt, tag, b0, b1 in proj_groups(c, w, KT, range(4), hb, 0,
                                           pre_wait=(u_ready if ch == 0 else None)):
            for tb, b in ((0, b0), (1, b1)):
                S.emit("vector", lambda v, tag=tag, mt=mt, tb=tb, b=b: (
                    v.wait_ge(c.s_pe, tag),
                    v.tensor_copy(xpc[:, mt * 1027 + 3 + tb * 512: mt * 1027 + 3 + tb * 512 + 512],
                                  c.banks[b][:, :]).then_inc(c.s_ev, 1)))
                c.bank_free[b] = S.inc("ev")
        if c.aux_free[1]:
            S.emit("tensor", lambda t, v=c.aux_free[1]: t.wait_ge(c.s_ev, v))
        def haloxp_mm(t, w=w, uh_tag=uh_tag):
            t.wait_ge(c.s_ev, uh_tag)
            for mt in range(4):
                for kt in range(KT):
                    ins = t.matmul(c.psF[:, 4 * mt:4 * mt + 3], w[:, kt, mt * 128:(mt + 1) * 128],
                                   c.uh[:, kt * 3:(kt + 1) * 3],
                                   start=(kt == 0), stop=(kt == KT - 1), skip_group_check=True)
            ins.then_inc(c.s_pe, 1)
        S.emit("tensor", haloxp_mm)
        htag = S.inc("pe")
        def haloxp_ev(v, htag=htag):
            v.wait_ge(c.s_pe, htag)
            for mt in range(4):
                ins = v.tensor_copy(xpc[:, mt * 1027: mt * 1027 + 3], c.psF[:, 4 * mt:4 * mt + 3])
            ins.then_inc(c.s_ev, 1)
        S.emit("vector", haloxp_ev)
        c.aux_free[1] = S.inc("ev")
        xp_done = c.aux_free[1]
        chunk_free[ci_xp] = xp_done
        advance()
        wait_chunk(c, "tensor", rdy[ci_z])
        wz = slot_ap(c, ci_z % 2, 16, 512)
        for mt, ztag, b0, b1 in proj_groups(c, wz, KT, range(4), hb, 0):
            mtg = ch * 4 + mt
            par = mt % 2
            xcp = xc if par == 0 else xc2
            S.emit("scalar", lambda a, mt=mt, mtg=mtg, xd=xp_done, xcp=xcp,
                   tg=c.conv_tap_done[par], yg=c.conv_ymul_done[par]: (
                a.wait_ge(c.s_ev, xd),
                (a.wait_ge(c.s_ev, max(tg, yg)) if max(tg, yg) else None),
                a.activation(xcp, xpc[:, mt * 1027 + 0: mt * 1027 + 1024], AF.Identity,
                             bias=c.convb[:, mtg:mtg + 1],
                             scale=c.convw[:, 4 * mtg:4 * mtg + 1]).then_inc(c.s_act, 1)))
            c0 = S.inc("act")
            def conv_taps(v, mt=mt, mtg=mtg, c0=c0, xcp=xcp):
                v.wait_ge(c.s_act, c0)
                for k in (1, 2, 3):
                    ins = v.scalar_tensor_tensor(xcp, xpc[:, mt * 1027 + k: mt * 1027 + k + 1024],
                                           c.convw[:, 4 * mtg + k:4 * mtg + k + 1], xcp,
                                           OP.mult, OP.add)
                    if k < 3:
                        v.drain()
                ins.then_inc(c.s_ev, 1)
            S.emit("vector", conv_taps)
            ctap = S.inc("ev")
            c.conv_tap_done[par] = ctap
            xcsp = xcs if par == 0 else xcs2
            zs0p = zs0 if par == 0 else zs0b
            zs1p = zs1 if par == 0 else zs1b
            def silus(a, ctap=ctap, ztag=ztag, b0=b0, b1=b1, xcp=xcp, xcsp=xcsp,
                      zs0p=zs0p, zs1p=zs1p, yg=c.conv_ymul_done[par]):
                a.wait_ge(c.s_ev, ctap)
                (a.wait_ge(c.s_ev, yg) if yg else None)
                a.activation(xcsp, xcp, AF.Silu)
                a.wait_ge(c.s_pe, ztag)
                a.activation(zs0p, c.banks[b0][:, :], AF.Silu)
                a.activation(zs1p, c.banks[b1][:, :], AF.Silu).then_inc(c.s_act, 1)
            S.emit("scalar", silus)
            stag = S.inc("act")
            def ymul(v, mt=mt, stag=stag, xcsp=xcsp, zs0p=zs0p, zs1p=zs1p):
                v.wait_ge(c.s_act, stag)
                v.tensor_mul(ych[:, mt * 1024: mt * 1024 + 512], xcsp[:, 0:512], zs0p)
                v.tensor_mul(ych[:, mt * 1024 + 512: mt * 1024 + 1024], xcsp[:, 512:1024], zs1p).then_inc(c.s_ev, 1)
            S.emit("vector", ymul)
            yt = S.inc("ev")
            c.conv_ymul_done[par] = yt
            c.bank_free[b0] = yt
            c.bank_free[b1] = yt
        y_done = S.get("ev")
        chunk_free[ci_z] = y_done
        advance()
        wait_chunk(c, "tensor", rdy[ci_wo])
        wo = slot_ap(c, ci_wo % 2, 4, 2048)
        for mo, tag, b0, b1 in proj_groups(c, wo, 4, range(KT), ych, 0, pre_wait=y_done):
            evac_add_xacc(c, mo, tag, b0, b1)
        chunk_free[ci_wo] = S.get("ev")
        advance()
    end_ring(c, chunk_free, len(chunks))

    emit_ln(c, "rms_hb", dst_hb=hb)
    h2_ready = S.get("ev")
    w1T, w2T = dram["w1T"], dram["w2T"]
    achunk = arena_v(c, 16384, 16384)
    def dma_mlp(kind_ch, slot):
        kind, ch = kind_ch
        if kind == "a":
            src = w1T.rearrange("(kt p) n -> p kt n", p=128)[:, :, ch * 512:(ch + 1) * 512]
            dst = slot_ap(c, slot, 16, 512)
            dmas = [(dst[:, 0:8, :], src[:, 0:8, :]), (dst[:, 8:16, :], src[:, 8:16, :])]
        else:
            src = w2T[ch * 512:(ch + 1) * 512, :].rearrange("(kt p) n -> p kt n", p=128)
            dst = slot_ap(c, slot, 4, 2048)
            dmas = [(dst[:, 0:4, :], src[:, 0:4, :])]
        return ring_load(c, slot, dmas)

    mchunks = []
    for big in range(4):
        mchunks += [("a", big * 4 + i) for i in range(4)] + [("w2", big * 4 + i) for i in range(4)]
    mrdy, mfree = {}, {}
    mi = 0
    def madv():
        nonlocal mi
        if mi < len(mchunks):
            if mi >= 2:
                c.slot_free[mi % 2] = mfree[mi - 2]
            mrdy[mi] = dma_mlp(mchunks[mi], mi % 2)
            mi += 1
    madv(); madv()
    for big in range(4):
        for sub in range(4):
            ci = big * 8 + sub
            wait_chunk(c, "tensor", mrdy[ci])
            w = slot_ap(c, ci % 2, 16, 512)
            for mt, tag, b0, b1 in proj_groups(c, w, KT, range(4), hb, 0,
                                               pre_wait=(h2_ready if ci == 0 else None)):
                at = sub * 4 + mt
                for tb, b in ((0, b0), (1, b1)):
                    S.emit("scalar", lambda a, tag=tag, at=at, tb=tb, b=b: (
                        a.wait_ge(c.s_pe, tag),
                        a.activation(achunk[:, at * 1024 + tb * 512: at * 1024 + tb * 512 + 512],
                                     c.banks[b][:, :], AF.Relu).then_inc(c.s_act, 1)))
                    S.inc("act")
                atv = S.get("act")
                S.emit("vector", lambda v, at=atv: (v.wait_ge(c.s_act, at),
                                                    v.engine_nop().then_inc(c.s_ev, 1)))
                fence = S.inc("ev")
                c.bank_free[b0] = fence
                c.bank_free[b1] = fence
            mfree[ci] = S.get("ev")
            madv()
        a_ready = S.get("ev")
        for sub in range(4):
            ci = big * 8 + 4 + sub
            wait_chunk(c, "tensor", mrdy[ci])
            w = slot_ap(c, ci % 2, 4, 2048)
            sub_rhs = arena_v(c, 16384 + sub * 4 * T, 4 * T)
            for mo, tag, b0, b1 in proj_groups(c, w, 4, range(KT), sub_rhs,
                                               0, pre_wait=(a_ready if sub == 0 else None)):
                evac_add_xacc(c, mo, tag, b0, b1)
            mfree[ci] = S.get("ev")
            madv()
    end_ring(c, mfree, len(mchunks))

    land = [c.fscr[:, 0:1024], c.fscr[:, 1024:2048]]
    xr = dram["xT"].rearrange("(kt p) t -> p kt t", p=128)
    land_free = [0, 0]
    S.emit("vector", lambda v: v.drain())
    ev_now, act_now = S.get("ev"), S.get("act")
    S.emit("sync", lambda s, e=ev_now: s.wait_ge(c.s_ev, e))
    S.emit("sync", lambda s, a=act_now: s.wait_ge(c.s_act, a))
    for kt in range(KT):
        sl = kt % 2
        if land_free[sl]:
            S.emit("sync", lambda s, v=land_free[sl]: s.wait_ge(c.s_ev, v))
        S.emit("sync", lambda s, kt=kt, sl=sl: s.dma_start(out=land[sl], in_=xr[:, kt, :]).then_inc(c.s_xl[sl], 16))
        lv = S.inc("xl%d" % sl, 16)
        S.emit("vector", lambda v, kt=kt, sl=sl, lv=lv: (
            v.wait_ge(c.s_xl[sl], lv),
            v.tensor_add(c.xacc[:, kt * T:(kt + 1) * T], c.xacc[:, kt * T:(kt + 1) * T],
                         land[sl]).then_inc(c.s_ev, 1)))
        land_free[sl] = S.inc("ev")



def emit_gating(c, dram):
    S = c.S
    hb = arena_v(c, 0, KT * T)
    logT = c.fscr[0:32, 2048:3072]
    ltok = c.fscr[:, 3072:3328]
    crT = c.fscr[0:32, 5120:6144]
    ptok = c.fscr[:, 3328:3360]
    msk = c.fscr[:, 3400:3432]
    msk2 = c.fscr[:, 3432:3464]
    cw = c.fscr[:, 3360:3368]
    ctile = c.fscr[:, 3464:3496]
    hb_ready = S.get("ev")
    if c.aux_free[0]:
        S.emit("tensor", lambda t, v=c.aux_free[0]: t.wait_ge(c.s_ev, v))
    if c.aux_free[1]:
        S.emit("tensor", lambda t, v=c.aux_free[1]: t.wait_ge(c.s_ev, v))
    def log_mm(t, hr=hb_ready):
        t.wait_ge(c.s_ev, hr)
        for kt in range(KT):
            t.matmul(c.psE[0:32, :], c.gwT[:, kt * 32:(kt + 1) * 32], hb[:, kt * T: kt * T + 512],
                     start=(kt == 0), stop=(kt == KT - 1), skip_group_check=True)
            ins = t.matmul(c.psF[0:32, :], c.gwT[:, kt * 32:(kt + 1) * 32], hb[:, kt * T + 512: kt * T + 1024],
                     start=(kt == 0), stop=(kt == KT - 1), skip_group_check=True)
        ins.then_inc(c.s_pe, 1)
    S.emit("tensor", log_mm)
    lt = S.inc("pe")
    S.emit("vector", lambda v, lt=lt: (
        v.wait_ge(c.s_pe, lt),
        v.tensor_copy(logT[:, 0:512], c.psE[0:32, :]),
        v.tensor_copy(logT[:, 512:1024], c.psF[0:32, :]).then_inc(c.s_ev, 1)))
    c.aux_free[0] = c.aux_free[1] = S.inc("ev")
    lev = c.aux_free[0]
    for i in range(8):
        if c.aux_free[0]:
            S.emit("tensor", lambda t, v=c.aux_free[0]: t.wait_ge(c.s_ev, v))
        S.emit("tensor", lambda t, i=i, lev=lev: (
            (t.wait_ge(c.s_ev, lev) if i == 0 else None),
            t.transpose(c.psE[:, 0:32], logT[:, i * 128:(i + 1) * 128], c.ident_f[0:32, 0:32]).then_inc(c.s_pe, 1)))
        tg = S.inc("pe")
        S.emit("vector", lambda v, i=i, tg=tg: (
            v.wait_ge(c.s_pe, tg),
            v.tensor_copy(ltok[:, i * 32:(i + 1) * 32], c.psE[:, 0:32]).then_inc(c.s_ev, 1)))
        c.aux_free[0] = S.inc("ev")
        c.ltok_tags[i] = c.aux_free[0]
    ltok_tags = dict(c.ltok_tags)
    for i in range(8):
        def gtile1(v, i=i, lt=ltok_tags[i]):
            v.wait_ge(c.s_ev, lt)
            l = ltok[:, i * 32:(i + 1) * 32]
            v.max(out=c.m8[:, :], in_=l)
            v.drain()
            v.tensor_scalar_mul(cw[:, 0:1], c.m8[:, 0:1], -1.0).then_inc(c.s_ev, 1)
        S.emit("vector", gtile1)
        t1 = S.inc("ev")
        S.emit("scalar", lambda a, i=i, t1=t1: (
            a.wait_ge(c.s_ev, t1),
            a.activation(ptok, ltok[:, i * 32:(i + 1) * 32], AF.Exp, bias=cw[:, 0:1]).then_inc(c.s_act, 1)))
        t2 = S.inc("act")
        def gtile2(v, i=i, t2=t2):
            l = ltok[:, i * 32:(i + 1) * 32]
            v.wait_ge(c.s_act, t2)
            v.tensor_scalar(msk, l, c.m8[:, 7:8], None, OP.is_ge)
            v.drain()
            v.tensor_mul(ptok, ptok, msk)
            v.drain()
            v.reduce_sum(cw[:, 1:2], ptok, AX.X)
            v.drain()
            v.tensor_scalar_max(cw[:, 2:3], cw[:, 1:2], 1e-12)
            v.drain()
            v.reciprocal(cw[:, 2:3], cw[:, 2:3])
            v.drain()
            v.tensor_mul(cw[:, 3:4], cw[:, 2:3], c.al128[:, 0:1])
            v.drain()
            v.tensor_scalar(msk, ptok, cw[:, 3:4], TAU, OP.mult, OP.is_ge)
            v.drain()
            v.reduce_sum(cw[:, 4:5], msk, AX.X)
            v.drain()
            v.tensor_scalar(cw[:, 5:6], cw[:, 4:5], 0.5, None, OP.is_lt)
            v.tensor_scalar(msk2, l, c.m8[:, 0:1], None, OP.is_ge)
            v.drain()
            v.tensor_scalar_mul(msk2, msk2, cw[:, 5:6])
            v.drain()
            v.tensor_max(msk, msk, msk2)
            v.drain()
            v.tensor_mul(ptok, ptok, msk)
            v.drain()
            v.reduce_sum(cw[:, 6:7], ptok, AX.X)
            v.drain()
            v.tensor_scalar_max(cw[:, 6:7], cw[:, 6:7], 1e-12)
            v.drain()
            v.reciprocal(cw[:, 6:7], cw[:, 6:7])
            v.drain()
            v.tensor_scalar_mul(ctile, ptok, cw[:, 6:7]).then_inc(c.s_ev, 1)
        S.emit("vector", gtile2)
        t3 = S.inc("ev")
        if c.aux_free[1]:
            S.emit("tensor", lambda t, v=c.aux_free[1]: t.wait_ge(c.s_ev, v))
        S.emit("tensor", lambda t, i=i, t3=t3: (
            t.wait_ge(c.s_ev, t3),
            t.transpose(c.psF[0:32, 0:128], ctile, c.ident_f[:, :]).then_inc(c.s_pe, 1)))
        tg = S.inc("pe")
        S.emit("vector", lambda v, i=i, tg=tg: (
            v.wait_ge(c.s_pe, tg),
            v.tensor_copy(crT[:, i * 128:(i + 1) * 128], c.psF[0:32, 0:128]).then_inc(c.s_ev, 1)))
        c.aux_free[1] = S.inc("ev")
    done = S.get("ev")
    S.emit("sync", lambda s, d=done: s.wait_ge(c.s_ev, d))
    S.emit("sync", lambda s: s.dma_start(out=dram["crows"][:, :], in_=crT).then_inc(c.s_crs, 16))
    S.inc("crs", 16)


def emit_moe(c, dram):
    S = c.S
    hb = arena_v(c, 0, KT * T)
    gsc = arena_v(c, 16384, 6144)
    hbuf = arena_v(c, 22528, 6144)
    cb = c.fscr[:, 0:2048]
    silu_t = c.fscr[:, 3072:4096]
    q_t = c.fscr[:, 4096:5120]
    guT, dnT, crows = dram["guT"], dram["dnT"], dram["crows"]
    crs_need = S.get("crs")

    def dma_moe(ek, slot):
        e, kind = ek
        if kind[0] == "t":
            j = int(kind[1])
            src = guT[e].rearrange("(kt p) n -> p kt n", p=128)[:, :, j * 512:(j + 1) * 512]
            dst = slot_ap(c, slot, 16, 512)
            dmas = [(dst[:, 0:8, :], src[:, 0:8, :]), (dst[:, 8:16, :], src[:, 8:16, :])]
        else:
            j = int(kind[1])
            src = dnT[e].rearrange("(kt p) n -> p kt n", p=128)[:, :, j * 1024:(j + 1) * 1024]
            dst = slot_ap(c, slot, 6, 1024)
            dmas = [(dst[:, 0:6, :], src[:, 0:6, :])]
        return ring_load(c, slot, dmas)

    kinds = ["t0", "t1", "t2", "d0", "d1"]
    chunks = [(e, k) for e in range(NE) for k in kinds]
    rdy, chunk_free = {}, {}
    ci_next = 0

    def advance():
        nonlocal ci_next
        if ci_next < len(chunks):
            ci = ci_next
            if ci >= 2:
                c.slot_free[ci % 2] = chunk_free[ci - 2]
            rdy[ci] = dma_moe(chunks[ci], ci % 2)
            ci_next += 1

    advance(); advance()
    crow_ready = {}
    crow_free = c.crow_free
    pool_all = {}
    silu_prev = {}
    bc_free = [c.aux_free[0], c.aux_free[1]]

    def load_crow(e):
        ns = (c.crow_phase + e) % 2
        if e <= 1:
            S.emit("gpsimd", lambda g, cn=crs_need: g.wait_ge(c.s_crs, cn))
        if crow_free[ns]:
            S.emit("gpsimd", lambda g, v=crow_free[ns]: g.wait_ge(c.s_pe, v))
        S.emit("gpsimd", lambda g, e=e, ns=ns: g.dma_start(out=c.crow_ap[ns], in_=crows[e:e + 1, :]).then_inc(c.s_cr[ns], 16))
        crow_ready[e] = (ns, S.inc("cr%d" % ns, 16))

    def emit_bcast(e):
        cslot = (c.crow_phase + e) % 2
        S.emit("tensor", lambda t, sv=crow_ready[e]: t.wait_ge(c.s_cr[sv[0]], sv[1]))
        psx = [c.psE, c.psF]
        for tb in range(2):
            if bc_free[tb]:
                S.emit("tensor", lambda t, v=bc_free[tb]: t.wait_ge(c.s_ev, v))
            S.emit("tensor", lambda t, cs=cslot, tb=tb: t.matmul(
                psx[tb][:, :], c.ones_f1[:, :], c.crow_ap[cs][:, tb * 512:(tb + 1) * 512],
                start=True, stop=True).then_inc(c.s_pe, 1))
            tag = S.inc("pe")
            pg = pool_all.get(e - 2, 0)
            S.emit("vector", lambda v, tag=tag, tb=tb, pg=pg, cs=cslot: (
                v.wait_ge(c.s_pe, tag),
                (v.wait_ge(c.s_pool, pg) if pg else None),
                v.tensor_copy(cb[:, cs * T + tb * 512: cs * T + tb * 512 + 512],
                              psx[tb][:, :]).then_inc(c.s_ev, 1)))
            bc_free[tb] = S.inc("ev")
            c.aux_free[tb] = bc_free[tb]
        crow_free[cslot] = S.get("pe")
        if e + 1 < NE:
            load_crow(e + 1)

    load_crow(0)
    for e in range(NE):
        if e == 0:
            emit_bcast(0)
        g_evac = {}
        pool_done = [0, 0]
        for j in range(12):
            tchunk = 5 * e + j // 4
            if j % 4 == 0:
                wait_chunk(c, "tensor", rdy[tchunk])
            w = slot_ap(c, tchunk % 2, 16, 512)
            mtj = j % 4
            b0, b1 = (0, 1) if j % 2 == 0 else (2, 3)
            pe_bank2(c, b0, b1)
            def gu_group(t, mtj=mtj, b0=b0, b1=b1, w=w):
                for kt in range(KT):
                    lhsT = w[:, kt, mtj * 128:(mtj + 1) * 128]
                    t.matmul(c.banks[b0][:, :], lhsT, hb[:, kt * T: kt * T + 512],
                             start=(kt == 0), stop=(kt == KT - 1), skip_group_check=True)
                    ins = t.matmul(c.banks[b1][:, :], lhsT, hb[:, kt * T + 512: kt * T + 1024],
                             start=(kt == 0), stop=(kt == KT - 1), skip_group_check=True)
                ins.then_inc(c.s_pe, 1)
            S.emit("tensor", gu_group)
            tag = S.inc("pe")
            if j < 6:
                for tb, b in ((0, b0), (1, b1)):
                    sg = silu_prev.get((j, tb), 0)
                    S.emit("vector", lambda v, tag=tag, j=j, tb=tb, b=b, sg=sg: (
                        v.wait_ge(c.s_pe, tag),
                        (v.wait_ge(c.s_act, sg) if sg else None),
                        v.tensor_copy(gsc[:, j * T + tb * 512: j * T + tb * 512 + 512],
                                      c.banks[b][:, :]).then_inc(c.s_ev, 1)))
                    c.bank_free[b] = S.inc("ev")
                    g_evac[(j, tb)] = c.bank_free[b]
            else:
                ju = j - 6
                for tb, b in ((0, b0), (1, b1)):
                    gt = max(g_evac[(ju, tb)], c.moe_silu_free[tb])
                    S.emit("scalar", lambda a, gt=gt, ju=ju, tb=tb: (
                        a.wait_ge(c.s_ev, gt),
                        a.activation(silu_t[:, tb * 512:(tb + 1) * 512],
                                     gsc[:, ju * T + tb * 512: ju * T + tb * 512 + 512],
                                     AF.Silu).then_inc(c.s_act, 1)))
                    tag_s = S.inc("act")
                    silu_prev[(ju, tb)] = tag_s
                    qf = c.moe_q_free[tb]
                    S.emit("vector", lambda v, tag=tag, ts=tag_s, tb=tb, b=b, qf=qf: (
                        v.wait_ge(c.s_pe, tag), v.wait_ge(c.s_act, ts),
                        (v.wait_ge(c.s_pool, qf) if qf else None),
                        v.tensor_mul(q_t[:, tb * 512:(tb + 1) * 512],
                                     silu_t[:, tb * 512:(tb + 1) * 512],
                                     c.banks[b][:, :]).then_inc(c.s_ev, 1)))
                    c.bank_free[b] = S.inc("ev")
                    tag_q = c.bank_free[b]
                    c.moe_silu_free[tb] = tag_q
                    cs = (c.crow_phase + e) % 2
                    S.emit("gpsimd", lambda g, tq=tag_q, ju=ju, tb=tb, cs=cs: (
                        g.wait_ge(c.s_ev, tq),
                        g.tensor_mul(hbuf[:, ju * T + tb * 512: ju * T + tb * 512 + 512],
                                     q_t[:, tb * 512:(tb + 1) * 512],
                                     cb[:, cs * T + tb * 512: cs * T + tb * 512 + 512]).then_inc(c.s_pool, 1)))
                    pool_done[tb] = S.inc("pool")
                    c.moe_q_free[tb] = pool_done[tb]
            if j in (3, 7, 11):
                chunk_free[5 * e + j // 4] = S.get("ev")
                advance()
        pool_all[e] = S.get("pool")
        if e + 1 < NE:
            emit_bcast(e + 1)
        for dh in range(2):
            dchunk = 5 * e + 3 + dh
            wait_chunk(c, "tensor", rdy[dchunk])
            S.emit("tensor", lambda t, v=pool_done[0]: t.wait_ge(c.s_pool, v))
            S.emit("tensor", lambda t, v=pool_done[1]: t.wait_ge(c.s_pool, v))
            w = slot_ap(c, dchunk % 2, 6, 1024)
            for mo, tag, b0, b1 in proj_groups(c, w, 6, range(8), hbuf, 0):
                evac_add_xacc(c, dh * 8 + mo, tag, b0, b1)
            chunk_free[dchunk] = S.get("ev")
            advance()
    end_ring(c, chunk_free, len(chunks))
    c.crow_phase += NE


def evac_add_xacc_mo(c, mo, tag, b0, b1):
    evac_add_xacc(c, mo, tag, b0, b1)



def emit_s3(c, dram, s3_upto=9):
    S = c.S
    h3 = arena_v(c, 0, KT * T)
    qbuf = arena_v(c, 16384, 16384)
    kstage = [arena_v(c, 32768, 1024), arena_v(c, 33792, 1024)]
    c.vstg = [arena_v(c, 34816, 512), arena_v(c, 35328, 512)]
    wqT = dram["wqT"]

    def dma_qkv(j, slot):
        src = wqT.rearrange("(kt p) n -> p kt n", p=128)[:, :, j * 512:(j + 1) * 512]
        dst = slot_ap(c, slot, 16, 512)
        return ring_load(c, slot, [(dst[:, 0:8, :], src[:, 0:8, :]), (dst[:, 8:16, :], src[:, 8:16, :])])

    order = [4, 5, 6, 7, 8, 9, 10, 11, 0, 1, 2, 3]
    rdy, chunk_free = {}, {}
    ci_next = 0
    def advance():
        nonlocal ci_next
        if ci_next < len(order):
            ci = ci_next
            if ci >= 2:
                c.slot_free[ci % 2] = chunk_free[ci - 2]
            rdy[ci] = dma_qkv(order[ci], ci % 2)
            ci_next += 1
    advance(); advance()
    kst_free = [0, 0]
    h3_ready = S.get("ev")
    k_coll_tag = v_coll_tag = 0

    for ci, j in enumerate(order):
        wait_chunk(c, "tensor", rdy[ci])
        w = slot_ap(c, ci % 2, 16, 512)
        kind = "q" if j < 4 else ("k" if j < 8 else "v")
        if kind == "v":
            vc = j - 8
            for tok_t in range(8):
                b = tok_t % 4
                pe_bank(c, b)
                def v_group(t, w=w, tok_t=tok_t, b=b, vc=vc,
                            pw=(h3_ready if ci == 0 and tok_t == 0 else 0)):
                    if pw:
                        t.wait_ge(c.s_ev, pw)
                    for kt in range(KT):
                        t.matmul(c.banks[b][:, :],
                                 h3[:, kt * T + tok_t * 128: kt * T + tok_t * 128 + 128],
                                 w[:, kt, :], start=(kt == 0), stop=False,
                                 skip_group_check=True)
                    ins = t.matmul(c.banks[b][:, :], c.ones1_bf[:, :],
                                   c.bvrow[:, vc * 512:(vc + 1) * 512],
                                   start=False, stop=True, skip_group_check=True)
                    ins.then_inc(c.s_pe, 1)
                S.emit("tensor", v_group)
                tag = S.inc("pe")
                vs = tok_t % 2
                if c.vstg_free[vs]:
                    S.emit("vector", lambda v2, vv=c.vstg_free[vs], vs=vs: v2.wait_ge(c.s_vs[vs], vv))
                S.emit("vector", lambda v2, tag=tag, b=b, vs=vs: (
                    v2.wait_ge(c.s_pe, tag),
                    v2.tensor_copy(c.vstg[vs], c.banks[b][:, :]).then_inc(c.s_ev, 1)))
                c.bank_free[b] = S.inc("ev")
                ev_t = c.bank_free[b]
                S.emit("sync", lambda s, ev=ev_t, tok_t=tok_t, vc=vc, vs=vs: (
                    s.wait_ge(c.s_ev, ev),
                    s.dma_start(out=dram["vtok"][tok_t * 128:(tok_t + 1) * 128,
                                                 vc * 512:(vc + 1) * 512],
                                in_=c.vstg[vs]).then_inc(c.s_vs[vs], 16)))
                c.vstg_free[vs] = S.inc("vs%d" % vs, 16)
        else:
            for mt, tag, b0, b1 in proj_groups(c, w, KT, range(4), h3, 0,
                                               pre_wait=(h3_ready if ci == 0 else None)):
                fo = (j % 4) * 4 + mt
                bcol = {"q": fo, "k": 16 + fo}[kind]
                if kind == "q":
                    for tb, b in ((0, b0), (1, b1)):
                        S.emit("scalar", lambda a, tag=tag, fo=fo, tb=tb, b=b, bc=bcol: (
                            a.wait_ge(c.s_pe, tag),
                            a.activation(qbuf[:, fo * T + tb * 512: fo * T + tb * 512 + 512],
                                         c.banks[b][:, :], AF.Identity,
                                         bias=c.bqkvT[:, bc:bc + 1], scale=INV_SQ_D).then_inc(c.s_act, 1)))
                        S.inc("act")
                    atv = S.get("act")
                    S.emit("vector", lambda v, at=atv: (v.wait_ge(c.s_act, at),
                                                        v.engine_nop().then_inc(c.s_ev, 1)))
                    fence = S.inc("ev")
                    c.bank_free[b0] = c.bank_free[b1] = fence
                else:
                    ksl = fo % 2
                    if kst_free[ksl]:
                        S.emit("scalar", lambda a, v=kst_free[ksl], ksl=ksl: a.wait_ge(c.s_ks[ksl], v))
                    for tb, b in ((0, b0), (1, b1)):
                        S.emit("scalar", lambda a, tag=tag, tb=tb, b=b, bc=bcol, ksl=ksl: (
                            a.wait_ge(c.s_pe, tag),
                            a.activation(kstage[ksl][:, tb * 512:(tb + 1) * 512], c.banks[b][:, :],
                                         AF.Identity, bias=c.bqkvT[:, bc:bc + 1]).then_inc(c.s_act, 1)))
                        S.inc("act")
                    atv = S.get("act")
                    S.emit("vector", lambda v, at=atv: (v.wait_ge(c.s_act, at),
                                                        v.engine_nop().then_inc(c.s_ev, 1)))
                    fence = S.inc("ev")
                    c.bank_free[b0] = c.bank_free[b1] = fence
                    S.emit("sync", lambda s, fence=fence, fo=fo, ksl=ksl: (
                        s.wait_ge(c.s_ev, fence),
                        s.dma_start(out=dram["kT"][fo * 128:(fo + 1) * 128, :],
                                    in_=kstage[ksl]).then_inc(c.s_ks[ksl], 16)))
                    kst_free[ksl] = S.inc("ks%d" % ksl, 16)
        chunk_free[ci] = S.get("ev")
        advance()
        if ci == 3:
            k0, k1 = S.get("ks0"), S.get("ks1")
            S.emit("gpsimd", lambda g, k0=k0, k1=k1: (
                g.wait_ge(c.s_ks[0], k0),
                g.wait_ge(c.s_ks[1], k1),
                g.collective_compute(
                    "AllGather", OP.bypass,
                    replica_groups=[[0, 1, 2, 3], [4, 5, 6, 7]],
                    ins=[dram["kvl"][0:1, :]],
                    outs=[dram["kvagK"][:, :]],
                ).then_inc(c.s_coll, 1)))
            k_coll_tag = S.inc("coll", 1)
        if ci == 7:
            k0, k1 = S.get("vs0"), S.get("vs1")
            S.emit("gpsimd", lambda g, k0=k0, k1=k1: (
                g.wait_ge(c.s_vs[0], k0),
                g.wait_ge(c.s_vs[1], k1),
                g.collective_compute(
                    "AllGather", OP.bypass,
                    replica_groups=[[0, 1, 2, 3], [4, 5, 6, 7]],
                    ins=[dram["kvl"][1:2, :]],
                    outs=[dram["kvagV"][:, :]],
                ).then_inc(c.s_coll, 1)))
            v_coll_tag = S.inc("coll", 1)

    end_ring(c, chunk_free, len(order))
    qkv_done = S.get("ev")
    if s3_upto <= 1:
        return

    ST = [arena_v(c, 0, 16384).rearrange("p (jt q) -> p jt q", q=512),
          arena_v(c, 32768, 16384).rearrange("p (jt q) -> p jt q", q=512)]
    rsb = [c.fscr[:, 0:512], c.fscr[:, 512:1024]]
    rows = c.fscr[0:1, :]
    srow = [rows[:, 2048:2560], rows[:, 2560:3072]]
    gmax = c.fscr[:, 3072:3073]
    mcols = [c.fscr[:, 3073:3074], c.fscr[:, 3074:3075]]
    negM = c.fscr[:, 3075:3076]
    grow = rows[:, 3104:3232]
    gm11 = rows[:, 3232:3233]
    ngm11 = rows[:, 3233:3234]
    woatT = dram["woatT"]

    S.emit("vector", lambda v: v.memset(gmax, -1e30).then_inc(c.s_ev, 1))
    gm_init = S.inc("ev")
    kf0, kf1 = S.get("ks0"), S.get("ks1")
    vf0, vf1 = S.get("vs0"), S.get("vs1")
    S.emit("vector", lambda v, a=kf0, b=kf1, e=vf0, f=vf1: (
        v.wait_ge(c.s_ks[0], a), v.wait_ge(c.s_ks[1], b),
        v.wait_ge(c.s_vs[0], e), v.wait_ge(c.s_vs[1], f),
        v.engine_nop().then_inc(c.s_ev, 1)))
    st1_guard = S.inc("ev")

    KSL = [49152, 53248]
    kslot_pe = [0, 0]
    kslot_war = [c.slot_free[1], c.slot_free[1]]
    krdy = {}

    def k_load(ch):
        sl = ch % 2
        r, tb = ch // 4, ch % 4
        if kslot_pe[sl]:
            S.emit("gpsimd", lambda g, v=kslot_pe[sl]: g.wait_ge(c.s_pe, v))
        elif kslot_war[sl]:
            S.emit("gpsimd", lambda g, v=kslot_war[sl]: g.wait_ge(c.s_ev, v))
        if ch < 2:
            S.emit("gpsimd", lambda g, cd=k_coll_tag: g.wait_ge(c.s_coll, cd))
        src = dram["kvagK"][r].rearrange("(kt p t) -> p kt t", p=128, t=T)[:, :, tb * 256:(tb + 1) * 256]
        dst = arena_v(c, KSL[sl], 4096).rearrange("p (kt n) -> p kt n", n=256)
        S.emit("gpsimd", lambda g, d=dst[:, 0:8, :], s=src[:, 0:8, :], sl=sl:
               g.dma_start(out=d, in_=s).then_inc(c.s_ld[sl], 16))
        S.emit("gpsimd", lambda g, d=dst[:, 8:16, :], s=src[:, 8:16, :], sl=sl:
               g.dma_start(out=d, in_=s).then_inc(c.s_ld[sl], 16))
        c.slot_val[sl] += 32
        krdy[ch] = (sl, c.slot_val[sl])

    k_load(0); k_load(1)
    for ch in range(16):
        wait_chunk(c, "tensor", krdy[ch])
        w = arena_v(c, KSL[ch % 2], 4096).rearrange("p (kt n) -> p kt n", n=256)
        for gi, (ksub, half) in enumerate(((0, 0), (0, 1), (1, 0), (1, 1))):
            b = gi
            pe_bank(c, b)
            def sc_group(t, w=w, ksub=ksub, half=half, b=b,
                         qd=(qkv_done if ch == 0 and gi == 0 else 0)):
                if qd:
                    t.wait_ge(c.s_ev, qd)
                for kt in range(KT):
                    ins = t.matmul(c.banks[b][:, :], w[:, kt, ksub * 128:(ksub + 1) * 128],
                                   qbuf[:, kt * T + half * 512: kt * T + half * 512 + 512],
                                   start=(kt == 0), stop=(kt == KT - 1), skip_group_check=True)
                ins.then_inc(c.s_pe, 1)
            S.emit("tensor", sc_group)
            tag = S.inc("pe")
            jt = ch * 2 + ksub
            mc = mcols[gi % 2]
            g1 = max(st1_guard, gm_init) if ch == 0 else 0
            S.emit("vector", lambda v, tag=tag, half=half, jt=jt, b=b, g1=g1, mc=mc: (
                v.wait_ge(c.s_pe, tag),
                (v.wait_ge(c.s_ev, g1) if g1 else None),
                v.tensor_copy(ST[half][:, jt, :], c.banks[b][:, :]),
                v.reduce_max(mc, c.banks[b][:, :], AX.X),
                v.drain(),
                v.tensor_max(gmax, gmax, mc).then_inc(c.s_ev, 1)))
            c.bank_free[b] = S.inc("ev")
        kslot_pe[ch % 2] = S.get("pe")
        if ch + 2 < 16:
            k_load(ch + 2)
    sc_done = S.get("ev")
    if s3_upto <= 2:
        return

    if c.aux_free[0]:
        S.emit("tensor", lambda t, v=c.aux_free[0]: t.wait_ge(c.s_ev, v))
    S.emit("tensor", lambda t, sd=sc_done: (
        t.wait_ge(c.s_ev, sd),
        t.transpose(c.psE[0:1, 0:128], gmax, c.ident_f[:, :]).then_inc(c.s_pe, 1)))
    t1 = S.inc("pe")
    S.emit("vector", lambda v, t1=t1: (
        v.wait_ge(c.s_pe, t1),
        v.tensor_copy(grow, c.psE[0:1, 0:128]),
        v.drain(),
        v.reduce_max(gm11, grow, AX.X),
        v.drain(),
        v.tensor_scalar_mul(ngm11, gm11, -1.0).then_inc(c.s_ev, 1)))
    c.aux_free[0] = S.inc("ev")
    ng_tag = c.aux_free[0]
    if c.aux_free[1]:
        S.emit("tensor", lambda t, v=c.aux_free[1]: t.wait_ge(c.s_ev, v))
    S.emit("tensor", lambda t, ng=ng_tag: (
        t.wait_ge(c.s_ev, ng),
        t.matmul(c.psF[:, 0:1], c.ones_f1[:, :], ngm11, start=True, stop=True).then_inc(c.s_pe, 1)))
    t2 = S.inc("pe")
    S.emit("vector", lambda v, t2=t2: (
        v.wait_ge(c.s_pe, t2),
        v.tensor_copy(negM, c.psF[:, 0:1]).then_inc(c.s_ev, 1)))
    c.aux_free[1] = S.inc("ev")
    negm_tag = c.aux_free[1]
    for half in range(2):
        for jt in range(32):
            S.emit("scalar", lambda a, half=half, jt=jt,
                   nm=(negm_tag if half == 0 and jt == 0 else 0): (
                (a.wait_ge(c.s_ev, nm) if nm else None),
                a.activation(ST[half][:, jt, :], ST[half][:, jt, :], AF.Exp,
                             bias=negM).then_inc(c.s_act, 1)))
            S.inc("act")
    exp_done = S.get("act")
    if c.aux_free[0]:
        S.emit("tensor", lambda t, v=c.aux_free[0]: t.wait_ge(c.s_ev, v))
    if c.aux_free[1]:
        S.emit("tensor", lambda t, v=c.aux_free[1]: t.wait_ge(c.s_ev, v))
    def sum_mm(t, ed=exp_done):
        t.wait_ge(c.s_act, ed)
        for jt in range(32):
            t.matmul(c.psE[0:1, :], c.ones_bf[:, :], ST[0][:, jt, :],
                     start=(jt == 0), stop=(jt == 31), skip_group_check=True)
            ins = t.matmul(c.psF[0:1, :], c.ones_bf[:, :], ST[1][:, jt, :],
                           start=(jt == 0), stop=(jt == 31), skip_group_check=True)
        ins.then_inc(c.s_pe, 1)
    S.emit("tensor", sum_mm)
    smt = S.inc("pe")
    S.emit("vector", lambda v, smt=smt: (
        v.wait_ge(c.s_pe, smt),
        v.tensor_copy(srow[0], c.psE[0:1, :]),
        v.tensor_copy(srow[1], c.psF[0:1, :]),
        v.drain(),
        v.tensor_scalar_max(srow[0], srow[0], 1e-30),
        v.tensor_scalar_max(srow[1], srow[1], 1e-30),
        v.drain(),
        v.reciprocal(srow[0], srow[0]),
        v.reciprocal(srow[1], srow[1]).then_inc(c.s_ev, 1)))
    rtag = S.inc("ev")
    c.aux_free[0] = c.aux_free[1] = rtag
    bcast_row(c, srow[0], rsb[0], 0, width=512, pre_ev=rtag)
    rs_done = bcast_row(c, srow[1], rsb[1], 1, width=512)
    if s3_upto <= 3:
        return

    ctxv = [arena_v(c, 16384, 8192), arena_v(c, 24576, 8192)]
    VSL = [49152, 53248]
    allb7 = [c.psA, c.psB, c.psC, c.psD, c.psE, c.psF, c.psG, c.psH]
    free7 = [c.bank_free[0], c.bank_free[1], c.bank_free[2], c.bank_free[3],
             c.aux_free[0], c.aux_free[1], c.psG_free, c.psH_free]
    pairs = [(0, ht) for ht in range(16)] + [(1, ht) for ht in range(16)]
    sets = [pairs[i:i + 8] for i in range(0, 32, 8)]
    vslot_pe = [0, 0]
    vq = ["gpsimd", "sync"]
    vrdy = {}

    def v_load(si, vci):
        sl = vci % 2
        q = vq[sl]
        if vslot_pe[sl]:
            S.emit(q, lambda g, v=vslot_pe[sl]: g.wait_ge(c.s_pe, v))
        elif kslot_pe[sl]:
            S.emit(q, lambda g, v=kslot_pe[sl]: g.wait_ge(c.s_pe, v))
        if si == 0 and vci < 2:
            S.emit(q, lambda g, cd=v_coll_tag: g.wait_ge(c.s_coll, cd))
        dst = arena_v(c, VSL[sl], 4096)
        for jbi in range(2):
            jb = 2 * vci + jbi
            r2, jl = jb // 8, jb % 8
            src = dram["kvagV"][r2].rearrange("(t d) -> t d", d=D)[jl * 128:(jl + 1) * 128, :]
            if sl == 0:
                S.emit(q, lambda g, d=dst[:, jbi * 2048:(jbi + 1) * 2048], s=src:
                       g.dma_start(out=d, in_=s).then_inc(c.s_ld[0], 16))
            else:
                S.emit(q, lambda g, d=dst[:, jbi * 2048:(jbi + 1) * 2048], s=src:
                       g.dma_start(out=d, in_=s).then_inc(c.s_xl[1], 16))
        if sl == 0:
            c.slot_val[0] += 32
            vrdy[(si, vci)] = ("ld0", c.slot_val[0])
        else:
            S.inc("xl1", 32)
            vrdy[(si, vci)] = ("xl1", S.get("xl1"))

    seq = [(si, vci) for si in range(4) for vci in range(16)]
    v_load(*seq[0]); v_load(*seq[1])
    seq_i = 2
    ctx0_done = 0
    st0_done_pe = 0
    for si, st_set in enumerate(sets):
        for ai in range(len(st_set)):
            if free7[ai]:
                S.emit("tensor", lambda t, v=free7[ai]: t.wait_ge(c.s_ev, v))
        for vci in range(16):
            sem_name, sem_val = vrdy[(si, vci)]
            if sem_name == "ld0":
                S.emit("tensor", lambda t, v=sem_val: t.wait_ge(c.s_ld[0], v))
            else:
                S.emit("tensor", lambda t, v=sem_val: t.wait_ge(c.s_xl[1], v))
            sl = vci % 2
            vsl = arena_v(c, VSL[sl], 4096)
            def ctx_group(t, vci=vci, st_set=st_set, vsl=vsl,
                          ed=(exp_done if si == 0 and vci == 0 else 0)):
                if ed:
                    t.wait_ge(c.s_act, ed)
                for jbi in range(2):
                    jb = 2 * vci + jbi
                    for ai, (h, ht) in enumerate(st_set):
                        ins = t.matmul(allb7[ai][:, :],
                                       vsl[:, jbi * 2048 + ht * 128: jbi * 2048 + ht * 128 + 128],
                                       ST[h][:, jb, :], start=(jb == 0), stop=(jb == 31),
                                       skip_group_check=True)
                ins.then_inc(c.s_pe, 1)
            S.emit("tensor", ctx_group)
            vslot_pe[sl] = S.inc("pe")
            if seq_i < len(seq):
                v_load(*seq[seq_i])
                seq_i += 1
        ctx_tag = S.get("pe")
        if si == 1:
            st0_done_pe = ctx_tag
        for ai, (h, ht) in enumerate(st_set):
            S.emit("vector", lambda v, ai=ai, h=h, ht=ht, ct=ctx_tag,
                   rd=(rs_done if si == 0 and ai == 0 else 0): (
                v.wait_ge(c.s_pe, ct),
                (v.wait_ge(c.s_ev, rd) if rd else None),
                v.tensor_mul(ctxv[h][:, ht * 512:(ht + 1) * 512], allb7[ai][:, :],
                             rsb[h]).then_inc(c.s_ev, 1)))
            free7[ai] = S.inc("ev")
        if si == 1:
            ctx0_done = S.get("ev")
    ctx1_done = S.get("ev")
    c.bank_free = free7[0:4]
    c.aux_free = free7[4:6]
    c.psG_free = free7[6]
    c.psH_free = free7[7]
    if s3_upto <= 4:
        return

    OSL = [0, 8192]
    oslot_pe = [0, 0]
    ordy = {}
    def o_load(ch4):
        sl = ch4 % 2
        if oslot_pe[sl]:
            S.emit("gpsimd", lambda g, v=oslot_pe[sl]: g.wait_ge(c.s_pe, v))
        else:
            S.emit("gpsimd", lambda g, v=st0_done_pe: g.wait_ge(c.s_pe, v))
        src = woatT.rearrange("(kt p) n -> p kt n", p=128)[:, :, ch4 * 512:(ch4 + 1) * 512]
        dst = arena_v(c, OSL[sl], 8192).rearrange("p (kt n) -> p kt n", n=512)
        S.emit("gpsimd", lambda g, d=dst[:, 0:8, :], s=src[:, 0:8, :], sl=sl:
               g.dma_start(out=d, in_=s).then_inc(c.s_ol[sl], 16))
        S.emit("gpsimd", lambda g, d=dst[:, 8:16, :], s=src[:, 8:16, :], sl=sl:
               g.dma_start(out=d, in_=s).then_inc(c.s_ol[sl], 16))
        S.inc("ol%d" % sl, 32)
        ordy[ch4] = (sl, S.get("ol%d" % sl))
    o_load(0); o_load(1)
    first_h = [ctx0_done, ctx1_done]
    for ch4 in range(4):
        osl, oval = ordy[ch4]
        S.emit("tensor", lambda t, sl=osl, v=oval: t.wait_ge(c.s_ol[sl], v))
        sl = ch4 % 2
        w = arena_v(c, OSL[sl], 8192).rearrange("p (kt n) -> p kt n", n=512)
        for half in range(2):
            for mt in range(4):
                mo = ch4 * 4 + mt
                b = (half * 4 + mt) % 4
                pe_bank(c, b)
                def op_group(t, mo=mo, mt=mt, b=b, w=w, half=half,
                             fh=first_h[half]):
                    if fh:
                        t.wait_ge(c.s_ev, fh)
                    for kt in range(KT):
                        ins = t.matmul(c.banks[b][:, :], w[:, kt, mt * 128:(mt + 1) * 128],
                                       ctxv[half][:, kt * 512:(kt + 1) * 512],
                                       start=(kt == 0), stop=(kt == KT - 1),
                                       skip_group_check=True)
                    ins.then_inc(c.s_pe, 1)
                S.emit("tensor", op_group)
                tag = S.inc("pe")
                first_h[half] = 0
                S.emit("vector", lambda v, tag=tag, mo=mo, b=b, q0=half * 512: (
                    v.wait_ge(c.s_pe, tag),
                    v.scalar_tensor_tensor(c.xacc[:, mo * T + q0: mo * T + q0 + 512],
                                           c.banks[b][:, :], c.boT[:, mo:mo + 1],
                                           c.xacc[:, mo * T + q0: mo * T + q0 + 512],
                                           OP.add, OP.add).then_inc(c.s_ev, 1)))
                c.bank_free[b] = S.inc("ev")
        oslot_pe[sl] = S.get("pe")
        if ch4 + 2 < 4:
            o_load(ch4 + 2)
    c.slot_free = [S.get("ev"), S.get("ev")]



def build(upto=4, s3_upto=9):
    from contextlib import ExitStack
    nc = bass.Bass("TRN2", target_bir_lowering=False, debug=False)
    dram = {}
    def din(name, shape, dt=F32):
        dram[name] = nc.dram_tensor(name, shape, dt, kind="ExternalInput").ap()
    din("xT", [D, T]); din("xhT", [128, 48])
    din("winT", [D, 8192]); din("convw_d", [128, 128]); din("convb_d", [128, 32])
    din("woT", [4096, D]); din("w1T", [D, DHID := 8192]); din("w2T", [8192, D])
    din("gw128", [128, KT * 32]); din("alpha", [1, 1])
    din("guT", [NE, D, 1536]); din("dnT", [NE, 768, D])
    din("wqT", [D, 6144]); din("bq128", [128, 48]); din("woatT", [D, D]); din("bo128", [128, 16])
    din("bv_row", [1, 2048])
    din("ones1", [1, 128]); din("ones128", [128, 1]); din("ident", [128, 128])
    din("cvals_d", [128, len(CONSTS)])
    dram["out"] = nc.dram_tensor("out", [D, T], F32, kind="ExternalOutput").ap()
    dram["crows"] = nc.dram_tensor("crows", [NE, T], F32, kind="Internal").ap()
    kvl = nc.dram_tensor("kvl", [2, D * T], BF16, kind="Internal").ap()
    dram["kvl"] = kvl
    dram["kvagK"] = nc.dram_tensor("kvagK", [4, D * T], BF16, kind="Internal").ap()
    dram["kvagV"] = nc.dram_tensor("kvagV", [4, D * T], BF16, kind="Internal").ap()
    dram["kT"] = kvl[0].rearrange("(f t) -> f t", t=T)
    dram["vtok"] = kvl[1].rearrange("(t d) -> t d", d=D)

    es = ExitStack()
    with es:
        c = _mk_ctx(nc, es)
        for i, v in enumerate(CONSTS):
            nc.const_aps.register(F32, v, c.cvals[:, i:i + 1])
        S = c.S
        def aux_dma(q, dst, src, cast=False):
            sm = c.s_aux if q == "sync" else c.s_auxg
            nm = "aux" if q == "sync" else "auxg"
            S.emit(q, lambda g, d=dst, s=src, sm=sm: g.dma_start(out=d, in_=s).then_inc(sm, 16))
            S.inc(nm, 16)
        xr = dram["xT"].rearrange("(kt p) t -> p kt t", p=128)
        xa = c.xacc.rearrange("p (kt t) -> p kt t", t=T)
        for i in range(4):
            aux_dma("sync", xa[:, i * 4:(i + 1) * 4, :], xr[:, i * 4:(i + 1) * 4, :])
        aux_dma("sync", c.xh[:, :], dram["xhT"])
        aux_dma("sync", c.convw[:, :], dram["convw_d"])
        aux_dma("sync", c.convb[:, :], dram["convb_d"])
        aux_dma("sync", c.bqkvT[:, :], dram["bq128"])
        aux_dma("sync", c.boT[:, :], dram["bo128"])
        aux_dma("sync", c.ones_f1[:, :], dram["ones1"])
        aux_dma("sync", c.ones_f128[:, :], dram["ones128"])
        aux_dma("sync", c.ident_f[:, :], dram["ident"])
        aux_dma("sync", c.alpha_sb[:, :], dram["alpha"])
        aux_dma("gpsimd", c.ones_bf[:, :], dram["ones128"])
        aux_dma("gpsimd", c.ones1_bf[:, :], dram["ones1"])
        aux_dma("gpsimd", c.bvrow[:, :], dram["bv_row"])
        aux_dma("gpsimd", c.gwT[:, :], dram["gw128"])
        aux_dma("sync", c.cvals[:, :], dram["cvals_d"])
        naux, nauxg = S.get("aux"), S.get("auxg")
        for eng in ("tensor", "vector", "scalar"):
            S.emit(eng, lambda t, na=naux: t.wait_ge(c.s_aux, na))
            S.emit(eng, lambda t, na=nauxg: t.wait_ge(c.s_auxg, na))
        S.emit("vector", lambda v: v.tensor_scalar_mul(c.bqkvT[:, 0:16], c.bqkvT[:, 0:16], INV_SQ_D).then_inc(c.s_ev, 1))
        S.inc("ev")
        S.emit("tensor", lambda t: t.matmul(c.psE[:, 0:1], c.ones_f1[:, :], c.alpha_sb[:, :],
                                            start=True, stop=True).then_inc(c.s_pe, 1))
        aptag = S.inc("pe")
        S.emit("vector", lambda v, ap=aptag: (
            v.wait_ge(c.s_pe, ap),
            v.tensor_copy(c.al128[:, 0:1], c.psE[:, 0:1]).then_inc(c.s_ev, 1)))
        c.aux_free[0] = S.inc("ev")

        emit_s1(c, dram)
        if upto >= 2:
            emit_ln(c, "ln_hb", dst_hb=arena_v(c, 0, KT * T))
            emit_gating(c, dram)
            emit_moe(c, dram)
        if upto >= 3:
            emit_ln(c, "ln_hb", dst_hb=arena_v(c, 0, KT * T))
            emit_s3(c, dram, s3_upto=s3_upto)
        if upto >= 4:
            emit_ln(c, "ln_hb", dst_hb=arena_v(c, 0, KT * T))
            emit_gating(c, dram)
            emit_moe(c, dram)
        done = S.get("ev")
        outr = dram["out"].rearrange("(kt p) t -> p kt t", p=128)
        S.emit("sync", lambda s, d=done: s.wait_ge(c.s_ev, d))
        for i in range(4):
            S.emit("sync", lambda s, i=i: s.dma_start(out=outr[:, i * 4:(i + 1) * 4, :],
                                                      in_=xa[:, i * 4:(i + 1) * 4, :]).then_inc(c.s_st, 16))
        S.emit("sync", lambda s: s.wait_ge(c.s_st, 64))

        with nc.Block() as block:
            @block.tensor
            def _(t):
                for f in S.ops["tensor"]: f(t)
            @block.vector
            def _(v):
                for f in S.ops["vector"]: f(v)
            @block.scalar
            def _(a):
                for f in S.ops["scalar"]: f(a)
            @block.gpsimd
            def _(g):
                for f in S.ops["gpsimd"]: f(g)
            @block.sync
            def _(s):
                for f in S.ops["sync"]: f(s)
    return nc


def _mk_ctx(nc, es):
    c = Ctx()
    c.nc = nc
    c.S = Sched()
    sb = lambda name, shape, dt: es.enter_context(nc.sbuf_tensor(name, shape, dt))
    ps = lambda name, shape, dt: es.enter_context(nc.psum_tensor(name, shape, dt))
    sem = lambda name: es.enter_context(nc.semaphore(name))
    c.xacc = sb("xacc", [128, KT * T], F32)
    c.arena = sb("arena", [128, ARENA_N], BF16)
    c.fscr = sb("fscr", [128, FSCR_N], F32)
    c.ones_bf = sb("ones_bf", [128, 1], BF16)
    c.ones_f1 = sb("ones_f1", [1, 128], F32)
    c.ones_f128 = sb("ones_f128", [128, 1], F32)
    c.bvrow = c.arena[0:1, 35840:37888]
    c.ones1_bf = c.arena[0:1, 37888:38016]
    c.ident_f = sb("ident_f", [128, 128], F32)
    c.convw = sb("convw", [128, 128], F32)
    c.convb = sb("convb", [128, 32], F32)
    c.bqkvT = sb("bqkvT", [128, 48], F32)
    c.boT = sb("boT", [128, 16], F32)
    c.gwT = sb("gwT", [128, KT * 32], BF16)
    c.alpha_sb = sb("alpha_sb", [1, 1], F32)
    c.al128 = sb("al128", [128, 1], F32)
    c.cvals = sb("cvals", [128, 10], F32)
    c.xh = sb("xh", [128, KT * 3], F32)
    c.uh = sb("uh", [128, KT * 3], BF16)
    c.m8 = sb("m8", [128, 8], F32)
    c.psA = ps("psA", [128, 512], F32)
    c.psB = ps("psB", [128, 512], F32)
    c.psC = ps("psC", [128, 512], F32)
    c.psD = ps("psD", [128, 512], F32)
    c.psE = ps("psE", [128, 512], F32)
    c.psF = ps("psF", [128, 512], F32)
    c.psG = ps("psG", [128, 512], F32)
    c.psH = ps("psH", [128, 512], F32)
    c.psG_free = 0
    c.psH_free = 0
    c.vstg_free = [0, 0]
    c.banks = [c.psA, c.psB, c.psC, c.psD]
    c.bank_free = [0, 0, 0, 0]
    c.aux_free = [0, 0]
    c.pst_free = [0, 0]
    c.s_pe = sem("s_pe"); c.s_ev = sem("s_ev"); c.s_act = sem("s_act")
    c.s_pool = sem("s_pool")
    c.s_ld = [sem("s_ld0"), sem("s_ld1")]
    c.s_aux = sem("s_aux"); c.s_auxg = sem("s_auxg")
    c.s_cr = [sem("s_cr0"), sem("s_cr1")]
    c.s_crs = sem("s_crs")
    c.s_ks = [sem("s_ks0"), sem("s_ks1")]; c.s_coll = sem("s_coll"); c.s_st = sem("s_st")
    c.s_xl = [sem("s_xl0"), sem("s_xl1")]
    c.s_ol = [sem("s_ol0"), sem("s_ol1")]
    c.s_vs = [sem("s_vs0"), sem("s_vs1")]
    c.slot_free = [0, 0]
    c.slot_val = [0, 0]
    c.crow_ap = [c.fscr[0:1, 2048:3072], c.fscr[0:1, 5120:6144]]
    c.crow_free = [0, 0]
    c.crow_phase = 0
    c.vdma_prev = None
    c.ln_tmp_free = [0, 0]
    c.conv_tap_done = [0, 0]
    c.conv_ymul_done = [0, 0]
    c.moe_silu_free = [0, 0]
    c.moe_q_free = [0, 0]
    c.ltok_tags = {}
    return c



def host_inputs(inputs, upto=4):
    x = np.asarray(inputs["L0_attn_a_x"], dtype=np.float32)
    f32 = lambda a: np.ascontiguousarray(np.asarray(a, dtype=np.float32))
    W_in = f32(inputs["L0_attn_a_W_in"])
    conv_w = f32(inputs["L0_attn_a_conv_w"])
    conv_b = f32(inputs["L0_attn_a_conv_b"])
    W_out = f32(inputs["L0_attn_a_W_out"])
    W1 = f32(inputs["L0_attn_a_W_mlp1"]); W2 = f32(inputs["L0_attn_a_W_mlp2"])
    wqkv = f32(inputs["L0_attn_b_w_qkv"]); bqkv = f32(inputs["L0_attn_b_b_qkv"])
    wo = f32(inputs["L0_attn_b_w_out"]); bo = f32(inputs["L0_attn_b_b_out"])
    gw = f32(inputs["L0_ffn_gate_weight"])
    gup = f32(inputs["L0_ffn_gate_up_proj"]); dnp = f32(inputs["L0_ffn_down_proj"])
    alpha = f32(inputs["L0_ffn_layer_alpha"]).reshape(1, 1)

    shared = {
        "winT": np.ascontiguousarray(W_in.T),
        "convw_d": np.ascontiguousarray(conv_w[:, 0, :].reshape(32, 128, 4).transpose(1, 0, 2).reshape(128, 128)),
        "convb_d": np.ascontiguousarray(conv_b.reshape(32, 128).T),
        "woT": np.ascontiguousarray(W_out.T),
        "w1T": np.ascontiguousarray(W1.T),
        "w2T": np.ascontiguousarray(W2.T),
        "gw128": np.ascontiguousarray(gw.T.reshape(16, 128, 32).transpose(1, 0, 2).reshape(128, 512)),
        "alpha": alpha,
        "guT": np.ascontiguousarray(gup.transpose(0, 2, 1)),
        "dnT": np.ascontiguousarray(dnp.transpose(0, 2, 1)),
        "wqT": np.ascontiguousarray(wqkv.T),
        "bq128": np.ascontiguousarray(bqkv.reshape(48, 128).T),
        "bv_row": np.ascontiguousarray(bqkv[4096:6144].reshape(1, 2048)),
        "woatT": np.ascontiguousarray(wo.T),
        "bo128": np.ascontiguousarray(bo.reshape(16, 128).T),
        "ones1": np.ones((1, 128), dtype=np.float32),
        "ones128": np.ones((128, 1), dtype=np.float32),
        "ident": np.eye(128, dtype=np.float32),
        "cvals_d": np.tile(np.asarray(CONSTS, dtype=np.float32)[None, :], (128, 1)),
    }
    in_maps = []
    for core in range(8):
        b, cc = core // 4, core % 4
        xs = x[b, cc * T:(cc + 1) * T, :]
        m = dict(shared)
        m["xT"] = np.ascontiguousarray(xs.T)
        if cc == 0:
            xh = np.zeros((3, 2048), dtype=np.float32)
        else:
            xh = x[b, cc * T - 3:cc * T, :]
        m["xhT"] = np.ascontiguousarray(xh.T.reshape(16, 128, 3).transpose(1, 0, 2).reshape(128, 48))
        in_maps.append(m)
    return in_maps


def kernel(**inputs):
    in_maps = host_inputs(inputs)
    nc = build(upto=4)
    res = run_bass_kernel_spmd(nc, in_maps, core_ids=list(range(8)))
    out = np.empty((2, 4096, 2048), dtype=np.float32)
    for core in range(8):
        b, cc = core // 4, core % 4
        out[b, cc * T:(cc + 1) * T, :] = res.results[core]["out"].T
    return out


# version 06
# speedup vs baseline: 1.0568x; 1.0046x over previous
import numpy as np
import ml_dtypes
import concourse.bass as bass
import concourse.mybir as mybir
from concourse.bass_utils import run_bass_kernel_spmd

F32 = mybir.dt.float32
BF16 = mybir.dt.bfloat16
AF = mybir.ActivationFunctionType
OP = mybir.AluOpType
AX = mybir.AxisListType

D = 2048
T = 1024
KT = 16
NE = 32
TAU = 0.05
EPS_LN = 1e-5
RSQ_D = float(np.sqrt(2048.0))
INV_SQ_D = float(1.0 / np.sqrt(2048.0))

CONSTS = [EPS := 1e-5, 1.0 / 2048.0, -2048.0, 1e-12, float(np.sqrt(2048.0)), -1.0,
          1e-30, 0.5, 0.05, float(1.0 / np.sqrt(2048.0))]

ARENA_N = 57344
RING_O = 40960
SLOT_N = 8192
FSCR_N = 6144


class Sched:
    ENGINES = ("tensor", "vector", "scalar", "gpsimd", "sync")

    def __init__(self):
        self.ops = {e: [] for e in self.ENGINES}
        self.cnt = {}

    def emit(self, eng, fn):
        self.ops[eng].append(fn)

    def inc(self, sem, n=1):
        self.cnt[sem] = self.cnt.get(sem, 0) + n
        return self.cnt[sem]

    def get(self, sem):
        return self.cnt.get(sem, 0)


class Ctx:
    pass


def arena_v(c, off, n):
    return c.arena[:, off:off + n]


def slot_ap(c, slot, n_in, width):
    return arena_v(c, RING_O + slot * SLOT_N, n_in * width).rearrange(
        "p (i w) -> p i w", w=width)


def ring_load(c, slot, dmas, queue="gpsimd"):
    S = c.S
    if c.slot_free[slot]:
        S.emit(queue, lambda g, v=c.slot_free[slot]: g.wait_ge(c.s_ev, v))
    for dst, src in dmas:
        S.emit(queue, lambda g, d=dst, s=src, sl=slot: g.dma_start(out=d, in_=s).then_inc(c.s_ld[sl], 16))
    c.slot_val[slot] += 16 * len(dmas)
    return (slot, c.slot_val[slot])


def end_ring(c, chunk_free, n):
    for ci in (n - 1, n - 2):
        if ci >= 0 and ci in chunk_free:
            c.slot_free[ci % 2] = max(c.slot_free[ci % 2], chunk_free[ci])


def wait_chunk(c, eng, rdy):
    c.S.emit(eng, lambda t, sl=rdy[0], v=rdy[1]: t.wait_ge(c.s_ld[sl], v))


def pe_bank(c, b):
    if c.bank_free[b]:
        c.S.emit("tensor", lambda t, v=c.bank_free[b]: t.wait_ge(c.s_ev, v))


def pe_bank2(c, b0, b1):
    w = max(c.bank_free[b0], c.bank_free[b1])
    if w:
        c.S.emit("tensor", lambda t, v=w: t.wait_ge(c.s_ev, v))



def proj_groups(c, w3, n_kt, mt_list, rhs_src, rhs_base, pre_wait=None,
                kt_waits=None):
    S = c.S
    for i, mt in enumerate(mt_list):
        b0, b1 = (0, 1) if i % 2 == 0 else (2, 3)
        pe_bank2(c, b0, b1)
        def group(t, mt=mt, b0=b0, b1=b1, w3=w3, n_kt=n_kt, rhs_src=rhs_src,
                  rhs_base=rhs_base, pw=(pre_wait if i == 0 else None),
                  kw=(kt_waits if i == 0 else None)):
            if pw:
                t.wait_ge(c.s_ev, pw)
            for kt in range(n_kt):
                if kw:
                    kind, val = kw[kt]
                    t.wait_ge(c.s_pool if kind == "pool" else c.s_ev, val)
                lhsT = w3[:, kt, mt * 128:(mt + 1) * 128]
                t.matmul(c.banks[b0][:, :], lhsT,
                         rhs_src[:, kt * T + rhs_base: kt * T + rhs_base + 512],
                         start=(kt == 0), stop=(kt == n_kt - 1), skip_group_check=True)
                ins = t.matmul(c.banks[b1][:, :], lhsT,
                         rhs_src[:, kt * T + rhs_base + 512: kt * T + rhs_base + 1024],
                         start=(kt == 0), stop=(kt == n_kt - 1), skip_group_check=True)
            ins.then_inc(c.s_pe, 1)
        S.emit("tensor", group)
        tag = S.inc("pe")
        yield mt, tag, b0, b1


def evac_add_xacc(c, mt, tag, b0, b1, bias_col=None):
    S = c.S
    for tb, b in ((0, b0), (1, b1)):
        if bias_col is None:
            S.emit("vector", lambda v, tag=tag, mt=mt, tb=tb, b=b: (
                v.wait_ge(c.s_pe, tag),
                v.tensor_add(c.xacc[:, mt * T + tb * 512: mt * T + tb * 512 + 512],
                             c.xacc[:, mt * T + tb * 512: mt * T + tb * 512 + 512],
                             c.banks[b][:, :]).then_inc(c.s_ev, 1)))
        else:
            S.emit("vector", lambda v, tag=tag, mt=mt, tb=tb, b=b, bc=bias_col: (
                v.wait_ge(c.s_pe, tag),
                v.scalar_tensor_tensor(c.xacc[:, mt * T + tb * 512: mt * T + tb * 512 + 512],
                                       c.banks[b][:, :], bc,
                                       c.xacc[:, mt * T + tb * 512: mt * T + tb * 512 + 512],
                                       OP.add, OP.add).then_inc(c.s_ev, 1)))
        c.bank_free[b] = S.inc("ev")


def emit_stats(c, want_mean, mrow, s2row):
    S = c.S
    castA = arena_v(c, 16384, 1024)
    castB = arena_v(c, 17408, 1024)
    sqA = arena_v(c, 18432, 1024)
    sqB = arena_v(c, 19456, 1024)
    cast_free = [0, 0]
    pre = S.get("ev")
    for b in range(4):
        pe_bank(c, b)
    for kt in range(KT):
        sl = kt % 2
        cbuf = castA if sl == 0 else castB
        qbuf = sqA if sl == 0 else sqB
        aw = cast_free[sl]
        S.emit("scalar", lambda a, kt=kt, cb=cbuf, aw=aw, pre=pre: (
            (a.wait_ge(c.s_ev, pre) if kt == 0 and pre else None),
            (a.wait_ge(c.s_pe, aw) if aw else None),
            a.copy(cb, c.xacc[:, kt * T:(kt + 1) * T]).then_inc(c.s_act, 1)))
        t_cast = S.inc("act")
        S.emit("vector", lambda v, cb=cbuf, sq=qbuf, tc=t_cast, aw=aw: (
            v.wait_ge(c.s_act, tc),
            (v.wait_ge(c.s_pe, aw) if aw else None),
            v.tensor_mul(sq, cb, cb).then_inc(c.s_ev, 1)))
        t_sq = S.inc("ev")
        def mmst(t, kt=kt, cb=cbuf, sq=qbuf, tc=t_cast, ts=t_sq, wm=want_mean):
            t.wait_ge(c.s_act, tc)
            t.wait_ge(c.s_ev, ts)
            first, last = kt == 0, kt == KT - 1
            if wm:
                t.matmul(c.psA[0:1, :], c.ones_bf[:, :], cb[:, 0:512], start=first, stop=last, skip_group_check=True)
                t.matmul(c.psB[0:1, :], c.ones_bf[:, :], cb[:, 512:1024], start=first, stop=last, skip_group_check=True)
            t.matmul(c.psC[0:1, :], c.ones_bf[:, :], sq[:, 0:512], start=first, stop=last, skip_group_check=True)
            ins = t.matmul(c.psD[0:1, :], c.ones_bf[:, :], sq[:, 512:1024], start=first, stop=last, skip_group_check=True)
            ins.then_inc(c.s_pe, 1)
        S.emit("tensor", mmst)
        cast_free[sl] = S.inc("pe")
    tag = S.get("pe")
    def evst(v, tag=tag, mrow=mrow, s2row=s2row, wm=want_mean):
        v.wait_ge(c.s_pe, tag)
        if wm:
            v.tensor_copy(mrow[:, 0:512], c.psA[0:1, :])
            v.tensor_copy(mrow[:, 512:1024], c.psB[0:1, :])
        v.tensor_copy(s2row[:, 0:512], c.psC[0:1, :])
        v.tensor_copy(s2row[:, 512:1024], c.psD[0:1, :]).then_inc(c.s_ev, 1)
    S.emit("vector", evst)
    done = S.inc("ev")
    for b in range(4):
        c.bank_free[b] = done
    return done


def bcast_row(c, row_ap, dst_f32_ap, aux, width=1024, pre_ev=None):
    S = c.S
    psx = c.psE if aux == 0 else c.psF
    nb = (width + 511) // 512
    for tb in range(nb):
        w = min(512, width - tb * 512)
        if c.aux_free[aux]:
            S.emit("tensor", lambda t, v=c.aux_free[aux]: t.wait_ge(c.s_ev, v))
        pv = pre_ev if tb == 0 else None
        S.emit("tensor", lambda t, tb=tb, r=row_ap, p=psx, w=w, pv=pv: (
            (t.wait_ge(c.s_ev, pv) if pv else None),
            t.matmul(p[:, 0:w], c.ones_f1[:, :], r[:, tb * 512:tb * 512 + w],
                     start=True, stop=True).then_inc(c.s_pe, 1)))
        tag = S.inc("pe")
        S.emit("vector", lambda v, tag=tag, tb=tb, p=psx, d=dst_f32_ap, w=w: (
            v.wait_ge(c.s_pe, tag),
            v.tensor_copy(d[:, tb * 512:tb * 512 + w], p[:, 0:w]).then_inc(c.s_ev, 1)))
        c.aux_free[aux] = S.inc("ev")
    return c.aux_free[aux]


def emit_ln(c, mode, dst_hb=None):
    S = c.S
    rows = c.fscr[0:1, :]
    mrow = rows[:, 2048:3072]
    s2row = rows[:, 3072:4096]
    trow = rows[:, 4096:5120]
    want_mean = mode.startswith("ln")
    st = emit_stats(c, want_mean, mrow, s2row)
    if want_mean:
        def rowm(v, st=st):
            v.wait_ge(c.s_ev, st)
            v.tensor_scalar_mul(mrow, mrow, 1.0 / 2048.0)
            v.drain()
            v.tensor_mul(trow, mrow, mrow)
            v.tensor_scalar(s2row, s2row, 1.0 / 2048.0, None, OP.mult)
            v.drain()
            v.tensor_sub(s2row, s2row, trow).then_inc(c.s_ev, 1)
        S.emit("vector", rowm)
        vtag = S.inc("ev")
        S.emit("scalar", lambda a, vt=vtag: (
            a.wait_ge(c.s_ev, vt),
            a.activation(trow, s2row, AF.Sqrt, bias=EPS_LN).then_inc(c.s_act, 1)))
        atag = S.inc("act")
        S.emit("vector", lambda v, at=atag: (
            v.wait_ge(c.s_act, at),
            v.reciprocal(s2row, trow).then_inc(c.s_ev, 1)))
        rtag = S.inc("ev")
        mb = c.fscr[:, 0:1024]
        rb = c.fscr[:, 1024:2048]
        bcast_row(c, mrow, mb, 0, pre_ev=rtag)
        done = bcast_row(c, s2row, rb, 1, pre_ev=rtag)
        for kt in range(KT):
            if mode == "ln_inplace":
                S.emit("vector", lambda v, kt=kt, d=done: (
                    (v.wait_ge(c.s_ev, d) if kt == 0 else None),
                    v.tensor_sub(c.xacc[:, kt * T:(kt + 1) * T], c.xacc[:, kt * T:(kt + 1) * T], mb).then_inc(c.s_ev, 1)))
                stag = S.inc("ev")
                S.emit("gpsimd", lambda g, kt=kt, stg=stag: (
                    g.wait_ge(c.s_ev, stg),
                    g.tensor_mul(c.xacc[:, kt * T:(kt + 1) * T], c.xacc[:, kt * T:(kt + 1) * T], rb).then_inc(c.s_pool, 1)))
                S.inc("pool")
            else:
                tmp = c.fscr[:, 4096 + (kt % 2) * 512:4096 + (kt % 2) * 512 + 512]
                tmp2 = c.fscr[:, 5120 + (kt % 2) * 512:5120 + (kt % 2) * 512 + 512]
                pw = c.ln_tmp_free[kt % 2]
                S.emit("vector", lambda v, kt=kt, d=done, tmp=tmp, tmp2=tmp2, pw=pw: (
                    (v.wait_ge(c.s_ev, d) if kt == 0 else None),
                    (v.wait_ge(c.s_pool, pw) if pw else None),
                    v.tensor_sub(tmp, c.xacc[:, kt * T:kt * T + 512], mb[:, 0:512]),
                    v.tensor_sub(tmp2, c.xacc[:, kt * T + 512:kt * T + 1024], mb[:, 512:1024]).then_inc(c.s_ev, 1)))
                stag = S.inc("ev")
                S.emit("gpsimd", lambda g, kt=kt, stg=stag, hb=dst_hb, tmp=tmp, tmp2=tmp2: (
                    g.wait_ge(c.s_ev, stg),
                    g.tensor_mul(hb[:, kt * T:kt * T + 512], tmp, rb[:, 0:512]),
                    g.tensor_mul(hb[:, kt * T + 512:kt * T + 1024], tmp2, rb[:, 512:1024]).then_inc(c.s_pool, 1)))
                c.ln_tmp_free[kt % 2] = S.inc("pool")
                c.hb_tags[kt] = ("pool", c.ln_tmp_free[kt % 2])
        pend = S.get("pool")
        S.emit("vector", lambda v, pe=pend: (v.wait_ge(c.s_pool, pe), v.engine_nop().then_inc(c.s_ev, 1)))
        S.inc("ev")
    else:
        S.emit("scalar", lambda a, st=st: (
            a.wait_ge(c.s_ev, st),
            a.activation(trow, s2row, AF.Sqrt).then_inc(c.s_act, 1)))
        atag = S.inc("act")
        def rowr(v, at=atag):
            v.wait_ge(c.s_act, at)
            v.tensor_scalar_max(trow, trow, 1e-12)
            v.drain()
            v.reciprocal(s2row, trow)
            v.drain()
            v.tensor_scalar_mul(s2row, s2row, RSQ_D).then_inc(c.s_ev, 1)
        S.emit("vector", rowr)
        rtag = S.inc("ev")
        sbb = c.fscr[:, 0:1024]
        done = bcast_row(c, s2row, sbb, 0, pre_ev=rtag)
        for kt in range(KT):
            S.emit("vector", lambda v, kt=kt, d=done, hb=dst_hb: (
                (v.wait_ge(c.s_ev, d) if kt == 0 else None),
                v.tensor_mul(hb[:, kt * T:(kt + 1) * T], c.xacc[:, kt * T:(kt + 1) * T], sbb).then_inc(c.s_ev, 1)))
            c.hb_tags[kt] = ("ev", S.inc("ev"))
    return S.get("ev")



def emit_s1(c, dram):
    S = c.S
    hb = arena_v(c, 0, KT * T)
    emit_ln(c, "ln_inplace")
    emit_ln(c, "rms_hb", dst_hb=hb)
    u_ready = S.get("ev")
    u_tags = list(c.hb_tags)

    rows = c.fscr[0:1, :]
    hm = rows[:, 5120:5123]
    hs = rows[:, 5123:5126]
    ht_ = rows[:, 5126:5129]
    hsq = c.fscr[:, 5632:5680]
    S.emit("vector", lambda v: v.tensor_mul(hsq, c.xh[:, :], c.xh[:, :]).then_inc(c.s_ev, 1))
    hq = S.inc("ev")
    if c.aux_free[0]:
        S.emit("tensor", lambda t, v=c.aux_free[0]: t.wait_ge(c.s_ev, v))
    if c.aux_free[1]:
        S.emit("tensor", lambda t, v=c.aux_free[1]: t.wait_ge(c.s_ev, v))
    def halo_stats(t, hq=hq):
        t.wait_ge(c.s_ev, hq)
        for kt in range(KT):
            t.matmul(c.psE[0:1, 0:3], c.ones_f128[:, :], c.xh[:, kt * 3:(kt + 1) * 3],
                     start=(kt == 0), stop=(kt == KT - 1), skip_group_check=True)
            ins = t.matmul(c.psF[0:1, 0:3], c.ones_f128[:, :], hsq[:, kt * 3:(kt + 1) * 3],
                     start=(kt == 0), stop=(kt == KT - 1), skip_group_check=True)
        ins.then_inc(c.s_pe, 1)
    S.emit("tensor", halo_stats)
    hstag = S.inc("pe")
    def halo_rows(v, hstag=hstag):
        v.wait_ge(c.s_pe, hstag)
        v.tensor_copy(hm, c.psE[0:1, 0:3])
        v.tensor_copy(hs, c.psF[0:1, 0:3])
        v.drain()
        v.tensor_scalar_mul(hm, hm, 1.0 / 2048.0)
        v.drain()
        v.tensor_mul(ht_, hm, hm)
        v.drain()
        v.tensor_scalar(ht_, ht_, -2048.0, None, OP.mult)
        v.drain()
        v.tensor_add(ht_, ht_, hs)
        v.drain()
        v.tensor_scalar_mul(hs, ht_, 1.0 / 2048.0).then_inc(c.s_ev, 1)
    S.emit("vector", halo_rows)
    c.aux_free[0] = c.aux_free[1] = S.inc("ev")
    v1 = c.aux_free[0]
    S.emit("scalar", lambda a, v1=v1: (
        a.wait_ge(c.s_ev, v1),
        a.activation(hs, hs, AF.Sqrt, bias=EPS_LN).then_inc(c.s_act, 1)))
    a1 = S.inc("act")
    def halo_rows2(v, a1=a1):
        v.wait_ge(c.s_act, a1)
        v.reciprocal(hs, hs)
        v.drain()
        v.tensor_mul(ht_, ht_, hs)
        v.drain()
        v.tensor_mul(ht_, ht_, hs).then_inc(c.s_ev, 1)
    S.emit("vector", halo_rows2)
    v2 = S.inc("ev")
    S.emit("scalar", lambda a, v2=v2: (
        a.wait_ge(c.s_ev, v2),
        a.activation(ht_, ht_, AF.Sqrt).then_inc(c.s_act, 1)))
    a2 = S.inc("act")
    def halo_rows3(v, a2=a2):
        v.wait_ge(c.s_act, a2)
        v.tensor_scalar_max(ht_, ht_, 1e-12)
        v.drain()
        v.reciprocal(ht_, ht_)
        v.drain()
        v.tensor_scalar_mul(ht_, ht_, RSQ_D).then_inc(c.s_ev, 1)
    S.emit("vector", halo_rows3)
    v3 = S.inc("ev")
    if c.aux_free[1]:
        S.emit("tensor", lambda t, v=c.aux_free[1]: t.wait_ge(c.s_ev, v))
    S.emit("tensor", lambda t, v3=v3: (
        t.wait_ge(c.s_ev, v3),
        t.matmul(c.psF[:, 0:9], c.ones_f1[:, :], rows[:, 5120:5129],
                 start=True, stop=True).then_inc(c.s_pe, 1)))
    hbtag = S.inc("pe")
    def halo_u(v, hbtag=hbtag):
        v.wait_ge(c.s_pe, hbtag)
        for kt in range(KT):
            v.tensor_sub(hsq[:, kt * 3:(kt + 1) * 3], c.xh[:, kt * 3:(kt + 1) * 3], c.psF[:, 0:3])
        v.drain()
        for kt in range(KT):
            v.tensor_mul(hsq[:, kt * 3:(kt + 1) * 3], hsq[:, kt * 3:(kt + 1) * 3], c.psF[:, 3:6])
        v.drain()
        for kt in range(KT):
            ins = v.tensor_mul(c.uh[:, kt * 3:(kt + 1) * 3], hsq[:, kt * 3:(kt + 1) * 3], c.psF[:, 6:9])
        ins.then_inc(c.s_ev, 1)
    S.emit("vector", halo_u)
    c.aux_free[1] = S.inc("ev")
    uh_tag = c.aux_free[1]

    winT, woT = dram["winT"], dram["woT"]
    xpc = arena_v(c, 16384, 4 * 1027)
    ych = arena_v(c, 20492, 4 * 1024)
    xc = c.fscr[:, 1024:2048]
    xc2 = c.fscr[:, 4096:5120]
    xcs = c.fscr[:, 2048:3072]
    xcs2 = c.fscr[:, 0:1024]
    zs0 = c.fscr[:, 3072:3584]
    zs1 = c.fscr[:, 3584:4096]
    zs0b = c.fscr[:, 5120:5632]
    zs1b = c.fscr[:, 5632:6144]

    def dma_chunk(kind_ch, slot):
        kind, ch = kind_ch
        if kind == "xp":
            src = winT.rearrange("(kt p) n -> p kt n", p=128)[:, :, ch * 512:(ch + 1) * 512]
            dst = slot_ap(c, slot, 16, 512)
            dmas = [(dst[:, 0:8, :], src[:, 0:8, :]), (dst[:, 8:16, :], src[:, 8:16, :])]
        elif kind == "z":
            src = winT.rearrange("(kt p) n -> p kt n", p=128)[:, :, 4096 + ch * 512:4096 + (ch + 1) * 512]
            dst = slot_ap(c, slot, 16, 512)
            dmas = [(dst[:, 0:8, :], src[:, 0:8, :]), (dst[:, 8:16, :], src[:, 8:16, :])]
        else:
            src = woT[ch * 512:(ch + 1) * 512, :].rearrange("(kt p) n -> p kt n", p=128)
            dst = slot_ap(c, slot, 4, 2048)
            dmas = [(dst[:, 0:4, :], src[:, 0:4, :])]
        return ring_load(c, slot, dmas)

    chunks = []
    for ch in range(8):
        chunks += [("xp", ch), ("z", ch), ("wo", ch)]
    rdy, chunk_free = {}, {}
    ci_next = 0

    def advance():
        nonlocal ci_next
        if ci_next < len(chunks):
            ci = ci_next
            if ci >= 2:
                c.slot_free[ci % 2] = chunk_free[ci - 2]
            rdy[ci] = dma_chunk(chunks[ci], ci % 2)
            ci_next += 1

    advance(); advance()
    for ch in range(8):
        ci_xp, ci_z, ci_wo = 3 * ch, 3 * ch + 1, 3 * ch + 2
        wait_chunk(c, "tensor", rdy[ci_xp])
        w = slot_ap(c, ci_xp % 2, 16, 512)
        for mt, tag, b0, b1 in proj_groups(c, w, KT, range(4), hb, 0,
                                           kt_waits=(u_tags if ch == 0 else None)):
            for tb, b in ((0, b0), (1, b1)):
                S.emit("vector", lambda v, tag=tag, mt=mt, tb=tb, b=b: (
                    v.wait_ge(c.s_pe, tag),
                    v.tensor_copy(xpc[:, mt * 1027 + 3 + tb * 512: mt * 1027 + 3 + tb * 512 + 512],
                                  c.banks[b][:, :]).then_inc(c.s_ev, 1)))
                c.bank_free[b] = S.inc("ev")
        if c.aux_free[1]:
            S.emit("tensor", lambda t, v=c.aux_free[1]: t.wait_ge(c.s_ev, v))
        def haloxp_mm(t, w=w, uh_tag=uh_tag):
            t.wait_ge(c.s_ev, uh_tag)
            for mt in range(4):
                for kt in range(KT):
                    ins = t.matmul(c.psF[:, 4 * mt:4 * mt + 3], w[:, kt, mt * 128:(mt + 1) * 128],
                                   c.uh[:, kt * 3:(kt + 1) * 3],
                                   start=(kt == 0), stop=(kt == KT - 1), skip_group_check=True)
            ins.then_inc(c.s_pe, 1)
        S.emit("tensor", haloxp_mm)
        htag = S.inc("pe")
        def haloxp_ev(v, htag=htag):
            v.wait_ge(c.s_pe, htag)
            for mt in range(4):
                ins = v.tensor_copy(xpc[:, mt * 1027: mt * 1027 + 3], c.psF[:, 4 * mt:4 * mt + 3])
            ins.then_inc(c.s_ev, 1)
        S.emit("vector", haloxp_ev)
        c.aux_free[1] = S.inc("ev")
        xp_done = c.aux_free[1]
        chunk_free[ci_xp] = xp_done
        advance()
        wait_chunk(c, "tensor", rdy[ci_z])
        wz = slot_ap(c, ci_z % 2, 16, 512)
        for mt, ztag, b0, b1 in proj_groups(c, wz, KT, range(4), hb, 0):
            mtg = ch * 4 + mt
            par = mt % 2
            xcp = xc if par == 0 else xc2
            S.emit("scalar", lambda a, mt=mt, mtg=mtg, xd=xp_done, xcp=xcp,
                   tg=c.conv_tap_done[par], yg=c.conv_ymul_done[par]: (
                a.wait_ge(c.s_ev, xd),
                (a.wait_ge(c.s_ev, max(tg, yg)) if max(tg, yg) else None),
                a.activation(xcp, xpc[:, mt * 1027 + 0: mt * 1027 + 1024], AF.Identity,
                             bias=c.convb[:, mtg:mtg + 1],
                             scale=c.convw[:, 4 * mtg:4 * mtg + 1]).then_inc(c.s_act, 1)))
            c0 = S.inc("act")
            def conv_taps(v, mt=mt, mtg=mtg, c0=c0, xcp=xcp):
                v.wait_ge(c.s_act, c0)
                for k in (1, 2, 3):
                    ins = v.scalar_tensor_tensor(xcp, xpc[:, mt * 1027 + k: mt * 1027 + k + 1024],
                                           c.convw[:, 4 * mtg + k:4 * mtg + k + 1], xcp,
                                           OP.mult, OP.add)
                    if k < 3:
                        v.drain()
                ins.then_inc(c.s_ev, 1)
            S.emit("vector", conv_taps)
            ctap = S.inc("ev")
            c.conv_tap_done[par] = ctap
            xcsp = xcs if par == 0 else xcs2
            zs0p = zs0 if par == 0 else zs0b
            zs1p = zs1 if par == 0 else zs1b
            def silus(a, ctap=ctap, ztag=ztag, b0=b0, b1=b1, xcp=xcp, xcsp=xcsp,
                      zs0p=zs0p, zs1p=zs1p, yg=c.conv_ymul_done[par]):
                a.wait_ge(c.s_ev, ctap)
                (a.wait_ge(c.s_ev, yg) if yg else None)
                a.activation(xcsp, xcp, AF.Silu)
                a.wait_ge(c.s_pe, ztag)
                a.activation(zs0p, c.banks[b0][:, :], AF.Silu)
                a.activation(zs1p, c.banks[b1][:, :], AF.Silu).then_inc(c.s_act, 1)
            S.emit("scalar", silus)
            stag = S.inc("act")
            def ymul(v, mt=mt, stag=stag, xcsp=xcsp, zs0p=zs0p, zs1p=zs1p):
                v.wait_ge(c.s_act, stag)
                v.tensor_mul(ych[:, mt * 1024: mt * 1024 + 512], xcsp[:, 0:512], zs0p)
                v.tensor_mul(ych[:, mt * 1024 + 512: mt * 1024 + 1024], xcsp[:, 512:1024], zs1p).then_inc(c.s_ev, 1)
            S.emit("vector", ymul)
            yt = S.inc("ev")
            c.conv_ymul_done[par] = yt
            c.bank_free[b0] = yt
            c.bank_free[b1] = yt
        y_done = S.get("ev")
        chunk_free[ci_z] = y_done
        advance()
        wait_chunk(c, "tensor", rdy[ci_wo])
        wo = slot_ap(c, ci_wo % 2, 4, 2048)
        for mo, tag, b0, b1 in proj_groups(c, wo, 4, range(KT), ych, 0, pre_wait=y_done):
            evac_add_xacc(c, mo, tag, b0, b1)
        chunk_free[ci_wo] = S.get("ev")
        advance()
    end_ring(c, chunk_free, len(chunks))

    emit_ln(c, "rms_hb", dst_hb=hb)
    h2_ready = S.get("ev")
    h2_tags = list(c.hb_tags)
    w1T, w2T = dram["w1T"], dram["w2T"]
    achunk = arena_v(c, 16384, 16384)
    def dma_mlp(kind_ch, slot):
        kind, ch = kind_ch
        if kind == "a":
            src = w1T.rearrange("(kt p) n -> p kt n", p=128)[:, :, ch * 512:(ch + 1) * 512]
            dst = slot_ap(c, slot, 16, 512)
            dmas = [(dst[:, 0:8, :], src[:, 0:8, :]), (dst[:, 8:16, :], src[:, 8:16, :])]
        else:
            src = w2T[ch * 512:(ch + 1) * 512, :].rearrange("(kt p) n -> p kt n", p=128)
            dst = slot_ap(c, slot, 4, 2048)
            dmas = [(dst[:, 0:4, :], src[:, 0:4, :])]
        return ring_load(c, slot, dmas)

    mchunks = []
    for big in range(4):
        mchunks += [("a", big * 4 + i) for i in range(4)] + [("w2", big * 4 + i) for i in range(4)]
    mrdy, mfree = {}, {}
    mi = 0
    def madv():
        nonlocal mi
        if mi < len(mchunks):
            if mi >= 2:
                c.slot_free[mi % 2] = mfree[mi - 2]
            mrdy[mi] = dma_mlp(mchunks[mi], mi % 2)
            mi += 1
    madv(); madv()
    for big in range(4):
        for sub in range(4):
            ci = big * 8 + sub
            wait_chunk(c, "tensor", mrdy[ci])
            w = slot_ap(c, ci % 2, 16, 512)
            for mt, tag, b0, b1 in proj_groups(c, w, KT, range(4), hb, 0,
                                               kt_waits=(h2_tags if ci == 0 else None)):
                at = sub * 4 + mt
                for tb, b in ((0, b0), (1, b1)):
                    S.emit("scalar", lambda a, tag=tag, at=at, tb=tb, b=b: (
                        a.wait_ge(c.s_pe, tag),
                        a.activation(achunk[:, at * 1024 + tb * 512: at * 1024 + tb * 512 + 512],
                                     c.banks[b][:, :], AF.Relu).then_inc(c.s_act, 1)))
                    S.inc("act")
                atv = S.get("act")
                S.emit("vector", lambda v, at=atv: (v.wait_ge(c.s_act, at),
                                                    v.engine_nop().then_inc(c.s_ev, 1)))
                fence = S.inc("ev")
                c.bank_free[b0] = fence
                c.bank_free[b1] = fence
            mfree[ci] = S.get("ev")
            madv()
        a_ready = S.get("ev")
        for sub in range(4):
            ci = big * 8 + 4 + sub
            wait_chunk(c, "tensor", mrdy[ci])
            w = slot_ap(c, ci % 2, 4, 2048)
            sub_rhs = arena_v(c, 16384 + sub * 4 * T, 4 * T)
            for mo, tag, b0, b1 in proj_groups(c, w, 4, range(KT), sub_rhs,
                                               0, pre_wait=(a_ready if sub == 0 else None)):
                evac_add_xacc(c, mo, tag, b0, b1)
            mfree[ci] = S.get("ev")
            madv()
    end_ring(c, mfree, len(mchunks))

    land = [c.fscr[:, 0:1024], c.fscr[:, 1024:2048]]
    xr = dram["xT"].rearrange("(kt p) t -> p kt t", p=128)
    land_free = [0, 0]
    S.emit("vector", lambda v: v.drain())
    ev_now, act_now = S.get("ev"), S.get("act")
    S.emit("sync", lambda s, e=ev_now: s.wait_ge(c.s_ev, e))
    S.emit("sync", lambda s, a=act_now: s.wait_ge(c.s_act, a))
    for kt in range(KT):
        sl = kt % 2
        if land_free[sl]:
            S.emit("sync", lambda s, v=land_free[sl]: s.wait_ge(c.s_ev, v))
        S.emit("sync", lambda s, kt=kt, sl=sl: s.dma_start(out=land[sl], in_=xr[:, kt, :]).then_inc(c.s_xl[sl], 16))
        lv = S.inc("xl%d" % sl, 16)
        S.emit("vector", lambda v, kt=kt, sl=sl, lv=lv: (
            v.wait_ge(c.s_xl[sl], lv),
            v.tensor_add(c.xacc[:, kt * T:(kt + 1) * T], c.xacc[:, kt * T:(kt + 1) * T],
                         land[sl]).then_inc(c.s_ev, 1)))
        land_free[sl] = S.inc("ev")



def emit_gating(c, dram):
    S = c.S
    hb = arena_v(c, 0, KT * T)
    logT = c.fscr[0:32, 2048:3072]
    ltok = c.fscr[:, 3072:3328]
    crT = c.fscr[0:32, 5120:6144]
    ptok = c.fscr[:, 3328:3360]
    msk = c.fscr[:, 3400:3432]
    msk2 = c.fscr[:, 3432:3464]
    cw = c.fscr[:, 3360:3368]
    ctile = c.fscr[:, 3464:3496]
    hb_ready = S.get("ev")
    if c.aux_free[0]:
        S.emit("tensor", lambda t, v=c.aux_free[0]: t.wait_ge(c.s_ev, v))
    if c.aux_free[1]:
        S.emit("tensor", lambda t, v=c.aux_free[1]: t.wait_ge(c.s_ev, v))
    def log_mm(t, kw=tuple(c.hb_tags)):
        for kt in range(KT):
            kind, val = kw[kt]
            t.wait_ge(c.s_pool if kind == "pool" else c.s_ev, val)
            t.matmul(c.psE[0:32, :], c.gwT[:, kt * 32:(kt + 1) * 32], hb[:, kt * T: kt * T + 512],
                     start=(kt == 0), stop=(kt == KT - 1), skip_group_check=True)
            ins = t.matmul(c.psF[0:32, :], c.gwT[:, kt * 32:(kt + 1) * 32], hb[:, kt * T + 512: kt * T + 1024],
                     start=(kt == 0), stop=(kt == KT - 1), skip_group_check=True)
        ins.then_inc(c.s_pe, 1)
    S.emit("tensor", log_mm)
    lt = S.inc("pe")
    S.emit("vector", lambda v, lt=lt: (
        v.wait_ge(c.s_pe, lt),
        v.tensor_copy(logT[:, 0:512], c.psE[0:32, :]),
        v.tensor_copy(logT[:, 512:1024], c.psF[0:32, :]).then_inc(c.s_ev, 1)))
    c.aux_free[0] = c.aux_free[1] = S.inc("ev")
    lev = c.aux_free[0]
    for i in range(8):
        if c.aux_free[0]:
            S.emit("tensor", lambda t, v=c.aux_free[0]: t.wait_ge(c.s_ev, v))
        S.emit("tensor", lambda t, i=i, lev=lev: (
            (t.wait_ge(c.s_ev, lev) if i == 0 else None),
            t.transpose(c.psE[:, 0:32], logT[:, i * 128:(i + 1) * 128], c.ident_f[0:32, 0:32]).then_inc(c.s_pe, 1)))
        tg = S.inc("pe")
        S.emit("vector", lambda v, i=i, tg=tg: (
            v.wait_ge(c.s_pe, tg),
            v.tensor_copy(ltok[:, i * 32:(i + 1) * 32], c.psE[:, 0:32]).then_inc(c.s_ev, 1)))
        c.aux_free[0] = S.inc("ev")
        c.ltok_tags[i] = c.aux_free[0]
    ltok_tags = dict(c.ltok_tags)
    for i in range(8):
        def gtile1(v, i=i, lt=ltok_tags[i]):
            v.wait_ge(c.s_ev, lt)
            l = ltok[:, i * 32:(i + 1) * 32]
            v.max(out=c.m8[:, :], in_=l)
            v.drain()
            v.tensor_scalar_mul(cw[:, 0:1], c.m8[:, 0:1], -1.0).then_inc(c.s_ev, 1)
        S.emit("vector", gtile1)
        t1 = S.inc("ev")
        S.emit("scalar", lambda a, i=i, t1=t1: (
            a.wait_ge(c.s_ev, t1),
            a.activation(ptok, ltok[:, i * 32:(i + 1) * 32], AF.Exp, bias=cw[:, 0:1]).then_inc(c.s_act, 1)))
        t2 = S.inc("act")
        def gtile2(v, i=i, t2=t2):
            l = ltok[:, i * 32:(i + 1) * 32]
            v.wait_ge(c.s_act, t2)
            v.tensor_scalar(msk, l, c.m8[:, 7:8], None, OP.is_ge)
            v.drain()
            v.tensor_mul(ptok, ptok, msk)
            v.drain()
            v.reduce_sum(cw[:, 1:2], ptok, AX.X)
            v.drain()
            v.tensor_scalar_max(cw[:, 2:3], cw[:, 1:2], 1e-12)
            v.drain()
            v.reciprocal(cw[:, 2:3], cw[:, 2:3])
            v.drain()
            v.tensor_mul(cw[:, 3:4], cw[:, 2:3], c.al128[:, 0:1])
            v.drain()
            v.tensor_scalar(msk, ptok, cw[:, 3:4], TAU, OP.mult, OP.is_ge)
            v.drain()
            v.reduce_sum(cw[:, 4:5], msk, AX.X)
            v.drain()
            v.tensor_scalar(cw[:, 5:6], cw[:, 4:5], 0.5, None, OP.is_lt)
            v.tensor_scalar(msk2, l, c.m8[:, 0:1], None, OP.is_ge)
            v.drain()
            v.tensor_scalar_mul(msk2, msk2, cw[:, 5:6])
            v.drain()
            v.tensor_max(msk, msk, msk2)
            v.drain()
            v.tensor_mul(ptok, ptok, msk)
            v.drain()
            v.reduce_sum(cw[:, 6:7], ptok, AX.X)
            v.drain()
            v.tensor_scalar_max(cw[:, 6:7], cw[:, 6:7], 1e-12)
            v.drain()
            v.reciprocal(cw[:, 6:7], cw[:, 6:7])
            v.drain()
            v.tensor_scalar_mul(ctile, ptok, cw[:, 6:7]).then_inc(c.s_ev, 1)
        S.emit("vector", gtile2)
        t3 = S.inc("ev")
        if c.aux_free[1]:
            S.emit("tensor", lambda t, v=c.aux_free[1]: t.wait_ge(c.s_ev, v))
        S.emit("tensor", lambda t, i=i, t3=t3: (
            t.wait_ge(c.s_ev, t3),
            t.transpose(c.psF[0:32, 0:128], ctile, c.ident_f[:, :]).then_inc(c.s_pe, 1)))
        tg = S.inc("pe")
        S.emit("vector", lambda v, i=i, tg=tg: (
            v.wait_ge(c.s_pe, tg),
            v.tensor_copy(crT[:, i * 128:(i + 1) * 128], c.psF[0:32, 0:128]).then_inc(c.s_ev, 1)))
        c.aux_free[1] = S.inc("ev")
    done = S.get("ev")
    S.emit("sync", lambda s, d=done: s.wait_ge(c.s_ev, d))
    S.emit("sync", lambda s: s.dma_start(out=dram["crows"][:, :], in_=crT).then_inc(c.s_crs, 16))
    S.inc("crs", 16)


def emit_moe(c, dram):
    S = c.S
    hb = arena_v(c, 0, KT * T)
    gsc = arena_v(c, 16384, 6144)
    hbuf = arena_v(c, 22528, 6144)
    cb = c.fscr[:, 0:2048]
    silu_t = c.fscr[:, 3072:4096]
    q_t = c.fscr[:, 4096:5120]
    guT, dnT, crows = dram["guT"], dram["dnT"], dram["crows"]
    crs_need = S.get("crs")

    def dma_moe(ek, slot):
        e, kind = ek
        if kind[0] == "t":
            j = int(kind[1])
            src = guT[e].rearrange("(kt p) n -> p kt n", p=128)[:, :, j * 512:(j + 1) * 512]
            dst = slot_ap(c, slot, 16, 512)
            dmas = [(dst[:, 0:8, :], src[:, 0:8, :]), (dst[:, 8:16, :], src[:, 8:16, :])]
        else:
            j = int(kind[1])
            src = dnT[e].rearrange("(kt p) n -> p kt n", p=128)[:, :, j * 1024:(j + 1) * 1024]
            dst = slot_ap(c, slot, 6, 1024)
            dmas = [(dst[:, 0:6, :], src[:, 0:6, :])]
        return ring_load(c, slot, dmas)

    kinds = ["t0", "t1", "t2", "d0", "d1"]
    chunks = [(e, k) for e in range(NE) for k in kinds]
    rdy, chunk_free = {}, {}
    ci_next = 0

    def advance():
        nonlocal ci_next
        if ci_next < len(chunks):
            ci = ci_next
            if ci >= 2:
                c.slot_free[ci % 2] = chunk_free[ci - 2]
            rdy[ci] = dma_moe(chunks[ci], ci % 2)
            ci_next += 1

    advance(); advance()
    crow_ready = {}
    crow_free = c.crow_free
    pool_all = {}
    silu_prev = {}
    bc_free = [c.aux_free[0], c.aux_free[1]]

    def load_crow(e):
        ns = (c.crow_phase + e) % 2
        if e <= 1:
            S.emit("gpsimd", lambda g, cn=crs_need: g.wait_ge(c.s_crs, cn))
        if crow_free[ns]:
            S.emit("gpsimd", lambda g, v=crow_free[ns]: g.wait_ge(c.s_pe, v))
        S.emit("gpsimd", lambda g, e=e, ns=ns: g.dma_start(out=c.crow_ap[ns], in_=crows[e:e + 1, :]).then_inc(c.s_cr[ns], 16))
        crow_ready[e] = (ns, S.inc("cr%d" % ns, 16))

    def emit_bcast(e):
        cslot = (c.crow_phase + e) % 2
        S.emit("tensor", lambda t, sv=crow_ready[e]: t.wait_ge(c.s_cr[sv[0]], sv[1]))
        psx = [c.psE, c.psF]
        for tb in range(2):
            if bc_free[tb]:
                S.emit("tensor", lambda t, v=bc_free[tb]: t.wait_ge(c.s_ev, v))
            S.emit("tensor", lambda t, cs=cslot, tb=tb: t.matmul(
                psx[tb][:, :], c.ones_f1[:, :], c.crow_ap[cs][:, tb * 512:(tb + 1) * 512],
                start=True, stop=True).then_inc(c.s_pe, 1))
            tag = S.inc("pe")
            pg = pool_all.get(e - 2, 0)
            S.emit("vector", lambda v, tag=tag, tb=tb, pg=pg, cs=cslot: (
                v.wait_ge(c.s_pe, tag),
                (v.wait_ge(c.s_pool, pg) if pg else None),
                v.tensor_copy(cb[:, cs * T + tb * 512: cs * T + tb * 512 + 512],
                              psx[tb][:, :]).then_inc(c.s_ev, 1)))
            bc_free[tb] = S.inc("ev")
            c.aux_free[tb] = bc_free[tb]
        crow_free[cslot] = S.get("pe")
        if e + 1 < NE:
            load_crow(e + 1)

    load_crow(0)
    for e in range(NE):
        if e == 0:
            emit_bcast(0)
        g_evac = {}
        pool_done = [0, 0]
        for j in range(12):
            tchunk = 5 * e + j // 4
            if j % 4 == 0:
                wait_chunk(c, "tensor", rdy[tchunk])
            w = slot_ap(c, tchunk % 2, 16, 512)
            mtj = j % 4
            b0, b1 = (0, 1) if j % 2 == 0 else (2, 3)
            pe_bank2(c, b0, b1)
            def gu_group(t, mtj=mtj, b0=b0, b1=b1, w=w):
                for kt in range(KT):
                    lhsT = w[:, kt, mtj * 128:(mtj + 1) * 128]
                    t.matmul(c.banks[b0][:, :], lhsT, hb[:, kt * T: kt * T + 512],
                             start=(kt == 0), stop=(kt == KT - 1), skip_group_check=True)
                    ins = t.matmul(c.banks[b1][:, :], lhsT, hb[:, kt * T + 512: kt * T + 1024],
                             start=(kt == 0), stop=(kt == KT - 1), skip_group_check=True)
                ins.then_inc(c.s_pe, 1)
            S.emit("tensor", gu_group)
            tag = S.inc("pe")
            if j < 6:
                for tb, b in ((0, b0), (1, b1)):
                    sg = silu_prev.get((j, tb), 0)
                    S.emit("vector", lambda v, tag=tag, j=j, tb=tb, b=b, sg=sg: (
                        v.wait_ge(c.s_pe, tag),
                        (v.wait_ge(c.s_act, sg) if sg else None),
                        v.tensor_copy(gsc[:, j * T + tb * 512: j * T + tb * 512 + 512],
                                      c.banks[b][:, :]).then_inc(c.s_ev, 1)))
                    c.bank_free[b] = S.inc("ev")
                    g_evac[(j, tb)] = c.bank_free[b]
            else:
                ju = j - 6
                for tb, b in ((0, b0), (1, b1)):
                    gt = max(g_evac[(ju, tb)], c.moe_silu_free[tb])
                    S.emit("scalar", lambda a, gt=gt, ju=ju, tb=tb: (
                        a.wait_ge(c.s_ev, gt),
                        a.activation(silu_t[:, tb * 512:(tb + 1) * 512],
                                     gsc[:, ju * T + tb * 512: ju * T + tb * 512 + 512],
                                     AF.Silu).then_inc(c.s_act, 1)))
                    tag_s = S.inc("act")
                    silu_prev[(ju, tb)] = tag_s
                    qf = c.moe_q_free[tb]
                    S.emit("vector", lambda v, tag=tag, ts=tag_s, tb=tb, b=b, qf=qf: (
                        v.wait_ge(c.s_pe, tag), v.wait_ge(c.s_act, ts),
                        (v.wait_ge(c.s_pool, qf) if qf else None),
                        v.tensor_mul(q_t[:, tb * 512:(tb + 1) * 512],
                                     silu_t[:, tb * 512:(tb + 1) * 512],
                                     c.banks[b][:, :]).then_inc(c.s_ev, 1)))
                    c.bank_free[b] = S.inc("ev")
                    tag_q = c.bank_free[b]
                    c.moe_silu_free[tb] = tag_q
                    cs = (c.crow_phase + e) % 2
                    S.emit("gpsimd", lambda g, tq=tag_q, ju=ju, tb=tb, cs=cs: (
                        g.wait_ge(c.s_ev, tq),
                        g.tensor_mul(hbuf[:, ju * T + tb * 512: ju * T + tb * 512 + 512],
                                     q_t[:, tb * 512:(tb + 1) * 512],
                                     cb[:, cs * T + tb * 512: cs * T + tb * 512 + 512]).then_inc(c.s_pool, 1)))
                    pool_done[tb] = S.inc("pool")
                    c.moe_q_free[tb] = pool_done[tb]
            if j in (3, 7, 11):
                chunk_free[5 * e + j // 4] = S.get("ev")
                advance()
        pool_all[e] = S.get("pool")
        if e + 1 < NE:
            emit_bcast(e + 1)
        for dh in range(2):
            dchunk = 5 * e + 3 + dh
            wait_chunk(c, "tensor", rdy[dchunk])
            S.emit("tensor", lambda t, v=pool_done[0]: t.wait_ge(c.s_pool, v))
            S.emit("tensor", lambda t, v=pool_done[1]: t.wait_ge(c.s_pool, v))
            w = slot_ap(c, dchunk % 2, 6, 1024)
            for mo, tag, b0, b1 in proj_groups(c, w, 6, range(8), hbuf, 0):
                evac_add_xacc(c, dh * 8 + mo, tag, b0, b1)
            chunk_free[dchunk] = S.get("ev")
            advance()
    end_ring(c, chunk_free, len(chunks))
    c.crow_phase += NE


def evac_add_xacc_mo(c, mo, tag, b0, b1):
    evac_add_xacc(c, mo, tag, b0, b1)



def emit_s3(c, dram, s3_upto=9):
    S = c.S
    h3 = arena_v(c, 0, KT * T)
    qbuf = arena_v(c, 16384, 16384)
    kstage = [arena_v(c, 32768, 1024), arena_v(c, 33792, 1024)]
    c.vstg = [arena_v(c, 34816, 512), arena_v(c, 35328, 512)]
    wqT = dram["wqT"]

    def dma_qkv(j, slot):
        src = wqT.rearrange("(kt p) n -> p kt n", p=128)[:, :, j * 512:(j + 1) * 512]
        dst = slot_ap(c, slot, 16, 512)
        return ring_load(c, slot, [(dst[:, 0:8, :], src[:, 0:8, :]), (dst[:, 8:16, :], src[:, 8:16, :])])

    order = [4, 5, 6, 7, 8, 9, 10, 11, 0, 1, 2, 3]
    rdy, chunk_free = {}, {}
    ci_next = 0
    def advance():
        nonlocal ci_next
        if ci_next < len(order):
            ci = ci_next
            if ci >= 2:
                c.slot_free[ci % 2] = chunk_free[ci - 2]
            rdy[ci] = dma_qkv(order[ci], ci % 2)
            ci_next += 1
    advance(); advance()
    kst_free = [0, 0]
    h3_ready = S.get("ev")
    h3_tags = list(c.hb_tags)
    k_coll_tag = v_coll_tag = 0

    for ci, j in enumerate(order):
        wait_chunk(c, "tensor", rdy[ci])
        w = slot_ap(c, ci % 2, 16, 512)
        kind = "q" if j < 4 else ("k" if j < 8 else "v")
        if kind == "v":
            vc = j - 8
            for tok_t in range(8):
                b = tok_t % 4
                pe_bank(c, b)
                def v_group(t, w=w, tok_t=tok_t, b=b, vc=vc,
                            pw=(h3_ready if ci == 0 and tok_t == 0 else 0)):
                    if pw:
                        t.wait_ge(c.s_ev, pw)
                    for kt in range(KT):
                        t.matmul(c.banks[b][:, :],
                                 h3[:, kt * T + tok_t * 128: kt * T + tok_t * 128 + 128],
                                 w[:, kt, :], start=(kt == 0), stop=False,
                                 skip_group_check=True)
                    ins = t.matmul(c.banks[b][:, :], c.ones1_bf[:, :],
                                   c.bvrow[:, vc * 512:(vc + 1) * 512],
                                   start=False, stop=True, skip_group_check=True)
                    ins.then_inc(c.s_pe, 1)
                S.emit("tensor", v_group)
                tag = S.inc("pe")
                vs = tok_t % 2
                if c.vstg_free[vs]:
                    S.emit("vector", lambda v2, vv=c.vstg_free[vs], vs=vs: v2.wait_ge(c.s_vs[vs], vv))
                S.emit("vector", lambda v2, tag=tag, b=b, vs=vs: (
                    v2.wait_ge(c.s_pe, tag),
                    v2.tensor_copy(c.vstg[vs], c.banks[b][:, :]).then_inc(c.s_ev, 1)))
                c.bank_free[b] = S.inc("ev")
                ev_t = c.bank_free[b]
                S.emit("sync", lambda s, ev=ev_t, tok_t=tok_t, vc=vc, vs=vs: (
                    s.wait_ge(c.s_ev, ev),
                    s.dma_start(out=dram["vtok"][tok_t * 128:(tok_t + 1) * 128,
                                                 vc * 512:(vc + 1) * 512],
                                in_=c.vstg[vs]).then_inc(c.s_vs[vs], 16)))
                c.vstg_free[vs] = S.inc("vs%d" % vs, 16)
        else:
            for mt, tag, b0, b1 in proj_groups(c, w, KT, range(4), h3, 0,
                                               kt_waits=(h3_tags if ci == 0 else None)):
                fo = (j % 4) * 4 + mt
                bcol = {"q": fo, "k": 16 + fo}[kind]
                if kind == "q":
                    for tb, b in ((0, b0), (1, b1)):
                        S.emit("scalar", lambda a, tag=tag, fo=fo, tb=tb, b=b, bc=bcol: (
                            a.wait_ge(c.s_pe, tag),
                            a.activation(qbuf[:, fo * T + tb * 512: fo * T + tb * 512 + 512],
                                         c.banks[b][:, :], AF.Identity,
                                         bias=c.bqkvT[:, bc:bc + 1], scale=INV_SQ_D).then_inc(c.s_act, 1)))
                        S.inc("act")
                    atv = S.get("act")
                    S.emit("vector", lambda v, at=atv: (v.wait_ge(c.s_act, at),
                                                        v.engine_nop().then_inc(c.s_ev, 1)))
                    fence = S.inc("ev")
                    c.bank_free[b0] = c.bank_free[b1] = fence
                else:
                    ksl = fo % 2
                    if kst_free[ksl]:
                        S.emit("scalar", lambda a, v=kst_free[ksl], ksl=ksl: a.wait_ge(c.s_ks[ksl], v))
                    for tb, b in ((0, b0), (1, b1)):
                        S.emit("scalar", lambda a, tag=tag, tb=tb, b=b, bc=bcol, ksl=ksl: (
                            a.wait_ge(c.s_pe, tag),
                            a.activation(kstage[ksl][:, tb * 512:(tb + 1) * 512], c.banks[b][:, :],
                                         AF.Identity, bias=c.bqkvT[:, bc:bc + 1]).then_inc(c.s_act, 1)))
                        S.inc("act")
                    atv = S.get("act")
                    S.emit("vector", lambda v, at=atv: (v.wait_ge(c.s_act, at),
                                                        v.engine_nop().then_inc(c.s_ev, 1)))
                    fence = S.inc("ev")
                    c.bank_free[b0] = c.bank_free[b1] = fence
                    S.emit("sync", lambda s, fence=fence, fo=fo, ksl=ksl: (
                        s.wait_ge(c.s_ev, fence),
                        s.dma_start(out=dram["kT"][fo * 128:(fo + 1) * 128, :],
                                    in_=kstage[ksl]).then_inc(c.s_ks[ksl], 16)))
                    kst_free[ksl] = S.inc("ks%d" % ksl, 16)
        chunk_free[ci] = S.get("ev")
        advance()
        if ci == 3:
            k0, k1 = S.get("ks0"), S.get("ks1")
            S.emit("gpsimd", lambda g, k0=k0, k1=k1: (
                g.wait_ge(c.s_ks[0], k0),
                g.wait_ge(c.s_ks[1], k1),
                g.collective_compute(
                    "AllGather", OP.bypass,
                    replica_groups=[[0, 1, 2, 3], [4, 5, 6, 7]],
                    ins=[dram["kvl"][0:1, :]],
                    outs=[dram["kvagK"][:, :]],
                ).then_inc(c.s_coll, 1)))
            k_coll_tag = S.inc("coll", 1)
        if ci == 7:
            k0, k1 = S.get("vs0"), S.get("vs1")
            S.emit("gpsimd", lambda g, k0=k0, k1=k1: (
                g.wait_ge(c.s_vs[0], k0),
                g.wait_ge(c.s_vs[1], k1),
                g.collective_compute(
                    "AllGather", OP.bypass,
                    replica_groups=[[0, 1, 2, 3], [4, 5, 6, 7]],
                    ins=[dram["kvl"][1:2, :]],
                    outs=[dram["kvagV"][:, :]],
                ).then_inc(c.s_coll, 1)))
            v_coll_tag = S.inc("coll", 1)

    end_ring(c, chunk_free, len(order))
    qkv_done = S.get("ev")
    if s3_upto <= 1:
        return

    ST = [arena_v(c, 0, 16384).rearrange("p (jt q) -> p jt q", q=512),
          arena_v(c, 32768, 16384).rearrange("p (jt q) -> p jt q", q=512)]
    rsb = [c.fscr[:, 0:512], c.fscr[:, 512:1024]]
    rows = c.fscr[0:1, :]
    srow = [rows[:, 2048:2560], rows[:, 2560:3072]]
    gmax = c.fscr[:, 3072:3073]
    mcols = [c.fscr[:, 3073:3074], c.fscr[:, 3074:3075]]
    negM = c.fscr[:, 3075:3076]
    grow = rows[:, 3104:3232]
    gm11 = rows[:, 3232:3233]
    ngm11 = rows[:, 3233:3234]
    woatT = dram["woatT"]

    S.emit("vector", lambda v: v.memset(gmax, -1e30).then_inc(c.s_ev, 1))
    gm_init = S.inc("ev")
    kf0, kf1 = S.get("ks0"), S.get("ks1")
    vf0, vf1 = S.get("vs0"), S.get("vs1")
    S.emit("vector", lambda v, a=kf0, b=kf1, e=vf0, f=vf1: (
        v.wait_ge(c.s_ks[0], a), v.wait_ge(c.s_ks[1], b),
        v.wait_ge(c.s_vs[0], e), v.wait_ge(c.s_vs[1], f),
        v.engine_nop().then_inc(c.s_ev, 1)))
    st1_guard = S.inc("ev")

    KSL = [49152, 53248]
    kslot_pe = [0, 0]
    kslot_war = [c.slot_free[1], c.slot_free[1]]
    krdy = {}

    def k_load(ch):
        sl = ch % 2
        q = "gpsimd" if sl == 0 else "sync"
        if kslot_pe[sl]:
            S.emit(q, lambda g, v=kslot_pe[sl]: g.wait_ge(c.s_pe, v))
        elif kslot_war[sl]:
            S.emit(q, lambda g, v=kslot_war[sl]: g.wait_ge(c.s_ev, v))
        if ch < 2:
            S.emit(q, lambda g, cd=k_coll_tag: g.wait_ge(c.s_coll, cd))
        src = dram["kvagK"][ch // 4].rearrange("(kt p t) -> p kt t", p=128, t=T)[:, :, (ch % 4) * 256:((ch % 4) + 1) * 256]
        dst = arena_v(c, KSL[sl], 4096).rearrange("p (kt n) -> p kt n", n=256)
        if sl == 0:
            S.emit(q, lambda g, d=dst[:, 0:8, :], s=src[:, 0:8, :]:
                   g.dma_start(out=d, in_=s).then_inc(c.s_ld[0], 16))
            S.emit(q, lambda g, d=dst[:, 8:16, :], s=src[:, 8:16, :]:
                   g.dma_start(out=d, in_=s).then_inc(c.s_ld[0], 16))
            c.slot_val[0] += 32
            krdy[ch] = ("ld0", c.slot_val[0])
        else:
            S.emit(q, lambda g, d=dst[:, 0:8, :], s=src[:, 0:8, :]:
                   g.dma_start(out=d, in_=s).then_inc(c.s_xl[1], 16))
            S.emit(q, lambda g, d=dst[:, 8:16, :], s=src[:, 8:16, :]:
                   g.dma_start(out=d, in_=s).then_inc(c.s_xl[1], 16))
            S.inc("xl1", 32)
            krdy[ch] = ("xl1", S.get("xl1"))

    k_load(0); k_load(1)
    for ch in range(16):
        kname, kval = krdy[ch]
        if kname == "ld0":
            S.emit("tensor", lambda t, v=kval: t.wait_ge(c.s_ld[0], v))
        else:
            S.emit("tensor", lambda t, v=kval: t.wait_ge(c.s_xl[1], v))
        w = arena_v(c, KSL[ch % 2], 4096).rearrange("p (kt n) -> p kt n", n=256)
        for gi, (ksub, half) in enumerate(((0, 0), (0, 1), (1, 0), (1, 1))):
            b = gi
            pe_bank(c, b)
            def sc_group(t, w=w, ksub=ksub, half=half, b=b,
                         qd=(qkv_done if ch == 0 and gi == 0 else 0)):
                if qd:
                    t.wait_ge(c.s_ev, qd)
                for kt in range(KT):
                    ins = t.matmul(c.banks[b][:, :], w[:, kt, ksub * 128:(ksub + 1) * 128],
                                   qbuf[:, kt * T + half * 512: kt * T + half * 512 + 512],
                                   start=(kt == 0), stop=(kt == KT - 1), skip_group_check=True)
                ins.then_inc(c.s_pe, 1)
            S.emit("tensor", sc_group)
            tag = S.inc("pe")
            jt = ch * 2 + ksub
            mc = mcols[gi % 2]
            g1 = max(st1_guard, gm_init) if ch == 0 else 0
            S.emit("vector", lambda v, tag=tag, half=half, jt=jt, b=b, g1=g1, mc=mc: (
                v.wait_ge(c.s_pe, tag),
                (v.wait_ge(c.s_ev, g1) if g1 else None),
                v.tensor_copy(ST[half][:, jt, :], c.banks[b][:, :]),
                v.reduce_max(mc, c.banks[b][:, :], AX.X),
                v.drain(),
                v.tensor_max(gmax, gmax, mc).then_inc(c.s_ev, 1)))
            c.bank_free[b] = S.inc("ev")
        kslot_pe[ch % 2] = S.get("pe")
        if ch + 2 < 16:
            k_load(ch + 2)
    sc_done = S.get("ev")
    if s3_upto <= 2:
        return

    if c.aux_free[0]:
        S.emit("tensor", lambda t, v=c.aux_free[0]: t.wait_ge(c.s_ev, v))
    S.emit("tensor", lambda t, sd=sc_done: (
        t.wait_ge(c.s_ev, sd),
        t.transpose(c.psE[0:1, 0:128], gmax, c.ident_f[:, :]).then_inc(c.s_pe, 1)))
    t1 = S.inc("pe")
    S.emit("vector", lambda v, t1=t1: (
        v.wait_ge(c.s_pe, t1),
        v.tensor_copy(grow, c.psE[0:1, 0:128]),
        v.drain(),
        v.reduce_max(gm11, grow, AX.X),
        v.drain(),
        v.tensor_scalar_mul(ngm11, gm11, -1.0).then_inc(c.s_ev, 1)))
    c.aux_free[0] = S.inc("ev")
    ng_tag = c.aux_free[0]
    if c.aux_free[1]:
        S.emit("tensor", lambda t, v=c.aux_free[1]: t.wait_ge(c.s_ev, v))
    S.emit("tensor", lambda t, ng=ng_tag: (
        t.wait_ge(c.s_ev, ng),
        t.matmul(c.psF[:, 0:1], c.ones_f1[:, :], ngm11, start=True, stop=True).then_inc(c.s_pe, 1)))
    t2 = S.inc("pe")
    S.emit("vector", lambda v, t2=t2: (
        v.wait_ge(c.s_pe, t2),
        v.tensor_copy(negM, c.psF[:, 0:1]).then_inc(c.s_ev, 1)))
    c.aux_free[1] = S.inc("ev")
    negm_tag = c.aux_free[1]
    for half in range(2):
        for jt in range(32):
            S.emit("scalar", lambda a, half=half, jt=jt,
                   nm=(negm_tag if half == 0 and jt == 0 else 0): (
                (a.wait_ge(c.s_ev, nm) if nm else None),
                a.activation(ST[half][:, jt, :], ST[half][:, jt, :], AF.Exp,
                             bias=negM).then_inc(c.s_act, 1)))
            S.inc("act")
    exp_done = S.get("act")
    if c.aux_free[0]:
        S.emit("tensor", lambda t, v=c.aux_free[0]: t.wait_ge(c.s_ev, v))
    if c.aux_free[1]:
        S.emit("tensor", lambda t, v=c.aux_free[1]: t.wait_ge(c.s_ev, v))
    def sum_mm(t, ed=exp_done):
        t.wait_ge(c.s_act, ed)
        for jt in range(32):
            t.matmul(c.psE[0:1, :], c.ones_bf[:, :], ST[0][:, jt, :],
                     start=(jt == 0), stop=(jt == 31), skip_group_check=True)
            ins = t.matmul(c.psF[0:1, :], c.ones_bf[:, :], ST[1][:, jt, :],
                           start=(jt == 0), stop=(jt == 31), skip_group_check=True)
        ins.then_inc(c.s_pe, 1)
    S.emit("tensor", sum_mm)
    smt = S.inc("pe")
    S.emit("vector", lambda v, smt=smt: (
        v.wait_ge(c.s_pe, smt),
        v.tensor_copy(srow[0], c.psE[0:1, :]),
        v.tensor_copy(srow[1], c.psF[0:1, :]),
        v.drain(),
        v.tensor_scalar_max(srow[0], srow[0], 1e-30),
        v.tensor_scalar_max(srow[1], srow[1], 1e-30),
        v.drain(),
        v.reciprocal(srow[0], srow[0]),
        v.reciprocal(srow[1], srow[1]).then_inc(c.s_ev, 1)))
    rtag = S.inc("ev")
    c.aux_free[0] = c.aux_free[1] = rtag
    bcast_row(c, srow[0], rsb[0], 0, width=512, pre_ev=rtag)
    rs_done = bcast_row(c, srow[1], rsb[1], 1, width=512)
    if s3_upto <= 3:
        return

    ctxv = [arena_v(c, 16384, 8192), arena_v(c, 24576, 8192)]
    VSL = [49152, 53248]
    allb7 = [c.psA, c.psB, c.psC, c.psD, c.psE, c.psF, c.psG, c.psH]
    free7 = [c.bank_free[0], c.bank_free[1], c.bank_free[2], c.bank_free[3],
             c.aux_free[0], c.aux_free[1], c.psG_free, c.psH_free]
    pairs = [(0, ht) for ht in range(16)] + [(1, ht) for ht in range(16)]
    sets = [pairs[i:i + 8] for i in range(0, 32, 8)]
    vslot_pe = [0, 0]
    vq = ["gpsimd", "sync"]
    vrdy = {}

    def v_load(si, vci):
        sl = vci % 2
        q = vq[sl]
        if vslot_pe[sl]:
            S.emit(q, lambda g, v=vslot_pe[sl]: g.wait_ge(c.s_pe, v))
        elif kslot_pe[sl]:
            S.emit(q, lambda g, v=kslot_pe[sl]: g.wait_ge(c.s_pe, v))
        if si == 0 and vci < 2:
            S.emit(q, lambda g, cd=v_coll_tag: g.wait_ge(c.s_coll, cd))
        dst = arena_v(c, VSL[sl], 4096)
        for jbi in range(2):
            jb = 2 * vci + jbi
            r2, jl = jb // 8, jb % 8
            src = dram["kvagV"][r2].rearrange("(t d) -> t d", d=D)[jl * 128:(jl + 1) * 128, :]
            if sl == 0:
                S.emit(q, lambda g, d=dst[:, jbi * 2048:(jbi + 1) * 2048], s=src:
                       g.dma_start(out=d, in_=s).then_inc(c.s_ld[0], 16))
            else:
                S.emit(q, lambda g, d=dst[:, jbi * 2048:(jbi + 1) * 2048], s=src:
                       g.dma_start(out=d, in_=s).then_inc(c.s_xl[1], 16))
        if sl == 0:
            c.slot_val[0] += 32
            vrdy[(si, vci)] = ("ld0", c.slot_val[0])
        else:
            S.inc("xl1", 32)
            vrdy[(si, vci)] = ("xl1", S.get("xl1"))

    seq = [(si, vci) for si in range(4) for vci in range(16)]
    v_load(*seq[0]); v_load(*seq[1])
    seq_i = 2
    ctx0_done = 0
    st0_done_pe = 0
    for si, st_set in enumerate(sets):
        for ai in range(len(st_set)):
            if free7[ai]:
                S.emit("tensor", lambda t, v=free7[ai]: t.wait_ge(c.s_ev, v))
        for vci in range(16):
            sem_name, sem_val = vrdy[(si, vci)]
            if sem_name == "ld0":
                S.emit("tensor", lambda t, v=sem_val: t.wait_ge(c.s_ld[0], v))
            else:
                S.emit("tensor", lambda t, v=sem_val: t.wait_ge(c.s_xl[1], v))
            sl = vci % 2
            vsl = arena_v(c, VSL[sl], 4096)
            def ctx_group(t, vci=vci, st_set=st_set, vsl=vsl,
                          ed=(exp_done if si == 0 and vci == 0 else 0)):
                if ed:
                    t.wait_ge(c.s_act, ed)
                for jbi in range(2):
                    jb = 2 * vci + jbi
                    for ai, (h, ht) in enumerate(st_set):
                        ins = t.matmul(allb7[ai][:, :],
                                       vsl[:, jbi * 2048 + ht * 128: jbi * 2048 + ht * 128 + 128],
                                       ST[h][:, jb, :], start=(jb == 0), stop=(jb == 31),
                                       skip_group_check=True)
                ins.then_inc(c.s_pe, 1)
            S.emit("tensor", ctx_group)
            vslot_pe[sl] = S.inc("pe")
            if seq_i < len(seq):
                v_load(*seq[seq_i])
                seq_i += 1
        ctx_tag = S.get("pe")
        if si == 1:
            st0_done_pe = ctx_tag
        for ai, (h, ht) in enumerate(st_set):
            S.emit("vector", lambda v, ai=ai, h=h, ht=ht, ct=ctx_tag,
                   rd=(rs_done if si == 0 and ai == 0 else 0): (
                v.wait_ge(c.s_pe, ct),
                (v.wait_ge(c.s_ev, rd) if rd else None),
                v.tensor_mul(ctxv[h][:, ht * 512:(ht + 1) * 512], allb7[ai][:, :],
                             rsb[h]).then_inc(c.s_ev, 1)))
            free7[ai] = S.inc("ev")
        if si == 1:
            ctx0_done = S.get("ev")
    ctx1_done = S.get("ev")
    c.bank_free = free7[0:4]
    c.aux_free = free7[4:6]
    c.psG_free = free7[6]
    c.psH_free = free7[7]
    if s3_upto <= 4:
        return

    OSL = [0, 8192]
    oslot_pe = [0, 0]
    ordy = {}
    def o_load(ch4):
        sl = ch4 % 2
        if oslot_pe[sl]:
            S.emit("gpsimd", lambda g, v=oslot_pe[sl]: g.wait_ge(c.s_pe, v))
        else:
            S.emit("gpsimd", lambda g, v=st0_done_pe: g.wait_ge(c.s_pe, v))
        src = woatT.rearrange("(kt p) n -> p kt n", p=128)[:, :, ch4 * 512:(ch4 + 1) * 512]
        dst = arena_v(c, OSL[sl], 8192).rearrange("p (kt n) -> p kt n", n=512)
        S.emit("gpsimd", lambda g, d=dst[:, 0:8, :], s=src[:, 0:8, :], sl=sl:
               g.dma_start(out=d, in_=s).then_inc(c.s_ol[sl], 16))
        S.emit("gpsimd", lambda g, d=dst[:, 8:16, :], s=src[:, 8:16, :], sl=sl:
               g.dma_start(out=d, in_=s).then_inc(c.s_ol[sl], 16))
        S.inc("ol%d" % sl, 32)
        ordy[ch4] = (sl, S.get("ol%d" % sl))
    o_load(0); o_load(1)
    first_h = [ctx0_done, ctx1_done]
    for ch4 in range(4):
        osl, oval = ordy[ch4]
        S.emit("tensor", lambda t, sl=osl, v=oval: t.wait_ge(c.s_ol[sl], v))
        sl = ch4 % 2
        w = arena_v(c, OSL[sl], 8192).rearrange("p (kt n) -> p kt n", n=512)
        for half in range(2):
            for mt in range(4):
                mo = ch4 * 4 + mt
                b = (half * 4 + mt) % 4
                pe_bank(c, b)
                def op_group(t, mo=mo, mt=mt, b=b, w=w, half=half,
                             fh=first_h[half]):
                    if fh:
                        t.wait_ge(c.s_ev, fh)
                    for kt in range(KT):
                        ins = t.matmul(c.banks[b][:, :], w[:, kt, mt * 128:(mt + 1) * 128],
                                       ctxv[half][:, kt * 512:(kt + 1) * 512],
                                       start=(kt == 0), stop=(kt == KT - 1),
                                       skip_group_check=True)
                    ins.then_inc(c.s_pe, 1)
                S.emit("tensor", op_group)
                tag = S.inc("pe")
                first_h[half] = 0
                S.emit("vector", lambda v, tag=tag, mo=mo, b=b, q0=half * 512: (
                    v.wait_ge(c.s_pe, tag),
                    v.scalar_tensor_tensor(c.xacc[:, mo * T + q0: mo * T + q0 + 512],
                                           c.banks[b][:, :], c.boT[:, mo:mo + 1],
                                           c.xacc[:, mo * T + q0: mo * T + q0 + 512],
                                           OP.add, OP.add).then_inc(c.s_ev, 1)))
                c.bank_free[b] = S.inc("ev")
        oslot_pe[sl] = S.get("pe")
        if ch4 + 2 < 4:
            o_load(ch4 + 2)
    c.slot_free = [S.get("ev"), S.get("ev")]



def build(upto=4, s3_upto=9):
    from contextlib import ExitStack
    nc = bass.Bass("TRN2", target_bir_lowering=False, debug=False)
    dram = {}
    def din(name, shape, dt=F32):
        dram[name] = nc.dram_tensor(name, shape, dt, kind="ExternalInput").ap()
    din("xT", [D, T]); din("xhT", [128, 48])
    din("winT", [D, 8192]); din("convw_d", [128, 128]); din("convb_d", [128, 32])
    din("woT", [4096, D]); din("w1T", [D, DHID := 8192]); din("w2T", [8192, D])
    din("gw128", [128, KT * 32]); din("alpha", [1, 1])
    din("guT", [NE, D, 1536]); din("dnT", [NE, 768, D])
    din("wqT", [D, 6144]); din("bq128", [128, 48]); din("woatT", [D, D]); din("bo128", [128, 16])
    din("bv_row", [1, 2048])
    din("ones1", [1, 128]); din("ones128", [128, 1]); din("ident", [128, 128])
    din("cvals_d", [128, len(CONSTS)])
    dram["out"] = nc.dram_tensor("out", [D, T], F32, kind="ExternalOutput").ap()
    dram["crows"] = nc.dram_tensor("crows", [NE, T], F32, kind="Internal").ap()
    kvl = nc.dram_tensor("kvl", [2, D * T], BF16, kind="Internal").ap()
    dram["kvl"] = kvl
    dram["kvagK"] = nc.dram_tensor("kvagK", [4, D * T], BF16, kind="Internal").ap()
    dram["kvagV"] = nc.dram_tensor("kvagV", [4, D * T], BF16, kind="Internal").ap()
    dram["kT"] = kvl[0].rearrange("(f t) -> f t", t=T)
    dram["vtok"] = kvl[1].rearrange("(t d) -> t d", d=D)

    es = ExitStack()
    with es:
        c = _mk_ctx(nc, es)
        for i, v in enumerate(CONSTS):
            nc.const_aps.register(F32, v, c.cvals[:, i:i + 1])
        S = c.S
        def aux_dma(q, dst, src, cast=False):
            sm = c.s_aux if q == "sync" else c.s_auxg
            nm = "aux" if q == "sync" else "auxg"
            S.emit(q, lambda g, d=dst, s=src, sm=sm: g.dma_start(out=d, in_=s).then_inc(sm, 16))
            S.inc(nm, 16)
        xr = dram["xT"].rearrange("(kt p) t -> p kt t", p=128)
        xa = c.xacc.rearrange("p (kt t) -> p kt t", t=T)
        for i in range(4):
            aux_dma("sync", xa[:, i * 4:(i + 1) * 4, :], xr[:, i * 4:(i + 1) * 4, :])
        aux_dma("sync", c.xh[:, :], dram["xhT"])
        aux_dma("sync", c.convw[:, :], dram["convw_d"])
        aux_dma("sync", c.convb[:, :], dram["convb_d"])
        aux_dma("sync", c.bqkvT[:, :], dram["bq128"])
        aux_dma("sync", c.boT[:, :], dram["bo128"])
        aux_dma("sync", c.ones_f1[:, :], dram["ones1"])
        aux_dma("sync", c.ones_f128[:, :], dram["ones128"])
        aux_dma("sync", c.ident_f[:, :], dram["ident"])
        aux_dma("sync", c.alpha_sb[:, :], dram["alpha"])
        aux_dma("gpsimd", c.ones_bf[:, :], dram["ones128"])
        aux_dma("gpsimd", c.ones1_bf[:, :], dram["ones1"])
        aux_dma("gpsimd", c.bvrow[:, :], dram["bv_row"])
        aux_dma("gpsimd", c.gwT[:, :], dram["gw128"])
        aux_dma("sync", c.cvals[:, :], dram["cvals_d"])
        naux, nauxg = S.get("aux"), S.get("auxg")
        for eng in ("tensor", "vector", "scalar"):
            S.emit(eng, lambda t, na=naux: t.wait_ge(c.s_aux, na))
            S.emit(eng, lambda t, na=nauxg: t.wait_ge(c.s_auxg, na))
        S.emit("vector", lambda v: v.tensor_scalar_mul(c.bqkvT[:, 0:16], c.bqkvT[:, 0:16], INV_SQ_D).then_inc(c.s_ev, 1))
        S.inc("ev")
        S.emit("tensor", lambda t: t.matmul(c.psE[:, 0:1], c.ones_f1[:, :], c.alpha_sb[:, :],
                                            start=True, stop=True).then_inc(c.s_pe, 1))
        aptag = S.inc("pe")
        S.emit("vector", lambda v, ap=aptag: (
            v.wait_ge(c.s_pe, ap),
            v.tensor_copy(c.al128[:, 0:1], c.psE[:, 0:1]).then_inc(c.s_ev, 1)))
        c.aux_free[0] = S.inc("ev")

        emit_s1(c, dram)
        if upto >= 2:
            emit_ln(c, "ln_hb", dst_hb=arena_v(c, 0, KT * T))
            emit_gating(c, dram)
            emit_moe(c, dram)
        if upto >= 3:
            emit_ln(c, "ln_hb", dst_hb=arena_v(c, 0, KT * T))
            emit_s3(c, dram, s3_upto=s3_upto)
        if upto >= 4:
            emit_ln(c, "ln_hb", dst_hb=arena_v(c, 0, KT * T))
            emit_gating(c, dram)
            emit_moe(c, dram)
        done = S.get("ev")
        outr = dram["out"].rearrange("(kt p) t -> p kt t", p=128)
        S.emit("sync", lambda s, d=done: s.wait_ge(c.s_ev, d))
        for i in range(4):
            S.emit("sync", lambda s, i=i: s.dma_start(out=outr[:, i * 4:(i + 1) * 4, :],
                                                      in_=xa[:, i * 4:(i + 1) * 4, :]).then_inc(c.s_st, 16))
        S.emit("sync", lambda s: s.wait_ge(c.s_st, 64))

        with nc.Block() as block:
            @block.tensor
            def _(t):
                for f in S.ops["tensor"]: f(t)
            @block.vector
            def _(v):
                for f in S.ops["vector"]: f(v)
            @block.scalar
            def _(a):
                for f in S.ops["scalar"]: f(a)
            @block.gpsimd
            def _(g):
                for f in S.ops["gpsimd"]: f(g)
            @block.sync
            def _(s):
                for f in S.ops["sync"]: f(s)
    return nc


def _mk_ctx(nc, es):
    c = Ctx()
    c.nc = nc
    c.S = Sched()
    sb = lambda name, shape, dt: es.enter_context(nc.sbuf_tensor(name, shape, dt))
    ps = lambda name, shape, dt: es.enter_context(nc.psum_tensor(name, shape, dt))
    sem = lambda name: es.enter_context(nc.semaphore(name))
    c.xacc = sb("xacc", [128, KT * T], F32)
    c.arena = sb("arena", [128, ARENA_N], BF16)
    c.fscr = sb("fscr", [128, FSCR_N], F32)
    c.ones_bf = sb("ones_bf", [128, 1], BF16)
    c.ones_f1 = sb("ones_f1", [1, 128], F32)
    c.ones_f128 = sb("ones_f128", [128, 1], F32)
    c.bvrow = c.arena[0:1, 35840:37888]
    c.ones1_bf = c.arena[0:1, 37888:38016]
    c.ident_f = sb("ident_f", [128, 128], F32)
    c.convw = sb("convw", [128, 128], F32)
    c.convb = sb("convb", [128, 32], F32)
    c.bqkvT = sb("bqkvT", [128, 48], F32)
    c.boT = sb("boT", [128, 16], F32)
    c.gwT = sb("gwT", [128, KT * 32], BF16)
    c.alpha_sb = sb("alpha_sb", [1, 1], F32)
    c.al128 = sb("al128", [128, 1], F32)
    c.cvals = sb("cvals", [128, 10], F32)
    c.xh = sb("xh", [128, KT * 3], F32)
    c.uh = sb("uh", [128, KT * 3], BF16)
    c.m8 = sb("m8", [128, 8], F32)
    c.psA = ps("psA", [128, 512], F32)
    c.psB = ps("psB", [128, 512], F32)
    c.psC = ps("psC", [128, 512], F32)
    c.psD = ps("psD", [128, 512], F32)
    c.psE = ps("psE", [128, 512], F32)
    c.psF = ps("psF", [128, 512], F32)
    c.psG = ps("psG", [128, 512], F32)
    c.psH = ps("psH", [128, 512], F32)
    c.psG_free = 0
    c.psH_free = 0
    c.vstg_free = [0, 0]
    c.banks = [c.psA, c.psB, c.psC, c.psD]
    c.bank_free = [0, 0, 0, 0]
    c.aux_free = [0, 0]
    c.pst_free = [0, 0]
    c.s_pe = sem("s_pe"); c.s_ev = sem("s_ev"); c.s_act = sem("s_act")
    c.s_pool = sem("s_pool")
    c.s_ld = [sem("s_ld0"), sem("s_ld1")]
    c.s_aux = sem("s_aux"); c.s_auxg = sem("s_auxg")
    c.s_cr = [sem("s_cr0"), sem("s_cr1")]
    c.s_crs = sem("s_crs")
    c.s_ks = [sem("s_ks0"), sem("s_ks1")]; c.s_coll = sem("s_coll"); c.s_st = sem("s_st")
    c.s_xl = [sem("s_xl0"), sem("s_xl1")]
    c.s_ol = [sem("s_ol0"), sem("s_ol1")]
    c.s_vs = [sem("s_vs0"), sem("s_vs1")]
    c.slot_free = [0, 0]
    c.slot_val = [0, 0]
    c.hb_tags = [("ev", 0)] * KT
    c.crow_ap = [c.fscr[0:1, 2048:3072], c.fscr[0:1, 5120:6144]]
    c.crow_free = [0, 0]
    c.crow_phase = 0
    c.vdma_prev = None
    c.ln_tmp_free = [0, 0]
    c.conv_tap_done = [0, 0]
    c.conv_ymul_done = [0, 0]
    c.moe_silu_free = [0, 0]
    c.moe_q_free = [0, 0]
    c.ltok_tags = {}
    return c



def host_inputs(inputs, upto=4):
    x = np.asarray(inputs["L0_attn_a_x"], dtype=np.float32)
    f32 = lambda a: np.ascontiguousarray(np.asarray(a, dtype=np.float32))
    W_in = f32(inputs["L0_attn_a_W_in"])
    conv_w = f32(inputs["L0_attn_a_conv_w"])
    conv_b = f32(inputs["L0_attn_a_conv_b"])
    W_out = f32(inputs["L0_attn_a_W_out"])
    W1 = f32(inputs["L0_attn_a_W_mlp1"]); W2 = f32(inputs["L0_attn_a_W_mlp2"])
    wqkv = f32(inputs["L0_attn_b_w_qkv"]); bqkv = f32(inputs["L0_attn_b_b_qkv"])
    wo = f32(inputs["L0_attn_b_w_out"]); bo = f32(inputs["L0_attn_b_b_out"])
    gw = f32(inputs["L0_ffn_gate_weight"])
    gup = f32(inputs["L0_ffn_gate_up_proj"]); dnp = f32(inputs["L0_ffn_down_proj"])
    alpha = f32(inputs["L0_ffn_layer_alpha"]).reshape(1, 1)

    shared = {
        "winT": np.ascontiguousarray(W_in.T),
        "convw_d": np.ascontiguousarray(conv_w[:, 0, :].reshape(32, 128, 4).transpose(1, 0, 2).reshape(128, 128)),
        "convb_d": np.ascontiguousarray(conv_b.reshape(32, 128).T),
        "woT": np.ascontiguousarray(W_out.T),
        "w1T": np.ascontiguousarray(W1.T),
        "w2T": np.ascontiguousarray(W2.T),
        "gw128": np.ascontiguousarray(gw.T.reshape(16, 128, 32).transpose(1, 0, 2).reshape(128, 512)),
        "alpha": alpha,
        "guT": np.ascontiguousarray(gup.transpose(0, 2, 1)),
        "dnT": np.ascontiguousarray(dnp.transpose(0, 2, 1)),
        "wqT": np.ascontiguousarray(wqkv.T),
        "bq128": np.ascontiguousarray(bqkv.reshape(48, 128).T),
        "bv_row": np.ascontiguousarray(bqkv[4096:6144].reshape(1, 2048)),
        "woatT": np.ascontiguousarray(wo.T),
        "bo128": np.ascontiguousarray(bo.reshape(16, 128).T),
        "ones1": np.ones((1, 128), dtype=np.float32),
        "ones128": np.ones((128, 1), dtype=np.float32),
        "ident": np.eye(128, dtype=np.float32),
        "cvals_d": np.tile(np.asarray(CONSTS, dtype=np.float32)[None, :], (128, 1)),
    }
    in_maps = []
    for core in range(8):
        b, cc = core // 4, core % 4
        xs = x[b, cc * T:(cc + 1) * T, :]
        m = dict(shared)
        m["xT"] = np.ascontiguousarray(xs.T)
        if cc == 0:
            xh = np.zeros((3, 2048), dtype=np.float32)
        else:
            xh = x[b, cc * T - 3:cc * T, :]
        m["xhT"] = np.ascontiguousarray(xh.T.reshape(16, 128, 3).transpose(1, 0, 2).reshape(128, 48))
        in_maps.append(m)
    return in_maps


def kernel(**inputs):
    in_maps = host_inputs(inputs)
    nc = build(upto=4)
    res = run_bass_kernel_spmd(nc, in_maps, core_ids=list(range(8)))
    out = np.empty((2, 4096, 2048), dtype=np.float32)
    for core in range(8):
        b, cc = core // 4, core % 4
        out[b, cc * T:(cc + 1) * T, :] = res.results[core]["out"].T
    return out
